# version 24
# baseline (speedup 1.0000x reference)
import contextlib
import math
import numpy as np
import concourse.bass as bass
import concourse.mybir as mybir
from concourse.bass_utils import run_bass_kernel_spmd

F32 = mybir.dt.float32
BF16 = mybir.dt.bfloat16
I32 = mybir.dt.int32
AX = mybir.AxisListType.X
AF = mybir.ActivationFunctionType
ALU = mybir.AluOpType

D = 1024
T = 2048
NC = 8
NT = 4
TQ = 512
NH = 16
DH = 64
EPS = 1e-5
TWO_PI = 2.0 * math.pi
SIN_SCALE = float(np.nextafter(np.float32(TWO_PI), np.float32(0.0)))
WSLOT = NC * TQ


def _bf():
    import ml_dtypes
    return ml_dtypes.bfloat16


class Sem:
    def __init__(self, handle, name):
        self.h = handle
        self.name = name
        self.n = 0

    def inc(self, k=1):
        self.n += k
        return self.n


class Prog:
    ENGINES = ("sync", "tensor", "vector", "scalar", "gpsimd")

    def __init__(self):
        self.items = {e: [] for e in self.ENGINES}
        self.lastwait = {}

    def em(self, eng, fn):
        self.items[eng].append(fn)

    def wait(self, eng, sem, val):
        if sem is None or not val or val <= 0:
            return
        key = (eng, sem.name)
        if self.lastwait.get(key, -1) >= val:
            return
        self.lastwait[key] = val
        self.em(eng, lambda E, s=sem.h, v=val: E.wait_ge(s, v))

    def finalize(self, block):
        for name in self.ENGINES:
            fns = self.items[name]

            def body(E, fns=fns):
                for fn in fns:
                    fn(E)
            getattr(block, name)(body)


def host_prepare(inputs):
    bf = _bf()

    def t_(a):
        return np.ascontiguousarray(np.asarray(a, np.float32).T)

    def c_(a, shape=None):
        a = np.asarray(a, np.float32)
        if shape is not None:
            a = a.reshape(shape)
        return np.ascontiguousarray(a)

    g = {}
    g["w_qkv0"] = t_(inputs["L0_attn_in_w"])
    g["w_o0"] = t_(inputs["L0_attn_out_w"])
    g["ln1_g"] = c_(inputs["L0_attn_ln1_g"], (NC, 128))
    g["ln1_b"] = c_(inputs["L0_attn_ln1_b"], (NC, 128))
    g["ln2_g"] = c_(inputs["L0_attn_ln2_g"], (NC, 128))
    g["ln2_b"] = c_(inputs["L0_attn_ln2_b"], (NC, 128))
    g["slopes"] = c_(inputs["L0_attn_slopes"], (1, NH))
    g["smear"] = c_(inputs["L0_attn_smear"], (1, NH))
    g["w_ff1"] = c_(inputs["L0_ffn_W1"])
    g["w_ff2"] = c_(inputs["L0_ffn_W2"])
    g["ff_b1"] = c_(inputs["L0_ffn_b1"], (32, 128))
    g["ff_b2"] = c_(inputs["L0_ffn_b2"], (NC, 128))
    g["ff_th"] = c_(inputs["L0_ffn_quantum_phases"], (32, 128))
    g["ff_wr"] = c_(inputs["L0_ffn_quantum_weights_real"], (32, 128))
    g["ff_wi"] = c_(inputs["L0_ffn_quantum_weights_imag"], (32, 128))
    g["w_qkv1"] = c_(inputs["L1_attn_w_attn"])
    g["b_qkv1"] = c_(inputs["L1_attn_b_attn"], (24, 128))
    g["la_a"] = c_(inputs["L1_attn_la_attn"])
    g["lb_a"] = c_(inputs["L1_attn_lb_attn"])
    g["w_p1"] = c_(inputs["L1_attn_w_proj"])
    g["b_p1"] = c_(inputs["L1_attn_b_proj"], (NC, 128))
    g["la_p"] = c_(inputs["L1_attn_la_proj"])
    g["lb_p"] = c_(inputs["L1_attn_lb_proj"])
    g["kv_s"] = c_(inputs["L1_attn_kv_scale"], (1, 1))
    g["kv_z"] = c_(inputs["L1_attn_kv_zero_point"], (1, 1))
    g["wg"] = t_(inputs["L1_ffn_Wg"])
    g["we_g"] = np.ascontiguousarray(np.transpose(np.asarray(inputs["L1_ffn_We_gate"], np.float32), (0, 2, 1)))
    g["we_u"] = np.ascontiguousarray(np.transpose(np.asarray(inputs["L1_ffn_We_up"], np.float32), (0, 2, 1)))
    g["we_d"] = np.ascontiguousarray(np.transpose(np.asarray(inputs["L1_ffn_We_down"], np.float32), (0, 2, 1)))
    g["ws_g"] = t_(inputs["L1_ffn_Ws_gate"])
    g["ws_u"] = t_(inputs["L1_ffn_Ws_up"])
    g["ws_d"] = t_(inputs["L1_ffn_Ws_down"])
    g["wsg"] = np.ascontiguousarray(
        np.tile(t_(inputs["L1_ffn_Wsg"]), (1, 128)))

    g["c_ones"] = np.ones((128, 128), np.float32).astype(bf)
    g["c_eye_f32"] = np.eye(128, dtype=np.float32)
    g["c_eye_bf"] = np.eye(128, dtype=np.float32).astype(bf)
    g["c_ones_row_f32"] = np.ones((1, 128), np.float32)
    p = np.arange(128)
    esel = np.zeros((64, NH, NH), np.float32)
    for h in range(NH):
        esel[:, h, h] = 1.0
    esel = np.ascontiguousarray(esel.reshape(64, NH * NH))
    g["c_esel"] = np.concatenate([esel, esel], axis=0).astype(bf)
    e64ab = np.zeros((96, 128), np.float32)
    e64ab[0, 0:64] = 1.0
    e64ab[32, 64:128] = 1.0
    g["c_e64ab"] = e64ab
    nege16 = np.zeros((NH, NH, 128), np.float32)
    e16p = np.zeros((NH, NH, 128), np.float32)
    for h in range(NH):
        nege16[h, h, :] = -1.0
        e16p[h, h, :] = 1.0
    g["c_nege16"] = np.concatenate(
        [np.ascontiguousarray(nege16.reshape(NH, NH * 128)),
         np.zeros((128 - NH, NH * 128), np.float32)], axis=0).astype(bf)
    g["c_e16p"] = np.ascontiguousarray(e16p.reshape(NH, NH * 128)).astype(bf)
    qsel = np.zeros((128, NC, NH), np.float32)
    for c in range(NC):
        for j in (2 * c, 2 * c + 1):
            qsel[(j % 2) * 64:(j % 2) * 64 + 64, c, j] = 1.0
    g["c_qsel"] = np.ascontiguousarray(qsel.reshape(128, NC * NH)).astype(bf)
    g["c_tri"] = np.where(p[:, None] > p[None, :], np.float32(-1e30),
                          np.float32(0.0)).astype(np.float32)
    g["c_iotak"] = (p[:, None] + 128.0 * np.arange(16)[None, :]).astype(np.float32)
    g["c_arange"] = np.arange(T, dtype=np.float32).reshape(1, T).astype(bf)
    sel2 = np.zeros((NH, NC * 128), np.float32)
    for c in range(NC):
        for pp in range(128):
            sel2[2 * c + pp // 64, c * 128 + pp] = 1.0
    g["c_sel2"] = sel2.astype(bf)
    return g


DRAM_SPECS_F32 = {
    "w_qkv0": [D, 3 * D], "w_o0": [D, D],
    "ln1_g": [NC, 128], "ln1_b": [NC, 128], "ln2_g": [NC, 128], "ln2_b": [NC, 128],
    "slopes": [1, NH], "smear": [1, NH],
    "w_ff1": [D, 4096], "w_ff2": [4096, D],
    "ff_b1": [32, 128], "ff_b2": [NC, 128], "ff_th": [32, 128],
    "ff_wr": [32, 128], "ff_wi": [32, 128],
    "w_qkv1": [D, 3 * D], "b_qkv1": [24, 128],
    "la_a": [D, 16], "lb_a": [16, 3 * D],
    "w_p1": [D, D], "b_p1": [NC, 128], "la_p": [D, 16], "lb_p": [16, D],
    "kv_s": [1, 1], "kv_z": [1, 1],
    "wg": [D, 16],
    "we_g": [16, D, 512], "we_u": [16, D, 512], "we_d": [16, 512, D],
    "ws_g": [D, 2048], "ws_u": [D, 2048], "ws_d": [2048, D], "wsg": [D, 128],
    "c_eye_f32": [128, 128], "c_ones_row_f32": [1, 128],
    "c_e64ab": [96, 128],
    "c_tri": [128, 128], "c_iotak": [128, 16],
}
DRAM_SPECS_BF = {"c_ones": [128, 128], "c_eye_bf": [128, 128],
                 "c_esel": [128, NH * NH],
                 "c_nege16": [128, NH * 128], "c_e16p": [NH, NH * 128],
                 "c_qsel": [128, NC * NH],
                 "c_arange": [1, T], "c_sel2": [NH, NC * 128]}


ATTN_UPTO = 9
SW_P2 = [True]
SW_KINDS = [("r",)]


def build_program(phases=(0, 1, 2, 3)):
    nc = bass.Bass("TRN2", target_bir_lowering=False)
    dx = nc.dram_tensor("x", [D, T], F32, kind="ExternalInput").ap()
    dout = nc.dram_tensor("out", [D, T], F32, kind="ExternalOutput").ap()
    dram = {}
    for k, shp in DRAM_SPECS_F32.items():
        dram[k] = nc.dram_tensor(k, shp, F32, kind="ExternalInput").ap()
    for k, shp in DRAM_SPECS_BF.items():
        dram[k] = nc.dram_tensor(k, shp, BF16, kind="ExternalInput").ap()

    P = Prog()
    top = contextlib.ExitStack()
    wb_birth = {}

    def sb(name, shp, dt=BF16, st=None):
        t = (st or top).enter_context(nc.sbuf_tensor(name, shp, dt))
        wb_birth[id(t)] = sT.n if "sT" in locals() or True else 0
        return t

    with top:
        sG = Sem(top.enter_context(nc.semaphore("sG")), "sG")
        sS = Sem(top.enter_context(nc.semaphore("sS")), "sS")
        sT = Sem(top.enter_context(nc.semaphore("sT")), "sT")
        sV = Sem(top.enter_context(nc.semaphore("sV")), "sV")
        sA = Sem(top.enter_context(nc.semaphore("sA")), "sA")

        def sb(name, shp, dt=BF16, st=None):
            t = (st or top).enter_context(nc.sbuf_tensor(name, shp, dt))
            wb_birth[id(t)] = (sT.n, sV.n, sA.n)
            return t

        X = sb("X", [128, NC * T], F32)
        XB = sb("XB", [128, NC * T], BF16)
        SQR = sb("SQR", [128, 2 * TQ], BF16)
        SMALLF = sb("SMALLF", [128, 512], F32)
        COLS = sb("COLS", [128, 224], F32)
        QCOL = sb("QCOL", [128, 24], F32)
        KVC = sb("KVC", [128, 8], F32)
        CONES = sb("CONES", [128, 128], BF16)
        CEYEF = sb("CEYEF", [128, 128], F32)
        CEYEB = sb("CEYEB", [128, 128], BF16)
        CONER = sb("CONER", [1, 128], F32)
        CSEL128 = sb("CSEL128", [128, NC * NH], BF16)
        CTRI = sb("CTRI", [128, 128], F32)
        CIOTAK = sb("CIOTAK", [128, 16], F32)

        SLP8 = sb("SLP8", [16, 4], F32)
        SLP8B = sb("SLP8B", [16, 2], BF16)
        SLPB = sb("SLPB", [128, 16], F32)
        SCOLS = sb("SCOLS", [128, 16], F32)
        MAXK = sb("MAXK", [16, 8], F32)
        WGB = sb("WGB", [128, NC * 16], BF16)
        WSGB = sb("WSGB", [128, NC * 128], BF16)
        GSCR = sb("GSCR", [128, 64], F32)
        GSCRB = sb("GSCRB", [128, NC * 16], BF16)
        RI32A = sb("RI32A", [128, TQ], I32)
        CEPS = sb("CEPS", [128, 1], F32)
        CPI2 = sb("CPI2", [128, 1], F32)

        PS = [top.enter_context(nc.psum_tensor(f"ps{i}", [128, 512], F32))
              for i in range(6)]

        psum_owner = {}

        def bw(pst):
            w = psum_owner.get(id(pst))
            return (w,) if w else ()

        def bs(pst, sem, val):
            psum_owner[id(pst)] = (sem, val)

        dma_sems = {}

        def dma(kind, stream, out_ap, in_ap, waits=(), slow=False):
            eng = "sync" if kind == "sync" else "gpsimd"
            if stream not in dma_sems:
                h = top.enter_context(nc.semaphore("sD_" + stream))
                dma_sems[stream] = Sem(h, "sD_" + stream)
            s = dma_sems[stream]
            tb = wb_birth.get(id(getattr(out_ap, "tensor", None)))
            if tb is not None:
                waits = waits + ((sT, tb[0]), (sV, tb[1]), (sA, tb[2]))
            for (sem, val) in waits:
                P.wait(eng, sem, val)
            v = s.inc(16)
            P.em(eng, lambda E, o=out_ap, i=in_ap, sh=s.h, sl=slow:
                 E.dma_start(out=o, in_=i, allow_slow_non_contiguous=sl)
                 .then_inc(sh, 16))
            return (s, v)

        PACE = [12]
        SW = [False]
        SW_P2[0] = SW_P2[0]

        def mm(out_ap, lhsT, rhs, start, stop, waits=(), tp=None, inc=None,
               swkind=None):
            if inc is None:
                inc = stop
            if SW[0] and (swkind in SW_KINDS[0]):
                inc = True
                P.em("tensor", lambda E, v=sT.n: E.wait_ge(sT.h, v))
            elif start:
                P.em("tensor", lambda E, v=max(0, sT.n - PACE[0]):
                     E.wait_ge(sT.h, v))
            for (sem, val) in waits:
                P.wait("tensor", sem, val)
            if inc:
                v = sT.inc()
                P.em("tensor",
                     lambda E, o=out_ap, l=lhsT, r=rhs, a=start, b=stop, t=tp:
                     E.matmul(o, l, r, start=a, stop=b, tile_position=t)
                     .then_inc(sT.h, 1))
                return v
            P.em("tensor", lambda E, o=out_ap, l=lhsT, r=rhs, a=start, b=stop, t=tp:
                 E.matmul(o, l, r, start=a, stop=b, tile_position=t))
            return sT.n

        def tr(out_ap, in_ap, ident, waits=()):
            P.em("tensor", lambda E, v=max(0, sT.n - PACE[0]): E.wait_ge(sT.h, v))
            for (sem, val) in waits:
                P.wait("tensor", sem, val)
            v = sT.inc()
            P.em("tensor", lambda E, o=out_ap, i=in_ap, d=ident:
                 E.transpose(o, i, d).then_inc(sT.h, 1))
            return v

        def vop(name, *args, waits=(), **kw):
            P.em("vector", lambda E, v=sV.n: E.wait_ge(sV.h, v))
            for (sem, val) in waits:
                P.wait("vector", sem, val)
            v = sV.inc()
            P.em("vector", lambda E, nm=name, a=args, k=kw:
                 getattr(E, nm)(*a, **k).then_inc(sV.h, 1))
            return v

        def aop(name, *args, waits=(), **kw):
            P.em("scalar", lambda E, v=sA.n: E.wait_ge(sA.h, v))
            for (sem, val) in waits:
                P.wait("scalar", sem, val)
            v = sA.inc()
            P.em("scalar", lambda E, nm=name, a=args, k=kw:
                 getattr(E, nm)(*a, **k).then_inc(sA.h, 1))
            return v

        def gop(name, *args, waits=(), **kw):
            P.em("gpsimd", lambda E, v=sG.n: E.wait_ge(sG.h, v))
            for (sem, val) in waits:
                P.wait("gpsimd", sem, val)
            v = sG.inc()
            P.em("gpsimd", lambda E, nm=name, a=args, k=kw:
                 getattr(E, nm)(*a, **k).then_inc(sG.h, 1))
            return v

        def XC(c):
            return X[:, c * T:(c + 1) * T]

        def XBC(c):
            return XB[:, c * T:(c + 1) * T]

        slot_use = {}

        def wload(wb, slot, dram_ap, ncols, swidth=None):
            bt, bv, ba = wb_birth.get(id(wb), (0, 0, 0))
            if ncols <= WSLOT:
                off = slot * (swidth or WSLOT)
                waits = ((sT, slot_use.get((id(wb), slot), 0)),
                         (sT, slot_use.get((id(wb), "all"), 0)),
                         (sT, bt), (sV, bv), (sA, ba))
                stream = wb.name + "_" + str(slot)
            else:
                off = 0
                waits = ((sT, max(slot_use.get((id(wb), 0), 0),
                                  slot_use.get((id(wb), 1), 0),
                                  slot_use.get((id(wb), "all"), 0), bt)),
                         (sV, bv), (sA, ba))
                stream = wb.name + "_all"
            return dma("gp", stream, wb[:, off:off + ncols], dram_ap, waits=waits)

        def wuse(wb, slot, vt, ncols=0):
            key = (id(wb), slot) if ncols <= WSLOT else (id(wb), "all")
            slot_use[key] = max(slot_use.get(key, 0), vt)

        vop("memset", CEPS[:, :], EPS)
        vop("memset", CPI2[:, :], math.pi / 2.0)
        v_x = dma("sync", "x", X[:, :].rearrange("p (c t) -> p c t", c=NC), dx.rearrange("(c p) t -> p c t", p=128))
        v_consts = 0
        for (name, tile) in (("c_ones", CONES), ("c_eye_f32", CEYEF),
                             ("c_eye_bf", CEYEB),
                             ("c_ones_row_f32", CONER),
                             ("c_qsel", CSEL128),
                             ("c_tri", CTRI), ("c_iotak", CIOTAK),
                             ):
            v_consts = dma("sync", "consts", tile[0:tile.shape[0], :], dram[name])
        v_wgb = dma("gp", "wgb", WGB[:, :].rearrange("p (c n) -> p c n", c=NC), dram["wg"].rearrange("(c p) n -> p c n", p=128))
        v_wsgb = dma("gp", "wsgb", WSGB[:, :].rearrange("p (c n) -> p c n", c=NC),
                     dram["wsg"].rearrange("(c p) n -> p c n", p=128))

        with contextlib.ExitStack() as pre:
            PSCR = sb("PSCR", [128, 448], F32, pre)
            CSEL2 = sb("CSEL2", [NH, NC * 128], BF16, pre)
            v_consts2 = dma("sync", "cols", CSEL2[0:NH, :], dram["c_sel2"])

            def load_cols(name, n, dst):
                vl = dma("sync", "cols", SMALLF[0:n, 0:128], dram[name],
                         waits=((sV, sV.n), (sT, sT.n), (sA, sA.n)))
                vt = tr(PS[0][:, 0:n], SMALLF[0:n, 0:128], CEYEF[0:n, 0:n],
                        waits=bw(PS[0]) + (vl, v_consts))
                vv = vop("tensor_copy", dst, PS[0][:, 0:n], waits=((sT, vt),))
                bs(PS[0], sV, vv)
                return vv

            load_cols("ln1_g", NC, COLS[:, 0:8])
            load_cols("ln1_b", NC, COLS[:, 8:16])
            load_cols("ln2_g", NC, COLS[:, 16:24])
            load_cols("ln2_b", NC, COLS[:, 24:32])
            load_cols("ff_b2", NC, COLS[:, 64:72])
            load_cols("b_p1", NC, COLS[:, 136:144])
            load_cols("b_qkv1", 24, QCOL[:, 0:24])
            v_th = load_cols("ff_th", 32, PSCR[:, 416:448])
            v_wr = load_cols("ff_wr", 32, PSCR[:, 0:32])
            v_wi = load_cols("ff_wi", 32, PSCR[:, 32:64])
            v_b1 = load_cols("ff_b1", 32, COLS[:, 104:136])
            TH = PSCR[:, 416:448]

            def sin32(dst, src, bias, base, extra_waits=()):
                u = PSCR[:, base:base + 32]
                kf = PSCR[:, base + 32:base + 64]
                r1 = PSCR[:, base + 64:base + 96]
                c1 = PSCR[:, base + 96:base + 128]
                vop("tensor_scalar", u, src, 1.0 / TWO_PI, bias / TWO_PI,
                    ALU.mult, ALU.add, waits=extra_waits)
                vop("tensor_copy", RI32A[:, 0:32], u)
                vop("tensor_copy", kf, RI32A[:, 0:32])
                vop("tensor_tensor", r1, u, kf, ALU.subtract)
                vop("tensor_scalar", c1, r1, 0.5, None, ALU.is_ge)
                vop("scalar_tensor_tensor", r1, c1, -1.0, r1, ALU.mult, ALU.add)
                vop("tensor_scalar", c1, r1, -0.5, None, ALU.is_lt)
                vv = vop("scalar_tensor_tensor", r1, c1, 1.0, r1, ALU.mult, ALU.add)
                return aop("activation", dst, r1, AF.Sin, scale=SIN_SCALE,
                           waits=((sV, vv),))

            va1 = sin32(PSCR[:, 320:352], TH, 0.0, 64, extra_waits=((sV, v_th),))
            va2 = sin32(PSCR[:, 352:384], TH, math.pi / 2.0, 192,
                        extra_waits=((sA, va1),))
            vop("tensor_scalar", PSCR[:, 128:160], TH, 1.0 / TWO_PI, None, ALU.mult,
                waits=((sA, va2),))
            vop("tensor_copy", RI32A[:, 32:64], PSCR[:, 128:160])
            vop("tensor_copy", PSCR[:, 160:192], RI32A[:, 32:64])
            vop("tensor_tensor", PSCR[:, 128:160], PSCR[:, 128:160],
                PSCR[:, 160:192], ALU.subtract)
            vop("tensor_scalar", PSCR[:, 224:256], PSCR[:, 128:160], 0.5, None,
                ALU.is_ge)
            vop("scalar_tensor_tensor", PSCR[:, 160:192], PSCR[:, 224:256], 1.0,
                PSCR[:, 160:192], ALU.mult, ALU.add)
            vop("tensor_scalar", PSCR[:, 224:256], PSCR[:, 128:160], -0.5, None,
                ALU.is_lt)
            vop("scalar_tensor_tensor", PSCR[:, 160:192], PSCR[:, 224:256], -1.0,
                PSCR[:, 160:192], ALU.mult, ALU.add)
            v_thst = vop("scalar_tensor_tensor", COLS[:, 32:64], PSCR[:, 160:192],
                         -TWO_PI, TH, ALU.mult, ALU.add)
            vop("tensor_tensor", PSCR[:, 384:416], PSCR[:, 352:384], PSCR[:, 0:32],
                ALU.mult, waits=((sA, va2), (sV, v_wr)))
            vop("tensor_tensor", PSCR[:, 64:96], PSCR[:, 320:352], PSCR[:, 32:64],
                ALU.mult, waits=((sV, v_wi),))
            vop("tensor_tensor", PSCR[:, 384:416], PSCR[:, 384:416], PSCR[:, 64:96],
                ALU.add)
            vop("tensor_tensor", PSCR[:, 384:416], PSCR[:, 384:416], PSCR[:, 32:64],
                ALU.mult)
            vop("tensor_scalar", COLS[:, 72:104], PSCR[:, 384:416], 0.1, None, ALU.mult)
            vop("tensor_copy", COLS[:, 160:192], PSCR[:, 320:352],
                waits=((sA, va2),))
            v_thb = vop("tensor_copy", COLS[:, 192:224], PSCR[:, 352:384])

            vl = dma("sync", "cols", SMALLF[0:1, 0:NH], dram["slopes"],
                     waits=((sV, sV.n), (sT, sT.n), (sA, sA.n)))
            vt = tr(PS[0][0:NH, 0:1], SMALLF[0:1, 0:NH], CEYEF[0:1, 0:1],
                    waits=bw(PS[0]) + (vl,))
            vv = vop("tensor_scalar", SLP8[0:NH, 0:1], PS[0][0:NH, 0:1], 8.0, None,
                     ALU.mult, waits=((sT, vt),))
            bs(PS[0], sV, vv)
            vt = mm(PS[1][:, 0:NH], CONER[:, :], SMALLF[0:1, 0:NH], True, True,
                    waits=bw(PS[1]) + (vl,))
            v_slb = vop("tensor_copy", SLPB[:, 0:16], PS[1][:, 0:NH], waits=((sT, vt),))
            bs(PS[1], sV, v_slb)
            vl = dma("sync", "cols", SMALLF[0:1, 16:16 + NH], dram["smear"],
                     waits=((sV, sV.n), (sT, sT.n), (sA, sA.n)))
            va = aop("activation", SMALLF[0:1, 32:32 + NH], SMALLF[0:1, 16:16 + NH],
                     AF.Sigmoid, waits=(vl,))
            vt = tr(PS[0][0:NH, 0:1], SMALLF[0:1, 32:32 + NH], CEYEF[0:1, 0:1],
                    waits=bw(PS[0]) + ((sA, va),))
            v_sg = vop("tensor_copy", SLP8B[0:NH, 0:1], PS[0][0:NH, 0:1],
                       waits=((sT, vt),))
            bs(PS[0], sV, v_sg)
            vt2 = 0
            for c in range(NC):
                vt2 = mm(PS[1][:, c:c + 1], CSEL2[:, c * 128:(c + 1) * 128],
                         SLP8B[0:NH, 0:1], c == 0, c == NC - 1,
                         waits=(bw(PS[1]) if c == 0 else ()) +
                         ((sV, v_sg), v_consts2))
            v_sc = vop("tensor_copy", SCOLS[:, 0:8], PS[1][:, 0:8], waits=((sT, vt2),))
            bs(PS[1], sV, v_sc)
            v_sc = vop("tensor_scalar", SCOLS[:, 8:16], SCOLS[:, 0:8], -1.0, 1.0,
                       ALU.mult, ALU.add, waits=((sV, v_sc),))
            vl = dma("sync", "cols", SMALLF[0:1, 48:49], dram["kv_s"],
                     waits=((sV, sV.n), (sT, sT.n), (sA, sA.n)))
            vl = dma("sync", "cols", SMALLF[0:1, 49:50], dram["kv_z"],
                     waits=((sV, sV.n), (sT, sT.n), (sA, sA.n)))
            vt = mm(PS[1][:, 0:2], CONER[:, :], SMALLF[0:1, 48:50], True, True,
                    waits=bw(PS[1]) + (vl,))
            vk = vop("tensor_copy", KVC[:, 2:3], PS[1][:, 0:1], waits=((sT, vt),))
            vk = vop("tensor_copy", KVC[:, 1:2], PS[1][:, 1:2])
            bs(PS[1], sV, vk)
            vk = vop("reciprocal", KVC[:, 0:1], KVC[:, 2:3], waits=((sV, vk),))
            vk = vop("tensor_tensor", KVC[:, 3:4], KVC[:, 2:3], KVC[:, 1:2], ALU.mult)
            v_kv = vop("tensor_scalar", KVC[:, 3:4], KVC[:, 3:4], -1.0, None, ALU.mult,
                       waits=((sV, vk),))

        def ln_full(MUt, RSTDt, SCRt, xb_fn, wait_a, gcol=None, bcol=None,
                    write_f32=False, src_fn=None, add_to_x=False):
            src_fn = src_fn or XC
            v_stats = [0] * NT
            v_last, a_last = 0, 0
            a_buf = {0: 0, 1: 0}
            a_tc = [[0] * NC for _ in range(NT)]
            v_cc = [0] * NC

            def wac(c):
                if callable(wait_a):
                    return wait_a(c)
                return wait_a[c] if isinstance(wait_a, list) else wait_a

            def stats(tq):
                sl = slice(tq * TQ, (tq + 1) * TQ)
                pa, pb = PS[4], PS[5]
                vta = vtb = 0
                for c in range(NC):
                    wc = wac(c)
                    vta = mm(pa[:, :], CONES[:, :], xb_fn(c)[:, sl], c == 0,
                             c == NC - 1,
                             waits=(bw(pa) if c == 0 else ()) + ((sA, wc),))
                    ring = SQR[:, (c % 2) * TQ:(c % 2 + 1) * TQ]
                    va_sq = aop("activation", ring, xb_fn(c)[:, sl], AF.Square,
                                waits=((sA, wc), (sT, vtb)))
                    vtb = mm(pb[:, :], CONES[:, :], ring, c == 0, c == NC - 1,
                             waits=(bw(pb) if c == 0 else ()) + ((sA, va_sq),),
                             inc=True)
                va_mu = aop("activation", MUt[:, sl], pa[:, :], AF.Copy,
                            scale=1.0 / D, waits=((sT, vta),))
                bs(pa, sA, va_mu)
                v1 = vop("scalar_tensor_tensor", SCRt[:, 2 * TQ:3 * TQ], MUt[:, sl],
                         -1.0, MUt[:, sl], ALU.mult, ALU.mult, waits=((sA, va_mu),))
                v2 = vop("scalar_tensor_tensor", SCRt[:, 2 * TQ:3 * TQ], pb[:, :],
                         1.0 / D, SCRt[:, 2 * TQ:3 * TQ], ALU.mult, ALU.add,
                         waits=((sV, v1), (sT, vtb)))
                bs(pb, sV, v2)
                va_sd = aop("activation", RSTDt[:, sl], SCRt[:, 2 * TQ:3 * TQ],
                            AF.Sqrt, bias=CEPS[:, 0:1], waits=((sV, v2),))
                vx = vop("reciprocal", RSTDt[:, sl], RSTDt[:, sl],
                         waits=((sA, va_sd),))
                vx = vop("scalar_tensor_tensor", MUt[:, sl], MUt[:, sl], -1.0,
                         RSTDt[:, sl], ALU.mult, ALU.mult, waits=((sV, vx),))
                v_stats[tq] = vx

            def apply(tq):
                nonlocal v_last, a_last
                sl = slice(tq * TQ, (tq + 1) * TQ)
                for c in range(NC):
                    par = c % 2
                    buf = SCRt[:, 0:TQ] if par == 0 else SCRt[:, TQ:2 * TQ]
                    vw = vop("tensor_tensor", buf, src_fn(c)[:, sl], RSTDt[:, sl],
                             ALU.mult, waits=((sV, v_stats[tq]), (sA, a_buf[par])))
                    vw = vop("tensor_tensor", buf, buf, MUt[:, sl], ALU.add,
                             waits=((sV, vw),))
                    if gcol is not None:
                        vw = vop("tensor_scalar", buf, buf, gcol[:, c:c + 1],
                                 bcol[:, c:c + 1], ALU.mult, ALU.add,
                                 waits=((sV, vw),))
                    if add_to_x:
                        v_last = vop("tensor_tensor", XC(c)[:, sl], XC(c)[:, sl],
                                     buf, ALU.add, waits=((sV, vw),))
                        v_cc[c] = v_last
                        continue
                    if write_f32:
                        vw = vop("tensor_copy", XC(c)[:, sl], buf, waits=((sV, vw),))
                        v_cc[c] = vw
                    a_last = aop("activation", XBC(c)[:, sl], buf, AF.Copy,
                                 waits=((sV, vw),))
                    a_tc[tq][c] = a_last
                    a_buf[par] = a_last
                    v_last = vw

            for tq in range(NT):
                stats(tq)
                if tq > 0:
                    apply(tq - 1)
            apply(NT - 1)
            return v_last, a_last, a_tc, v_cc

        def post_ln_scoped(wait_vc, tag, t_xbr=0):
            with contextlib.ExitStack() as lnst:
                MUt = sb("MU" + tag, [128, T], F32, lnst)
                RSTDt = sb("RS" + tag, [128, T], F32, lnst)
                SCRt = sb("SC" + tag, [128, 3 * TQ], F32, lnst)
                a_cast_c = [None] * NC

                def getc(c):
                    if a_cast_c[c] is None:
                        wv = wait_vc[c] if isinstance(wait_vc, list) else wait_vc
                        a_cast_c[c] = aop("activation", XBC(c), XC(c), AF.Copy,
                                          waits=((sV, wv), (sT, t_xbr)))
                    return a_cast_c[c]

                return ln_full(MUt, RSTDt, SCRt, lambda c: XBC(c), getc,
                               write_f32=True)

        def attention(layer, v_in_v, a_in_a, atc_in=None):
            is0 = (layer == 0)
            tag = str(layer)
            use = {}
            wqname = "w_qkv0" if is0 else "w_qkv1"
            stA = contextlib.ExitStack()
            with stA:
                ARK = sb("ARK" + tag, [128, NC * T], BF16, stA)
                ARV = sb("ARV" + tag, [128, 16 * D], BF16, stA)
                HROW = sb("HROW" + tag, [16, TQ], BF16, stA)
                AUXR = sb("AUXR" + tag, [128, TQ], BF16, stA) if is0 else None
                T1L = sb("T1L" + tag, [16, T], BF16, stA) if not is0 else None
                WS1L = sb("WS1L" + tag, [16, 3 * D], BF16, stA) if not is0 else None
                BIASC = sb("BIASC" + tag, [128, NH * 16], F32, stA) if is0 else None
                CAR = sb("CAR", [1, TQ], BF16, stA) if is0 else None

                def AK(c):
                    return ARK[:, c * T:(c + 1) * T]

                def AVS(kt):
                    return ARV[:, kt * D:(kt + 1) * D]

                if is0:
                    with contextlib.ExitStack() as lnst:
                        MUt = sb("MUa", [128, T], F32, lnst)
                        RSTDt = sb("RSa", [128, T], F32, lnst)
                        SCRt = sb("SCa", [128, 3 * TQ], F32, lnst)
                        a_cast = 0
                        for c in range(NC):
                            a_cast = aop("activation", XBC(c), XC(c), AF.Copy,
                                         waits=(v_x, (sV, v_in_v)))
                        v_hin, a_hin, atc_h, _ = ln_full(
                            MUt, RSTDt, SCRt, lambda c: XBC(c), a_cast,
                            gcol=COLS[:, 0:8], bcol=COLS[:, 8:16])
                    v_biasc = 0
                    for h in range(NH):
                        v_biasc = vop("tensor_scalar", BIASC[:, h * 16:(h + 1) * 16],
                                      CIOTAK[:, :], SLPB[:, h:h + 1], None, ALU.mult,
                                      waits=((sV, v_slb), v_consts))
                else:
                    v_hin, a_hin = v_in_v, a_in_a
                    atc_h = atc_in
                    v_biasc = 0

                def wh(tq, c):
                    if atc_h is not None:
                        return ((sA, atc_h[tq][c]),)
                    return ((sA, a_hin), (sV, v_hin))

                if ATTN_UPTO <= 1:
                    return post_ln_scoped(sV.n, "pa" + tag)
                a_t1 = 0
                if not is0:
                    use["ws1"] = dma("gp", "ws1l", WS1L[:, :], dram["lb_a"],
                                     waits=((sT, sT.n),))
                    vla = dma("gp", "gscrb", GSCRB[:, 0:NC * 16].rearrange("p (c r) -> p c r", c=NC),
                              dram["la_a"].rearrange("(c p) r -> p c r", p=128),
                              waits=((sT, sT.n),))
                    for tq in range(NT):
                        pa = PS[4]
                        vt = 0
                        for c in range(NC):
                            vt = mm(pa[0:16, :], GSCRB[:, c * 16:(c + 1) * 16],
                                    XBC(c)[:, tq * TQ:(tq + 1) * TQ], c == 0,
                                    c == NC - 1,
                                    waits=(bw(pa) if c == 0 else ()) +
                                    (vla,) + wh(tq, c))
                        a_t1 = aop("activation", T1L[:, tq * TQ:(tq + 1) * TQ],
                                   pa[0:16, :], AF.Copy, scale=2.0, waits=((sT, vt),))
                        bs(pa, sA, a_t1)

                with contextlib.ExitStack() as kvst:
                    PSB0 = kvst.enter_context(
                        nc.psum_tensor("psb0" + tag, [128, 1024], BF16))
                    PSB1 = kvst.enter_context(
                        nc.psum_tensor("psb1" + tag, [128, 1024], BF16))
                    WBkv = sb("WBkv" + tag, [128, 2 * WSLOT], BF16, kvst)
                    VSTG = sb("VSTG" + tag, [128, 2 * TQ], BF16, kvst)
                    if is0:
                        KRAW = sb("KRAW", [128, 2 * (TQ + 1)], BF16, kvst)
                        KTMP = sb("KTMP", [128, TQ], BF16, kvst)
                    else:
                        FQY = sb("FQY", [128, TQ], F32, kvst)
                        FQC = SQR[:, 0:TQ]
                        FQR = sb("FQR", [128, TQ], F32, kvst)
                        FQI = RI32A

                    MAGIC = 12582912.0

                    def fakequant(psrc, dst_ap, wt):
                        a1 = aop("activation", FQY[:, :], psrc, AF.Identity,
                                 scale=KVC[:, 0:1], bias=KVC[:, 1:2],
                                 waits=wt + ((sV, v_kv), (sV, use.get("fq", 0))))
                        vop("tensor_scalar", FQR[:, :], FQY[:, :], MAGIC, None,
                            ALU.add, waits=((sA, a1),))
                        vop("tensor_scalar", FQR[:, :], FQR[:, :], -MAGIC, None,
                            ALU.add)
                        vq = vop("tensor_scalar", FQR[:, :], FQR[:, :], 0.0, 255.0,
                                 ALU.max, ALU.min)
                        use["fq"] = vq
                        return aop("activation", dst_ap, FQR[:, :], AF.Identity,
                                   scale=KVC[:, 2:3], bias=KVC[:, 3:4],
                                   waits=((sV, vq),))

                    wq = dram[wqname].rearrange("(c p) n -> p c n", p=128)
                    pb4 = [PS[0], PS[1], PS[2], PS[3]]
                    pidx = 0
                    pend_v = [None]

                    def emit_vtr(a_ev, tqv, c2v, vparv):
                        for j in range(4):
                            kt = tqv * 4 + j
                            pbx = PSB0 if j % 2 == 0 else PSB1
                            vt2 = tr(pbx[:, 0:128],
                                     VSTG[:, vparv * TQ + j * 128:
                                          vparv * TQ + (j + 1) * 128],
                                     CEYEB[:, :],
                                     waits=bw(pbx) + ((sA, a_ev),))
                            vv2 = vop("tensor_copy",
                                      AVS(kt)[:, c2v * 128:(c2v + 1) * 128],
                                      pbx[:, 0:128], waits=((sT, vt2),))
                            bs(pbx, sV, vv2)
                            use[("vstg", vparv)] = vt2

                    for ns in range(2, 6):
                        slot = ns % 2
                        v_wl = wload(WBkv, slot, wq[:, :, ns * TQ:(ns + 1) * TQ],
                                     WSLOT)
                        for nsub in range(4):
                            n128 = ns * 4 + nsub
                            for tq in range(NT):
                                pa = pb4[pidx]
                                pidx = (pidx + 1) % 4
                                vt = 0
                                for c in range(NC):
                                    w0 = bw(pa) if c == 0 else ()
                                    vt = mm(pa[:, :],
                                            WBkv[:, slot * WSLOT + c * TQ +
                                                 nsub * 128:slot * WSLOT +
                                                 c * TQ + nsub * 128 + 128],
                                            XBC(c)[:, tq * TQ:(tq + 1) * TQ],
                                            c == 0, (c == NC - 1) and is0,
                                            waits=w0 + (v_wl,) + wh(tq, c))
                                if not is0:
                                    vt = mm(pa[:, :],
                                            WS1L[:, n128 * 128:(n128 + 1) * 128],
                                            T1L[:, tq * TQ:(tq + 1) * TQ], False,
                                            True, waits=((sA, a_t1), use["ws1"]))
                                wuse(WBkv, slot, vt)
                                sl = slice(tq * TQ, (tq + 1) * TQ)
                                if n128 < 16:
                                    c2 = n128 - 8
                                    if is0:
                                        ksl = tq % 2
                                        base = ksl * (TQ + 1)
                                        a_ev = aop("activation",
                                                   KRAW[:, base + 1:base + 1 + TQ],
                                                   pa[:, :], AF.Copy,
                                                   waits=((sT, vt),
                                                          (sV, use.get(("kr", ksl), 0))))
                                        bs(pa, sA, a_ev)
                                        if tq == 0:
                                            vb = vop("memset",
                                                     KRAW[:, base:base + 1], 0.0,
                                                     waits=((sA, a_ev),))
                                        else:
                                            ob = (1 - ksl) * (TQ + 1)
                                            vb = vop("tensor_copy",
                                                     KRAW[:, base:base + 1],
                                                     KRAW[:, ob + TQ:ob + TQ + 1],
                                                     waits=((sA, a_ev),))
                                            use[("kr", 1 - ksl)] = max(
                                                use.get(("kr", 1 - ksl), 0), vb)
                                        vb = vop("tensor_scalar", KTMP[:, :],
                                                 KRAW[:, base + 1:base + 1 + TQ],
                                                 SCOLS[:, 8 + c2:9 + c2], None,
                                                 ALU.mult,
                                                 waits=((sV, vb), (sV, v_sc)))
                                        use[("kr", ksl)] = vop(
                                            "scalar_tensor_tensor", AK(c2)[:, sl],
                                            KRAW[:, base:base + TQ],
                                            SCOLS[:, c2:c2 + 1], KTMP[:, :],
                                            ALU.mult, ALU.add, waits=((sV, vb),))
                                    else:
                                        a_ev = fakequant(pa[:, :], AK(c2)[:, sl],
                                                         ((sT, vt),))
                                        bs(pa, sA, a_ev)
                                else:
                                    c2 = n128 - 16
                                    vpar = (n128 * NT + tq) % 2
                                    vdst = VSTG[:, vpar * TQ:(vpar + 1) * TQ]
                                    wv = ((sT, vt),
                                          (sT, use.get(("vstg", vpar), 0)))
                                    if is0:
                                        a_ev = aop("activation", vdst,
                                                   pa[:, :], AF.Copy, waits=wv)
                                    else:
                                        a_ev = fakequant(pa[:, :], vdst, wv)
                                    bs(pa, sA, a_ev)
                                    if pend_v[0] is not None:
                                        emit_vtr(*pend_v[0])
                                    pend_v[0] = (a_ev, tq, c2, vpar)
                    if pend_v[0] is not None:
                        emit_vtr(*pend_v[0])
                    v_kv_v = sV.n
                    a_kv_a = sA.n

                    vk2 = 0
                    for tq in range(NT):
                        sl = slice(tq * TQ, (tq + 1) * TQ)
                        pbk = PS[5]
                        vtb = 0
                        for cp in range(NC):
                            ring = SQR[:, (cp % 2) * TQ:(cp % 2 + 1) * TQ]
                            ak = aop("activation", ring[:, :], AK(cp)[:, sl],
                                     AF.Square,
                                     waits=((sV, v_kv_v), (sA, a_kv_a), (sT, vtb)))
                            vtb = mm(pbk[0:16, :],
                                     CSEL128[:, cp * 16:(cp + 1) * 16],
                                     ring[:, :], cp == 0, cp == NC - 1,
                                     waits=(bw(pbk) if cp == 0 else ()) + ((sA, ak),),
                                     inc=True)
                        vk2 = vop("reduce_max", MAXK[0:16, tq:tq + 1], pbk[0:16, :],
                                  AX, waits=((sT, vtb),))
                        bs(pbk, sV, vk2)
                    vk2 = vop("reduce_max", MAXK[0:16, 4:5], MAXK[0:16, 0:4], AX,
                              waits=((sV, vk2),))
                    a_mk = aop("activation", MAXK[0:16, 5:6], MAXK[0:16, 4:5],
                               AF.Sqrt, waits=((sV, vk2),))

                if ATTN_UPTO <= 2:
                    return post_ln_scoped(sV.n, "pa" + tag)
                PS6 = stA.enter_context(
                    nc.psum_tensor("ps6" + tag, [128, 512], F32))
                PS7 = stA.enter_context(
                    nc.psum_tensor("ps7" + tag, [128, 512], F32))
                ARQ = sb("ARQ" + tag, [128, 2 * NC * TQ], BF16, stA)
                PT = sb("PT" + tag, [128, 4 * TQ], BF16, stA)
                ONRM = SMALLF
                WBq = sb("WBq" + tag, [128, 2048], BF16, stA)
                BIA1 = sb("BIA1", [128, 16], F32, stA) if not is0 else None
                if is0:
                    NEGE16 = sb("NEGE16", [128, NH * 128], BF16, stA)
                    v_ng = dma("sync", "nege", NEGE16[:, :], dram["c_nege16"])
                else:
                    NEGE16, v_ng = None, 0

                def AQZ(c, hi):
                    return ARQ[:, (2 * c + hi) * TQ:(2 * c + hi + 1) * TQ]

                v_zq = 0
                for b16 in range(2 * NC):
                    v_zq = vop("memset", ARQ[:, b16 * TQ:(b16 + 1) * TQ], 0.0)
                if is0:
                    v_zq = vop("memset", AUXR[0:128, 0:TQ], 0.0)

                wq = dram[wqname].rearrange("(c p) n -> p c n", p=128)
                for qw in range(NT):
                    hqs = (qw,)
                    pb4 = [PS[0], PS[1], PS[2], PS[3]]
                    pidx = 0
                    for n128 in range(8):
                        v_wl = wload(WBq, n128 % 2,
                                     wq[:, :, n128 * 128:(n128 + 1) * 128], 1024,
                                     swidth=1024)
                        if True:
                            for ti, tq in enumerate(hqs):
                                pa = pb4[pidx]
                                pidx = (pidx + 1) % 4
                                vt = 0
                                for c in range(NC):
                                    w0 = bw(pa) if c == 0 else ()
                                    vt = mm(pa[:, :],
                                            WBq[:, (n128 % 2) * 1024 + c * 128:
                                                (n128 % 2) * 1024 + c * 128 + 128],
                                            XBC(c)[:, tq * TQ:(tq + 1) * TQ],
                                            c == 0, (c == NC - 1) and is0,
                                            waits=w0 + (v_wl,) + wh(tq, c))
                                if not is0:
                                    vt = mm(pa[:, :],
                                            WS1L[:, n128 * 128:(n128 + 1) * 128],
                                            T1L[:, tq * TQ:(tq + 1) * TQ], False,
                                            True, waits=((sA, a_t1),))
                                wuse(WBq, n128 % 2, vt)
                                dsl = slice(ti * TQ, (ti + 1) * TQ)
                                for hi in range(2):
                                    r0 = hi * 64
                                    dst = AQZ(n128, hi)[r0:r0 + 64, dsl]
                                    if is0:
                                        a_ev = aop("activation", dst,
                                                   pa[r0:r0 + 64, :], AF.Copy,
                                                   waits=((sT, vt), (sV, v_zq)))
                                    else:
                                        a_ev = aop("activation", dst,
                                                   pa[r0:r0 + 64, :], AF.Identity,
                                                   bias=QCOL[r0:r0 + 64,
                                                             n128:n128 + 1],
                                                   waits=((sT, vt), (sV, v_zq)))
                                bs(pa, sA, a_ev)
                    a_q = sA.n

                    v_aux = 0
                    for ti, tq in enumerate(hqs):
                        pbk = PS[5]
                        vtb = 0
                        for cp in range(NC):
                            ring = SQR[:, (cp % 2) * TQ:(cp % 2 + 1) * TQ]
                            ak = 0
                            for hi in range(2):
                                r0 = hi * 64
                                ak = aop("activation", ring[r0:r0 + 64, :],
                                         AQZ(cp, hi)[r0:r0 + 64,
                                                     ti * TQ:(ti + 1) * TQ],
                                         AF.Square,
                                         waits=((sA, a_q), (sT, vtb),
                                                (sV, use.get("vo", 0))))
                            vtb = mm(pbk[0:16, :],
                                     CSEL128[:, cp * 16:(cp + 1) * 16],
                                     ring[:, :], cp == 0, cp == NC - 1,
                                     waits=(bw(pbk) if cp == 0 else ()) + ((sA, ak),),
                                     inc=True)
                        a_qn = aop("activation", HROW[0:16, 0:TQ],
                                   pbk[0:16, :], AF.Sqrt,
                                   waits=((sT, vtb), (sV, use.get("aux", 0))))
                        bs(pbk, sA, a_qn)
                        if is0:
                            v_car = dma("sync", "car", CAR[0:1, :],
                                        dram["c_arange"][0:1, tq * TQ:(tq + 1) * TQ],
                                        waits=((sT, use.get("slq", 0)),))
                            pq = PS[4]
                            vt = mm(pq[0:16, :], CONES[0:1, 0:16],
                                    CAR[0:1, 0:TQ], True, True,
                                    waits=bw(pq) + (v_car,))
                            use["slq"] = vt
                            vsl = vop("tensor_scalar", SMALLF[0:16, 0:TQ],
                                      pq[0:16, :], SLP8[0:NH, 0:1], None, ALU.mult,
                                      waits=((sT, vt), (sT, use.get("smf", 0))))
                            bs(pq, sV, vsl)
                            v_aux = vop("scalar_tensor_tensor",
                                        AUXR[0:16, 0:TQ],
                                        HROW[0:16, 0:TQ],
                                        MAXK[0:16, 5:6], SMALLF[0:16, 0:TQ],
                                        ALU.mult, ALU.add,
                                        waits=((sA, a_mk), (sA, a_qn),
                                               (sV, vsl), (sT, use.get("aux_mm", 0))))
                        else:
                            vmq = vop("reduce_max", MAXK[0:16, 6:7],
                                      HROW[0:16, 0:TQ], AX, waits=((sA, a_qn),))
                            use["aux"] = vmq
                            vmq = vop("tensor_tensor", MAXK[0:16, 7:8],
                                      MAXK[0:16, 6:7], MAXK[0:16, 5:6], ALU.mult,
                                      waits=((sV, vmq), (sA, a_mk)))
                            vdg = vop("tensor_scalar", GSCRB[0:16, 0:16],
                                      CEYEF[0:16, 0:16], MAXK[0:16, 7:8], None,
                                      ALU.mult,
                                      waits=((sV, vmq), (sT, use.get("biamm", 0))))
                            pq = PS[4]
                            vtb2 = mm(pq[:, 0:16], CONES[0:16, :],
                                      GSCRB[0:16, 0:16], True, True,
                                      waits=bw(pq) + ((sV, vdg),))
                            use["biamm"] = vtb2
                            a_bia = aop("activation", BIA1[:, 0:16], pq[:, 0:16],
                                        AF.Copy, scale=-0.125,
                                        waits=((sT, vtb2),))
                            bs(pq, sA, a_bia)
                            use["bia_a"] = a_bia

                    SW[0] = bool(SW_P2[0])
                    psS2 = [[PS[0], PS[3]], [PS[1], PS[4]]]
                    psOL = [PS[2], PS[5]]
                    psL2 = [PS6, PS7]
                    for pr in range(NH // 2):
                        c = pr
                        for ti, tq in enumerate(hqs):
                            qc = tq
                            qsl_q = slice(ti * TQ, (ti + 1) * TQ)
                            ktmax = qc * 4 + 4
                            vt_s = {}
                            ostate = {"vt_o": 0, "vt_l": 0, "vt_oA": 0, "vt_lA": 0}

                            def emit_S(kt, pr=pr, qsl_q=qsl_q):
                                for hi in range(2):
                                    h = 2 * pr + hi
                                    pS = psS2[hi][kt % 2]
                                    if is0:
                                        mm(pS[:, 0:TQ],
                                           AK(c)[:, kt * 128:(kt + 1) * 128],
                                           AQZ(c, hi)[:, qsl_q], True, False,
                                           waits=bw(pS) + ((sA, a_q), (sV, v_zq)),
                                           swkind="s")
                                        vt = mm(pS[:, 0:TQ],
                                                NEGE16[:, h * 128:(h + 1) * 128],
                                                AUXR[0:128, 0:TQ], False, True,
                                                waits=((sV, v_aux), v_ng),
                                                swkind="s")
                                        use["aux_mm"] = vt
                                    else:
                                        vt = mm(pS[:, 0:TQ],
                                                AK(c)[:, kt * 128:(kt + 1) * 128],
                                                AQZ(c, hi)[:, qsl_q], True,
                                                True,
                                                waits=bw(pS) + ((sA, a_q),
                                                                (sV, v_zq)),
                                                swkind="s")
                                    vt_s[(hi, kt)] = vt

                            def emit_C(kt, pr=pr, qc=qc, ktmax=ktmax):
                                voff = max(0, (kt - qc * 4) * 128)
                                first = (kt == 0)
                                last = (kt == ktmax - 1)
                                for hi in range(2):
                                    h = 2 * pr + hi
                                    pS = psS2[hi][kt % 2]
                                    vt = vt_s[(hi, kt)]
                                    if kt >= qc * 4:
                                        vtri = vop("tensor_tensor",
                                                   pS[:, voff:voff + 128],
                                                   pS[:, voff:voff + 128],
                                                   CTRI[:, :], ALU.add,
                                                   waits=((sT, vt), v_consts))
                                        wa = ((sV, vtri),)
                                    else:
                                        wa = ((sT, vt),)
                                    slab_i = 2 * hi + kt % 2
                                    slab = PT[:, slab_i * TQ + voff:
                                              (slab_i + 1) * TQ]
                                    wa = wa + ((sT, use.get(("pt", slab_i), 0)),)
                                    if is0:
                                        a_exp = aop("activation", slab,
                                                    pS[:, voff:TQ], AF.Exp,
                                                    scale=0.125,
                                                    bias=BIASC[:, h * 16 + kt:
                                                               h * 16 + kt + 1],
                                                    waits=wa + ((sV, v_biasc),))
                                    else:
                                        a_exp = aop("activation", slab,
                                                    pS[:, voff:TQ], AF.Exp,
                                                    scale=0.125,
                                                    bias=BIA1[:, h:h + 1],
                                                    waits=wa +
                                                    ((sA, use.get("bia_a", 0)),))
                                    bs(pS, sA, a_exp)
                                    wo = bw(psOL[hi]) if first else ()
                                    wl = bw(psL2[hi]) if first else ()
                                    if hi == 0:
                                        vt_o = mm(psOL[0][0:64, voff:TQ],
                                                  AVS(kt)[:, h * DH:(h + 1) * DH],
                                                  slab, first, last,
                                                  waits=wo + ((sA, a_exp),),
                                                  swkind="av")
                                        vt_l = mm(psL2[0][:, voff:TQ],
                                                  CONES[:, :], slab, first, last,
                                                  waits=wl + ((sA, a_exp),),
                                                  inc=True, swkind="l")
                                        ostate["vt_oA"], ostate["vt_lA"] = vt_o, vt_l
                                    else:
                                        vt_o = mm(psOL[1][64:128, voff:TQ],
                                                  AVS(kt)[:, h * DH:(h + 1) * DH],
                                                  slab, first, last,
                                                  waits=wo + ((sA, a_exp),),
                                                  tp=(0, 64), swkind="av")
                                        vt_l = mm(psL2[1][:, voff:TQ],
                                                  CONES[:, :], slab, first, last,
                                                  waits=wl + ((sA, a_exp),),
                                                  inc=True, swkind="l")
                                        ostate["vt_o"], ostate["vt_l"] = vt_o, vt_l
                                    use[("pt", slab_i)] = vt_l

                            for kt in range(ktmax):
                                emit_S(kt)
                                if kt > 0:
                                    emit_C(kt - 1)
                            emit_C(ktmax - 1)

                            OSTG = SQR[:, 0:TQ]
                            cpy = vop("tensor_copy", OSTG[0:64, :],
                                      psOL[0][0:64, 0:TQ],
                                      waits=((sT, ostate["vt_oA"]),))
                            bs(psOL[0], sV, cpy)
                            cpy = vop("tensor_copy", OSTG[64:128, :],
                                      psOL[1][64:128, 0:TQ],
                                      waits=((sV, cpy), (sT, ostate["vt_o"])))
                            bs(psOL[1], sV, cpy)
                            vr = vop("reciprocal", ONRM[0:64, 0:TQ],
                                     psL2[0][0:64, 0:TQ],
                                     waits=((sT, ostate["vt_lA"]),))
                            bs(psL2[0], sV, vr)
                            vr = vop("reciprocal", ONRM[64:128, 0:TQ],
                                     psL2[1][64:128, 0:TQ],
                                     waits=((sV, vr), (sT, ostate["vt_l"])))
                            bs(psL2[1], sV, vr)
                            vo = vop("tensor_tensor",
                                     XBC(c)[0:64, tq * TQ:(tq + 1) * TQ],
                                     OSTG[0:64, :], ONRM[0:64, 0:TQ],
                                     ALU.mult, waits=((sV, vr),))
                            vo = vop("tensor_tensor",
                                     XBC(c)[64:128, tq * TQ:(tq + 1) * TQ],
                                     OSTG[64:128, :], ONRM[64:128, 0:TQ],
                                     ALU.mult, waits=((sV, vo),))
                            v_out_v = vo
                            use["vo"] = vo
                            use["aux"] = v_aux
                    SW[0] = False
            v_attn_v = v_out_v
            if ATTN_UPTO <= 3:
                return post_ln_scoped(sV.n, "pa" + tag)

            with contextlib.ExitStack() as est:
                WBp = sb("WBp" + tag, [128, WSLOT], BF16, est)
                MUt = sb("MUe" + tag, [128, T], F32, est)
                RSTDt = sb("RSe" + tag, [128, T], F32, est)
                SCRt = sb("SCe" + tag, [128, (3 if is0 else 1) * TQ], F32, est)
                if is0:
                    APF = sb("APF", [128, NC * T], F32, est)
                else:
                    T1P = sb("T1P", [16, T], BF16, est)
                    WS2L = sb("WS2L", [16, D], BF16, est)

                a_t1p = 0
                if not is0:
                    v_ws2 = dma("gp", "ws2l", WS2L[:, :], dram["lb_p"], waits=((sT, sT.n),))
                    vla = dma("gp", "gscrb", GSCRB[:, 0:NC * 16].rearrange("p (c r) -> p c r", c=NC),
                              dram["la_p"].rearrange("(c p) r -> p c r", p=128),
                              waits=((sT, sT.n),))
                    for tq in range(NT):
                        pa = PS[4]
                        vt = 0
                        for c in range(NC):
                            vt = mm(pa[0:16, :], GSCRB[:, c * 16:(c + 1) * 16],
                                    XBC(c)[:, tq * TQ:(tq + 1) * TQ], c == 0,
                                    c == NC - 1,
                                    waits=(bw(pa) if c == 0 else ()) +
                                    (vla, (sV, v_attn_v)))
                        a_t1p = aop("activation", T1P[:, tq * TQ:(tq + 1) * TQ],
                                    pa[0:16, :], AF.Copy, scale=2.0,
                                    waits=((sT, vt),))
                        bs(pa, sA, a_t1p)

                wp = dram["w_o0" if is0 else "w_p1"].rearrange(
                    "(c p) n -> p c n", p=128)
                pb4 = [PS[0], PS[1], PS[2], PS[3]]
                pidx = 0
                v_add = 0
                v_xcc = [0] * NC
                a_apf = 0
                for ns in range(2):
                    slot = 0
                    v_wl = wload(WBp, slot, wp[:, :, ns * TQ:(ns + 1) * TQ], WSLOT)
                    for nsub in range(4):
                        n128 = ns * 4 + nsub
                        for tq in range(NT):
                            pa = pb4[pidx]
                            pidx = (pidx + 1) % 4
                            vt = 0
                            for cc in range(NC):
                                w0 = bw(pa) if cc == 0 else ()
                                vt = mm(pa[:, :],
                                        WBp[:, cc * TQ + nsub * 128:
                                            cc * TQ + nsub * 128 + 128],
                                        XBC(cc)[:, tq * TQ:(tq + 1) * TQ],
                                        cc == 0, (cc == NC - 1) and is0,
                                        waits=w0 + (v_wl, (sV, v_attn_v)))
                            if not is0:
                                vt = mm(pa[:, :], WS2L[:, n128 * 128:(n128 + 1) * 128],
                                        T1P[:, tq * TQ:(tq + 1) * TQ], False, True,
                                        waits=((sA, a_t1p), v_ws2))
                            wuse(WBp, slot, vt)
                            sl = slice(tq * TQ, (tq + 1) * TQ)
                            if is0:
                                a_apf = aop("activation",
                                            APF[:, n128 * T:(n128 + 1) * T][:, sl],
                                            pa[:, :], AF.Copy, waits=((sT, vt),))
                                bs(pa, sA, a_apf)
                            else:
                                vv = vop("tensor_scalar", SCRt[:, 0:TQ], pa[:, :],
                                         COLS[:, 136 + n128:137 + n128], None,
                                         ALU.add, waits=((sT, vt),))
                                v_add = vop("tensor_tensor", XC(n128)[:, sl],
                                            XC(n128)[:, sl], SCRt[:, 0:TQ], ALU.add,
                                            waits=((sV, vv),))
                                v_xcc[n128] = v_add
                                bs(pa, sV, v_add)
                t_xbr = sT.n

                if is0:
                    a_c2 = 0
                    for c in range(NC):
                        a_c2 = aop("activation", XBC(c), APF[:, c * T:(c + 1) * T],
                                   AF.Copy, waits=((sA, a_apf), (sV, v_attn_v)))
                    v_add, _, _, v_xcc = ln_full(
                        MUt, RSTDt, SCRt, lambda c: XBC(c), a_c2,
                        gcol=COLS[:, 16:24], bcol=COLS[:, 24:32],
                        src_fn=lambda c: APF[:, c * T:(c + 1) * T],
                        add_to_x=True)
                    t_xbr = 0
            return post_ln_scoped(v_xcc, "pa" + tag, t_xbr=t_xbr)

        def ffn0(v_in_v, a_in_a, atc_in=None):

            def wh(tq, c):
                if atc_in is not None:
                    return ((sA, atc_in[tq][c]),)
                return ((sA, a_in_a), (sV, v_in_v))
            st = contextlib.ExitStack()
            with st:
                HID = sb("HID", [128, 32 * 1024], BF16, st)
                WBf = sb("WBf", [128, 2 * WSLOT], BF16, st)
                B0 = sb("B0f", [128, 2 * TQ], F32, st)
                B1 = sb("B1f", [128, 2 * TQ], F32, st)
                B2 = sb("B2f", [128, 2 * TQ], F32, st)
                SINB = sb("SINB", [128, 2 * TQ], F32, st)
                w1 = dram["w_ff1"].rearrange("(c p) n -> p c n", p=128)
                w2 = dram["w_ff2"].rearrange("(f p) n -> p f n", p=128)
                pb4 = [PS[0], PS[1], PS[2], PS[3]]
                pidx = 0
                v_add = 0
                v_xcc = [0] * NC
                use_f = {}
                for half in range(2):
                    tqs = (2 * half, 2 * half + 1)
                    a_hid = 0
                    a_hid_t = {}
                    for fs in range(8):
                        slot = fs % 2
                        v_wl = wload(WBf, slot, w1[:, :, fs * TQ:(fs + 1) * TQ],
                                     WSLOT)
                        for fsub in range(4):
                            fc = fs * 4 + fsub
                            sthc = COLS[:, 160 + fc:161 + fc]
                            cthc = COLS[:, 192 + fc:193 + fc]
                            ecol = COLS[:, 72 + fc:73 + fc]
                            b1c = COLS[:, 104 + fc:105 + fc]
                            for ti, tq in enumerate(tqs):
                                pa = pb4[pidx]
                                pidx = (pidx + 1) % 4
                                vt = 0
                                for c in range(NC):
                                    w0 = bw(pa) if c == 0 else ()
                                    vt = mm(pa[:, :],
                                            WBf[:, slot * WSLOT + c * TQ +
                                                fsub * 128:slot * WSLOT + c * TQ +
                                                fsub * 128 + 128],
                                            XBC(c)[:, tq * TQ:(tq + 1) * TQ],
                                            c == 0, c == NC - 1,
                                            waits=w0 + (v_wl,) + wh(tq, c))
                                wuse(WBf, slot, vt)
                                par = (fc * 2 + ti) % 2
                                qsl = slice(par * TQ, (par + 1) * TQ)
                                a_cz = aop("activation", B1[:, qsl], pa[:, :],
                                           AF.Sin, scale=0.1, bias=CPI2[:, 0:1],
                                           waits=((sT, vt),
                                                  (sV, use_f.get(("b1", par), 0))))
                                aop("activation", SINB[:, qsl], pa[:, :],
                                    AF.Sin, scale=0.1, waits=())
                                a_h = aop("activation", B0[:, qsl], pa[:, :],
                                          AF.Copy, waits=())
                                bs(pa, sA, a_h)
                                vop("tensor_scalar", B1[:, qsl], B1[:, qsl],
                                    sthc, None, ALU.mult,
                                    waits=((sA, a_h), (sV, v_thb)))
                                vop("scalar_tensor_tensor", B1[:, qsl],
                                    SINB[:, qsl], cthc, B1[:, qsl], ALU.mult,
                                    ALU.add)
                                vv = vop("scalar_tensor_tensor", B2[:, qsl],
                                         B1[:, qsl], ecol, B0[:, qsl], ALU.mult,
                                         ALU.add,
                                         waits=((sA, use_f.get(("rl", par), 0)),))
                                use_f[("b1", par)] = vv
                                dst = HID[:, fc * 1024 + ti * TQ:
                                          fc * 1024 + (ti + 1) * TQ]
                                a_hid = aop("activation", dst, B2[:, qsl], AF.Relu,
                                            bias=b1c, waits=((sV, vv),))
                                use_f[("rl", par)] = a_hid
                                a_hid_t[(fc, ti)] = a_hid
                    for n128 in range(8):
                        slot2 = n128 % 2
                        v_wl = wload(WBf, slot2,
                                     w2[:, :, n128 * 128:(n128 + 1) * 128],
                                     32 * 128)
                        b2c = COLS[:, 64 + n128:65 + n128]
                        for ti, tq in enumerate(tqs):
                            pa = pb4[pidx]
                            pidx = (pidx + 1) % 4
                            vt = 0
                            for fc in range(32):
                                w0 = bw(pa) if fc == 0 else ()
                                vt = mm(pa[:, :],
                                        WBf[:, slot2 * WSLOT + fc * 128:
                                            slot2 * WSLOT + fc * 128 + 128],
                                        HID[:, fc * 1024 + ti * TQ:
                                            fc * 1024 + (ti + 1) * TQ],
                                        fc == 0, fc == 31,
                                        waits=w0 + (v_wl, (sA, a_hid_t[(fc, ti)])))
                            wuse(WBf, slot2, vt)
                            sl = slice(tq * TQ, (tq + 1) * TQ)
                            vv = vop("scalar_tensor_tensor", XC(n128)[:, sl],
                                     pa[:, :], 1.0, XC(n128)[:, sl], ALU.bypass,
                                     ALU.add, waits=((sT, vt),))
                            v_add = vop("tensor_scalar", XC(n128)[:, sl],
                                        XC(n128)[:, sl], b2c, None, ALU.add,
                                        waits=((sV, vv),))
                            v_xcc[n128] = v_add
                            bs(pa, sV, v_add)
            return post_ln_scoped(v_xcc, "pf")

        def moe(v_in_v, a_in_a, atc_in=None):

            def wh(tq, c):
                if atc_in is not None:
                    return ((sA, atc_in[tq][c]),)
                return ((sA, a_in_a), (sV, v_in_v))
            st = contextlib.ExitStack()
            with st:
                HPAIR = sb("HPAIR", [128, 8 * T], BF16, st)
                WBm = sb("WBm", [128, 4 * WSLOT], BF16, st)
                WTOK = sb("WTOK", [128, 256], F32, st)
                WT = sb("WT", [16, T], BF16, st)
                WBC = sb("WBC", [128, T], F32, st)
                SGBr = sb("SGBr", [128, T], F32, st)
                SGR = sb("SGR", [128, TQ], BF16, st)
                E16P = sb("E16P", [NH, NH * 128], BF16, st)
                v_e16p = dma("sync", "e16p", E16P[0:NH, :], dram["c_e16p"])
                USR = sb("USR", [128, TQ], F32, st)
                SCRm = sb("SCRm", [128, TQ], F32, st)
                pb4 = [PS[0], PS[1], PS[2], PS[3]]
                pidx = 0
                v_wt = 0
                pend_w = [None]

                def emit_wtr(vvw, ttw):
                    vtt = tr(PS[4][0:16, 0:128], WTOK[:, ttw * 16:(ttw + 1) * 16],
                             CEYEF[:, :], waits=bw(PS[4]) + ((sV, vvw),))
                    vwt = vop("tensor_copy", WT[0:16, ttw * 128:(ttw + 1) * 128],
                              PS[4][0:16, 0:128], waits=((sT, vtt),))
                    bs(PS[4], sV, vwt)
                    return vwt

                for tt in range(16):
                    pa = PS[5]
                    vt = 0
                    for c in range(NC):
                        vt = mm(pa[:, 0:16], XBC(c)[:, tt * 128:(tt + 1) * 128],
                                WGB[:, c * 16:(c + 1) * 16], c == 0, c == NC - 1,
                                waits=(bw(pa) if c == 0 else ()) +
                                (v_wgb,) + wh(tt // 4, c))
                    g0 = GSCR[:, 16:32]
                    vv = vop("reduce_max", GSCR[:, 32:33], pa[:, 0:16], AX,
                             negate=True, waits=((sT, vt),))
                    a1 = aop("activation", g0, pa[:, 0:16], AF.Exp,
                             bias=GSCR[:, 32:33], waits=((sV, vv),))
                    bs(pa, sA, a1)
                    vv = vop("reduce_max", GSCR[:, 33:34], g0, AX, waits=((sA, a1),))
                    vv = vop("tensor_scalar", GSCR[:, 48:64], g0, GSCR[:, 33:34],
                             None, ALU.is_ge, waits=((sV, vv),))
                    vv = vop("scalar_tensor_tensor", GSCR[:, 0:16], GSCR[:, 48:64],
                             -1e30, g0, ALU.mult, ALU.add, waits=((sV, vv),))
                    vv = vop("reduce_max", GSCR[:, 34:35], GSCR[:, 0:16], AX,
                             waits=((sV, vv),))
                    vv = vop("tensor_scalar", GSCR[:, 48:64], g0, GSCR[:, 34:35],
                             None, ALU.is_ge, waits=((sV, vv),))
                    vv = vop("tensor_tensor", GSCR[:, 35:36], GSCR[:, 33:34],
                             GSCR[:, 34:35], ALU.add, waits=((sV, vv),))
                    vv = vop("reciprocal", GSCR[:, 35:36], GSCR[:, 35:36],
                             waits=((sV, vv),))
                    vv = vop("scalar_tensor_tensor", WTOK[:, tt * 16:(tt + 1) * 16],
                             g0, GSCR[:, 35:36], GSCR[:, 48:64], ALU.mult, ALU.mult,
                             waits=((sV, vv),))
                    if pend_w[0] is not None:
                        v_wt = emit_wtr(*pend_w[0])
                    pend_w[0] = (vv, tt)
                if pend_w[0] is not None:
                    v_wt = emit_wtr(*pend_w[0])

                v_add = 0
                v_xcc = [0] * NC
                for pr in range(8):
                    v_h = 0
                    v_h_t = {}
                    sg0 = (pr % 2) * 2
                    sg1 = 2 - sg0
                    sdn = sg0
                    for ei, e in enumerate((2 * pr, 2 * pr + 1)):
                        sgu = sg0 if ei == 0 else sg1
                        vbc = 0
                        for tq in range(NT):
                            pa = PS[5]
                            vt = mm(pa[:, :], E16P[:, e * 128:(e + 1) * 128],
                                    WT[0:NH, tq * TQ:(tq + 1) * TQ], True, True,
                                    waits=bw(pa) + ((sV, v_wt), v_e16p))
                            vbc = vop("tensor_copy", WBC[:, tq * TQ:(tq + 1) * TQ],
                                      pa[:, :], waits=((sT, vt),))
                            bs(pa, sV, vbc)
                        ge = dram["we_g"][e].rearrange("(c p) n -> p c n", p=128)
                        ue = dram["we_u"][e].rearrange("(c p) n -> p c n", p=128)
                        v_wg = wload(WBm, sgu, ge, WSLOT)
                        v_wu = wload(WBm, sgu + 1, ue, WSLOT)
                        if ei == 1:
                            vd0 = wload(WBm, sdn, dram["we_d"][2 * pr].rearrange(
                                "(h p) n -> p h n", p=128), WSLOT)
                            vd1 = wload(WBm, sdn + 1,
                                        dram["we_d"][2 * pr + 1].rearrange(
                                            "(h p) n -> p h n", p=128), WSLOT)
                        for hc in range(4):
                            for tq in range(NT):
                                pg = pb4[pidx]
                                pidx = (pidx + 1) % 4
                                pu = pb4[pidx]
                                pidx = (pidx + 1) % 4
                                vtg = vtu = 0
                                for c in range(NC):
                                    w0 = bw(pg) if c == 0 else ()
                                    vtg = mm(pg[:, :],
                                             WBm[:, sgu * WSLOT + c * TQ +
                                                 hc * 128:sgu * WSLOT + c * TQ +
                                                 hc * 128 + 128],
                                             XBC(c)[:, tq * TQ:(tq + 1) * TQ],
                                             c == 0, c == NC - 1,
                                             waits=w0 + (v_wg,))
                                for c in range(NC):
                                    w0 = bw(pu) if c == 0 else ()
                                    vtu = mm(pu[:, :],
                                             WBm[:, (sgu + 1) * WSLOT + c * TQ +
                                                 hc * 128:(sgu + 1) * WSLOT +
                                                 c * TQ + hc * 128 + 128],
                                             XBC(c)[:, tq * TQ:(tq + 1) * TQ],
                                             c == 0, c == NC - 1,
                                             waits=w0 + (v_wu,))
                                wuse(WBm, sgu, vtg)
                                wuse(WBm, sgu + 1, vtu)
                                a_sg = aop("activation", SGR[:, :], pg[:, :],
                                           AF.Silu, waits=((sT, vtg), (sV, v_h)))
                                bs(pg, sA, a_sg)
                                vv = vop("scalar_tensor_tensor", USR[:, :],
                                         pu[:, :], 1.0,
                                         WBC[:, tq * TQ:(tq + 1) * TQ], ALU.bypass,
                                         ALU.mult, waits=((sT, vtu), (sV, vbc)))
                                bs(pu, sV, vv)
                                dst = HPAIR[:, (ei * 4 + hc) * T + tq * TQ:
                                            (ei * 4 + hc) * T + (tq + 1) * TQ]
                                v_h = vop("tensor_tensor", dst, SGR[:, :],
                                          USR[:, :], ALU.mult,
                                          waits=((sV, vv), (sA, a_sg)))
                                v_h_t[(ei * 4 + hc, tq)] = v_h
                    for n128 in range(NC):
                        for tq in range(NT):
                            pa = pb4[pidx]
                            pidx = (pidx + 1) % 4
                            vt = 0
                            first = True
                            for ei in range(2):
                                vwl = vd0 if ei == 0 else vd1
                                for hc in range(4):
                                    w0 = bw(pa) if first else ()
                                    vt = mm(pa[:, :],
                                            WBm[:, (sdn + ei) * WSLOT + hc * 1024 +
                                                n128 * 128:(sdn + ei) * WSLOT +
                                                hc * 1024 + n128 * 128 + 128],
                                            HPAIR[:, (ei * 4 + hc) * T + tq * TQ:
                                                  (ei * 4 + hc) * T +
                                                  (tq + 1) * TQ],
                                            first, (ei == 1 and hc == 3),
                                            waits=w0 + (vwl,
                                                        (sV, v_h_t[(ei * 4 + hc,
                                                                    tq)])))
                                    first = False
                            wuse(WBm, sdn, vt)
                            wuse(WBm, sdn + 1, vt)
                            sl = slice(tq * TQ, (tq + 1) * TQ)
                            v_add = vop("scalar_tensor_tensor", XC(n128)[:, sl],
                                        pa[:, :], 1.0, XC(n128)[:, sl], ALU.bypass,
                                        ALU.add, waits=((sT, vt),))
                            bs(pa, sV, v_add)

                vsg = 0
                for tq in range(NT):
                    pa = PS[5]
                    vt = 0
                    for c in range(NC):
                        vt = mm(pa[:, :], WSGB[:, c * 128:(c + 1) * 128],
                                XBC(c)[:, tq * TQ:(tq + 1) * TQ], c == 0,
                                c == NC - 1,
                                waits=(bw(pa) if c == 0 else ()) + (v_wsgb,))
                    vsg = aop("activation", SGBr[:, tq * TQ:(tq + 1) * TQ],
                              pa[:, :], AF.Sigmoid, waits=((sT, vt),))
                    bs(pa, sA, vsg)
                wsg_ = dram["ws_g"].rearrange("(c p) n -> p c n", p=128)
                wsu_ = dram["ws_u"].rearrange("(c p) n -> p c n", p=128)
                wsd_ = dram["ws_d"].rearrange("(f p) n -> p f n", p=128)
                for half in range(2):
                    tqs = (2 * half, 2 * half + 1)
                    a_hid = 0
                    a_hid_t = {}
                    for fs in range(4):
                        v_wg = wload(WBm, 0, wsg_[:, :, fs * TQ:(fs + 1) * TQ],
                                     WSLOT)
                        v_wu = wload(WBm, 1, wsu_[:, :, fs * TQ:(fs + 1) * TQ],
                                     WSLOT)
                        for fsub in range(4):
                            fc = fs * 4 + fsub
                            for ti, tq in enumerate(tqs):
                                pg = pb4[pidx]
                                pidx = (pidx + 1) % 4
                                pu = pb4[pidx]
                                pidx = (pidx + 1) % 4
                                vtg = vtu = 0
                                for c in range(NC):
                                    w0 = bw(pg) if c == 0 else ()
                                    vtg = mm(pg[:, :],
                                             WBm[:, c * TQ + fsub * 128:
                                                 c * TQ + fsub * 128 + 128],
                                             XBC(c)[:, tq * TQ:(tq + 1) * TQ],
                                             c == 0, c == NC - 1,
                                             waits=w0 + (v_wg,))
                                for c in range(NC):
                                    w0 = bw(pu) if c == 0 else ()
                                    vtu = mm(pu[:, :],
                                             WBm[:, WSLOT + c * TQ + fsub * 128:
                                                 WSLOT + c * TQ + fsub * 128 + 128],
                                             XBC(c)[:, tq * TQ:(tq + 1) * TQ],
                                             c == 0, c == NC - 1,
                                             waits=w0 + (v_wu,))
                                wuse(WBm, 0, vtg)
                                wuse(WBm, 1, vtu)
                                a_sg2 = aop("activation", SGR[:, :], pg[:, :],
                                            AF.Silu, waits=((sT, vtg), (sV, a_hid)))
                                bs(pg, sA, a_sg2)
                                dst = HPAIR[:, fc * 1024 + ti * TQ:
                                            fc * 1024 + (ti + 1) * TQ]
                                a_hid = vop("tensor_tensor", dst, SGR[:, :],
                                            pu[:, :], ALU.mult,
                                            waits=((sA, a_sg2), (sT, vtu)))
                                a_hid_t[(fc, ti)] = a_hid
                                bs(pu, sV, a_hid)
                    for ws in range(2):
                        v_wl = wload(WBm, 0, wsd_[:, :, ws * TQ:(ws + 1) * TQ],
                                     16 * TQ)
                        for nsub in range(4):
                            n128 = ws * 4 + nsub
                            for ti, tq in enumerate(tqs):
                                pa = pb4[pidx]
                                pidx = (pidx + 1) % 4
                                vt = 0
                                for fc in range(16):
                                    w0 = bw(pa) if fc == 0 else ()
                                    vt = mm(pa[:, :],
                                            WBm[:, fc * TQ + nsub * 128:
                                                fc * TQ + nsub * 128 + 128],
                                            HPAIR[:, fc * 1024 + ti * TQ:
                                                  fc * 1024 + (ti + 1) * TQ],
                                            fc == 0, fc == 15,
                                            waits=w0 + (v_wl,
                                                        (sV, a_hid_t[(fc, ti)])))
                                wuse(WBm, 0, vt, ncols=2 * WSLOT)
                                sl = slice(tq * TQ, (tq + 1) * TQ)
                                vv = vop("scalar_tensor_tensor", SCRm[:, 0:TQ],
                                         pa[:, :], 1.0, SGBr[:, sl], ALU.bypass,
                                         ALU.mult, waits=((sT, vt), (sA, vsg)))
                                bs(pa, sV, vv)
                                v_add = vop("tensor_tensor", XC(n128)[:, sl],
                                            XC(n128)[:, sl], SCRm[:, 0:TQ],
                                            ALU.add, waits=((sV, vv),))
                                v_xcc[n128] = v_add
            return post_ln_scoped(v_xcc, "pm")

        vv, va, atc = sV.n, sA.n, None
        if 0 in phases:
            vv, va, atc, _ = attention(0, vv, va, atc)
        if 1 in phases:
            vv, va, atc, _ = ffn0(vv, va, atc)
        if 2 in phases:
            vv, va, atc, _ = attention(1, vv, va, atc)
        if 3 in phases:
            vv, va, atc, _ = moe(vv, va, atc)

        P.wait("sync", sV, vv)
        P.wait("sync", sA, va)
        P.wait("sync", v_x[0], v_x[1])
        v_st = dma("sync", "out", dout.rearrange("(c p) t -> p c t", p=128),
                   X[:, :].rearrange("p (c t) -> p c t", c=NC))
        P.wait("sync", v_st[0], v_st[1])

        with nc.Block() as block:
            P.finalize(block)

    return nc


def kernel(**inputs):
    g = host_prepare(inputs)
    x = np.asarray(inputs["L0_attn_x"], np.float32)
    nc = build_program()
    in_maps = []
    for core in range(8):
        m = dict(g)
        m["x"] = np.ascontiguousarray(x[core].T)
        in_maps.append(m)
    res = run_bass_kernel_spmd(nc, in_maps, core_ids=list(range(8)))
    out = np.stack([np.ascontiguousarray(r["out"].T) for r in res.results], axis=0)
    return out.astype(np.float32)


# version 25
# speedup vs baseline: 1.0011x; 1.0011x over previous
import contextlib
import math
import numpy as np
import concourse.bass as bass
import concourse.mybir as mybir
from concourse.bass_utils import run_bass_kernel_spmd

F32 = mybir.dt.float32
BF16 = mybir.dt.bfloat16
I32 = mybir.dt.int32
AX = mybir.AxisListType.X
AF = mybir.ActivationFunctionType
ALU = mybir.AluOpType

D = 1024
T = 2048
NC = 8
NT = 4
TQ = 512
NH = 16
DH = 64
EPS = 1e-5
TWO_PI = 2.0 * math.pi
SIN_SCALE = float(np.nextafter(np.float32(TWO_PI), np.float32(0.0)))
WSLOT = NC * TQ


def _bf():
    import ml_dtypes
    return ml_dtypes.bfloat16


class Sem:
    def __init__(self, handle, name):
        self.h = handle
        self.name = name
        self.n = 0

    def inc(self, k=1):
        self.n += k
        return self.n


class Prog:
    ENGINES = ("sync", "tensor", "vector", "scalar", "gpsimd")

    def __init__(self):
        self.items = {e: [] for e in self.ENGINES}
        self.lastwait = {}

    def em(self, eng, fn):
        self.items[eng].append(fn)

    def wait(self, eng, sem, val):
        if sem is None or not val or val <= 0:
            return
        key = (eng, sem.name)
        if self.lastwait.get(key, -1) >= val:
            return
        self.lastwait[key] = val
        self.em(eng, lambda E, s=sem.h, v=val: E.wait_ge(s, v))

    def finalize(self, block):
        for name in self.ENGINES:
            fns = self.items[name]

            def body(E, fns=fns):
                for fn in fns:
                    fn(E)
            getattr(block, name)(body)


def host_prepare(inputs):
    bf = _bf()

    def t_(a):
        return np.ascontiguousarray(np.asarray(a, np.float32).T)

    def c_(a, shape=None):
        a = np.asarray(a, np.float32)
        if shape is not None:
            a = a.reshape(shape)
        return np.ascontiguousarray(a)

    g = {}
    g["w_qkv0"] = t_(inputs["L0_attn_in_w"])
    g["w_o0"] = t_(inputs["L0_attn_out_w"])
    g["ln1_g"] = c_(inputs["L0_attn_ln1_g"], (NC, 128))
    g["ln1_b"] = c_(inputs["L0_attn_ln1_b"], (NC, 128))
    g["ln2_g"] = c_(inputs["L0_attn_ln2_g"], (NC, 128))
    g["ln2_b"] = c_(inputs["L0_attn_ln2_b"], (NC, 128))
    g["slopes"] = c_(inputs["L0_attn_slopes"], (1, NH))
    g["smear"] = c_(inputs["L0_attn_smear"], (1, NH))
    g["w_ff1"] = c_(inputs["L0_ffn_W1"])
    g["w_ff2"] = c_(inputs["L0_ffn_W2"])
    g["ff_b1"] = c_(inputs["L0_ffn_b1"], (32, 128))
    g["ff_b2"] = c_(inputs["L0_ffn_b2"], (NC, 128))
    g["ff_th"] = c_(inputs["L0_ffn_quantum_phases"], (32, 128))
    g["ff_wr"] = c_(inputs["L0_ffn_quantum_weights_real"], (32, 128))
    g["ff_wi"] = c_(inputs["L0_ffn_quantum_weights_imag"], (32, 128))
    g["w_qkv1"] = c_(inputs["L1_attn_w_attn"])
    g["b_qkv1"] = c_(inputs["L1_attn_b_attn"], (24, 128))
    g["la_a"] = c_(inputs["L1_attn_la_attn"])
    g["lb_a"] = c_(inputs["L1_attn_lb_attn"])
    g["w_p1"] = c_(inputs["L1_attn_w_proj"])
    g["b_p1"] = c_(inputs["L1_attn_b_proj"], (NC, 128))
    g["la_p"] = c_(inputs["L1_attn_la_proj"])
    g["lb_p"] = c_(inputs["L1_attn_lb_proj"])
    g["kv_s"] = c_(inputs["L1_attn_kv_scale"], (1, 1))
    g["kv_z"] = c_(inputs["L1_attn_kv_zero_point"], (1, 1))
    g["wg"] = t_(inputs["L1_ffn_Wg"])
    g["we_g"] = np.ascontiguousarray(np.transpose(np.asarray(inputs["L1_ffn_We_gate"], np.float32), (0, 2, 1)))
    g["we_u"] = np.ascontiguousarray(np.transpose(np.asarray(inputs["L1_ffn_We_up"], np.float32), (0, 2, 1)))
    g["we_d"] = np.ascontiguousarray(np.transpose(np.asarray(inputs["L1_ffn_We_down"], np.float32), (0, 2, 1)))
    g["ws_g"] = t_(inputs["L1_ffn_Ws_gate"])
    g["ws_u"] = t_(inputs["L1_ffn_Ws_up"])
    g["ws_d"] = t_(inputs["L1_ffn_Ws_down"])
    g["wsg"] = np.ascontiguousarray(
        np.tile(t_(inputs["L1_ffn_Wsg"]), (1, 128)))

    g["c_ones"] = np.ones((128, 128), np.float32).astype(bf)
    g["c_eye_f32"] = np.eye(128, dtype=np.float32)
    g["c_eye_bf"] = np.eye(128, dtype=np.float32).astype(bf)
    g["c_ones_row_f32"] = np.ones((1, 128), np.float32)
    p = np.arange(128)
    esel = np.zeros((64, NH, NH), np.float32)
    for h in range(NH):
        esel[:, h, h] = 1.0
    esel = np.ascontiguousarray(esel.reshape(64, NH * NH))
    g["c_esel"] = np.concatenate([esel, esel], axis=0).astype(bf)
    e64ab = np.zeros((96, 128), np.float32)
    e64ab[0, 0:64] = 1.0
    e64ab[32, 64:128] = 1.0
    g["c_e64ab"] = e64ab
    nege16 = np.zeros((NH, NH, 128), np.float32)
    e16p = np.zeros((NH, NH, 128), np.float32)
    for h in range(NH):
        nege16[h, h, :] = -1.0
        e16p[h, h, :] = 1.0
    g["c_nege16"] = np.concatenate(
        [np.ascontiguousarray(nege16.reshape(NH, NH * 128)),
         np.zeros((128 - NH, NH * 128), np.float32)], axis=0).astype(bf)
    g["c_e16p"] = np.ascontiguousarray(e16p.reshape(NH, NH * 128)).astype(bf)
    qsel = np.zeros((128, NC, NH), np.float32)
    for c in range(NC):
        for j in (2 * c, 2 * c + 1):
            qsel[(j % 2) * 64:(j % 2) * 64 + 64, c, j] = 1.0
    g["c_qsel"] = np.ascontiguousarray(qsel.reshape(128, NC * NH)).astype(bf)
    g["c_tri"] = np.where(p[:, None] > p[None, :], np.float32(-1e30),
                          np.float32(0.0)).astype(np.float32)
    g["c_iotak"] = (p[:, None] + 128.0 * np.arange(16)[None, :]).astype(np.float32)
    g["c_arange"] = np.arange(T, dtype=np.float32).reshape(1, T).astype(bf)
    sel2 = np.zeros((NH, NC * 128), np.float32)
    for c in range(NC):
        for pp in range(128):
            sel2[2 * c + pp // 64, c * 128 + pp] = 1.0
    g["c_sel2"] = sel2.astype(bf)
    return g


DRAM_SPECS_F32 = {
    "w_qkv0": [D, 3 * D], "w_o0": [D, D],
    "ln1_g": [NC, 128], "ln1_b": [NC, 128], "ln2_g": [NC, 128], "ln2_b": [NC, 128],
    "slopes": [1, NH], "smear": [1, NH],
    "w_ff1": [D, 4096], "w_ff2": [4096, D],
    "ff_b1": [32, 128], "ff_b2": [NC, 128], "ff_th": [32, 128],
    "ff_wr": [32, 128], "ff_wi": [32, 128],
    "w_qkv1": [D, 3 * D], "b_qkv1": [24, 128],
    "la_a": [D, 16], "lb_a": [16, 3 * D],
    "w_p1": [D, D], "b_p1": [NC, 128], "la_p": [D, 16], "lb_p": [16, D],
    "kv_s": [1, 1], "kv_z": [1, 1],
    "wg": [D, 16],
    "we_g": [16, D, 512], "we_u": [16, D, 512], "we_d": [16, 512, D],
    "ws_g": [D, 2048], "ws_u": [D, 2048], "ws_d": [2048, D], "wsg": [D, 128],
    "c_eye_f32": [128, 128], "c_ones_row_f32": [1, 128],
    "c_e64ab": [96, 128],
    "c_tri": [128, 128], "c_iotak": [128, 16],
}
DRAM_SPECS_BF = {"c_ones": [128, 128], "c_eye_bf": [128, 128],
                 "c_esel": [128, NH * NH],
                 "c_nege16": [128, NH * 128], "c_e16p": [NH, NH * 128],
                 "c_qsel": [128, NC * NH],
                 "c_arange": [1, T], "c_sel2": [NH, NC * 128]}


ATTN_UPTO = 9
SW_P2 = [True]
SW_KINDS = [("r",)]


def build_program(phases=(0, 1, 2, 3)):
    nc = bass.Bass("TRN2", target_bir_lowering=False)
    dx = nc.dram_tensor("x", [D, T], F32, kind="ExternalInput").ap()
    dout = nc.dram_tensor("out", [D, T], F32, kind="ExternalOutput").ap()
    dram = {}
    for k, shp in DRAM_SPECS_F32.items():
        dram[k] = nc.dram_tensor(k, shp, F32, kind="ExternalInput").ap()
    for k, shp in DRAM_SPECS_BF.items():
        dram[k] = nc.dram_tensor(k, shp, BF16, kind="ExternalInput").ap()

    P = Prog()
    top = contextlib.ExitStack()
    wb_birth = {}

    def sb(name, shp, dt=BF16, st=None):
        t = (st or top).enter_context(nc.sbuf_tensor(name, shp, dt))
        wb_birth[id(t)] = sT.n if "sT" in locals() or True else 0
        return t

    with top:
        sG = Sem(top.enter_context(nc.semaphore("sG")), "sG")
        sS = Sem(top.enter_context(nc.semaphore("sS")), "sS")
        sT = Sem(top.enter_context(nc.semaphore("sT")), "sT")
        sV = Sem(top.enter_context(nc.semaphore("sV")), "sV")
        sA = Sem(top.enter_context(nc.semaphore("sA")), "sA")

        def sb(name, shp, dt=BF16, st=None):
            t = (st or top).enter_context(nc.sbuf_tensor(name, shp, dt))
            wb_birth[id(t)] = (sT.n, sV.n, sA.n)
            return t

        X = sb("X", [128, NC * T], F32)
        XB = sb("XB", [128, NC * T], BF16)
        SQR = sb("SQR", [128, 2 * TQ], BF16)
        SMALLF = sb("SMALLF", [128, 512], F32)
        COLS = sb("COLS", [128, 224], F32)
        QCOL = sb("QCOL", [128, 24], F32)
        KVC = sb("KVC", [128, 8], F32)
        CONES = sb("CONES", [128, 128], BF16)
        CEYEF = sb("CEYEF", [128, 128], F32)
        CEYEB = sb("CEYEB", [128, 128], BF16)
        CONER = sb("CONER", [1, 128], F32)
        CSEL128 = sb("CSEL128", [128, NC * NH], BF16)
        CTRI = sb("CTRI", [128, 128], F32)
        CIOTAK = sb("CIOTAK", [128, 16], F32)

        SLP8 = sb("SLP8", [16, 4], F32)
        SLP8B = sb("SLP8B", [16, 2], BF16)
        SLPB = sb("SLPB", [128, 16], F32)
        SCOLS = sb("SCOLS", [128, 16], F32)
        MAXK = sb("MAXK", [16, 8], F32)
        WGB = sb("WGB", [128, NC * 16], BF16)
        WSGB = sb("WSGB", [128, NC * 128], BF16)
        GSCR = sb("GSCR", [128, 64], F32)
        GSCRB = sb("GSCRB", [128, NC * 16], BF16)
        RI32A = sb("RI32A", [128, TQ], I32)
        CEPS = sb("CEPS", [128, 1], F32)
        CPI2 = sb("CPI2", [128, 1], F32)

        PS = [top.enter_context(nc.psum_tensor(f"ps{i}", [128, 512], F32))
              for i in range(6)]

        psum_owner = {}

        def bw(pst):
            w = psum_owner.get(id(pst))
            return (w,) if w else ()

        def bs(pst, sem, val):
            psum_owner[id(pst)] = (sem, val)

        dma_sems = {}

        def dma(kind, stream, out_ap, in_ap, waits=(), slow=False):
            eng = "sync" if kind == "sync" else "gpsimd"
            if stream not in dma_sems:
                h = top.enter_context(nc.semaphore("sD_" + stream))
                dma_sems[stream] = Sem(h, "sD_" + stream)
            s = dma_sems[stream]
            tb = wb_birth.get(id(getattr(out_ap, "tensor", None)))
            if tb is not None:
                waits = waits + ((sT, tb[0]), (sV, tb[1]), (sA, tb[2]))
            for (sem, val) in waits:
                P.wait(eng, sem, val)
            v = s.inc(16)
            P.em(eng, lambda E, o=out_ap, i=in_ap, sh=s.h, sl=slow:
                 E.dma_start(out=o, in_=i, allow_slow_non_contiguous=sl)
                 .then_inc(sh, 16))
            return (s, v)

        PACE = [12]
        SW = [False]
        SW_P2[0] = SW_P2[0]

        def mm(out_ap, lhsT, rhs, start, stop, waits=(), tp=None, inc=None,
               swkind=None):
            if inc is None:
                inc = stop
            if SW[0] and (swkind in SW_KINDS[0]):
                inc = True
                P.em("tensor", lambda E, v=sT.n: E.wait_ge(sT.h, v))
            elif start:
                P.em("tensor", lambda E, v=max(0, sT.n - PACE[0]):
                     E.wait_ge(sT.h, v))
            for (sem, val) in waits:
                P.wait("tensor", sem, val)
            if inc:
                v = sT.inc()
                P.em("tensor",
                     lambda E, o=out_ap, l=lhsT, r=rhs, a=start, b=stop, t=tp:
                     E.matmul(o, l, r, start=a, stop=b, tile_position=t)
                     .then_inc(sT.h, 1))
                return v
            P.em("tensor", lambda E, o=out_ap, l=lhsT, r=rhs, a=start, b=stop, t=tp:
                 E.matmul(o, l, r, start=a, stop=b, tile_position=t))
            return sT.n

        def tr(out_ap, in_ap, ident, waits=()):
            P.em("tensor", lambda E, v=max(0, sT.n - PACE[0]): E.wait_ge(sT.h, v))
            for (sem, val) in waits:
                P.wait("tensor", sem, val)
            v = sT.inc()
            P.em("tensor", lambda E, o=out_ap, i=in_ap, d=ident:
                 E.transpose(o, i, d).then_inc(sT.h, 1))
            return v

        def vop(name, *args, waits=(), **kw):
            P.em("vector", lambda E, v=sV.n: E.wait_ge(sV.h, v))
            for (sem, val) in waits:
                P.wait("vector", sem, val)
            v = sV.inc()
            P.em("vector", lambda E, nm=name, a=args, k=kw:
                 getattr(E, nm)(*a, **k).then_inc(sV.h, 1))
            return v

        def aop(name, *args, waits=(), **kw):
            P.em("scalar", lambda E, v=sA.n: E.wait_ge(sA.h, v))
            for (sem, val) in waits:
                P.wait("scalar", sem, val)
            v = sA.inc()
            P.em("scalar", lambda E, nm=name, a=args, k=kw:
                 getattr(E, nm)(*a, **k).then_inc(sA.h, 1))
            return v

        def gop(name, *args, waits=(), **kw):
            P.em("gpsimd", lambda E, v=sG.n: E.wait_ge(sG.h, v))
            for (sem, val) in waits:
                P.wait("gpsimd", sem, val)
            v = sG.inc()
            P.em("gpsimd", lambda E, nm=name, a=args, k=kw:
                 getattr(E, nm)(*a, **k).then_inc(sG.h, 1))
            return v

        def XC(c):
            return X[:, c * T:(c + 1) * T]

        def XBC(c):
            return XB[:, c * T:(c + 1) * T]

        slot_use = {}

        def wload(wb, slot, dram_ap, ncols, swidth=None):
            bt, bv, ba = wb_birth.get(id(wb), (0, 0, 0))
            if ncols <= WSLOT:
                off = slot * (swidth or WSLOT)
                waits = ((sT, slot_use.get((id(wb), slot), 0)),
                         (sT, slot_use.get((id(wb), "all"), 0)),
                         (sT, bt), (sV, bv), (sA, ba))
                stream = wb.name + "_" + str(slot)
            else:
                off = 0
                waits = ((sT, max(slot_use.get((id(wb), 0), 0),
                                  slot_use.get((id(wb), 1), 0),
                                  slot_use.get((id(wb), "all"), 0), bt)),
                         (sV, bv), (sA, ba))
                stream = wb.name + "_all"
            return dma("gp", stream, wb[:, off:off + ncols], dram_ap, waits=waits)

        def wuse(wb, slot, vt, ncols=0):
            key = (id(wb), slot) if ncols <= WSLOT else (id(wb), "all")
            slot_use[key] = max(slot_use.get(key, 0), vt)

        vop("memset", CEPS[:, :], EPS)
        vop("memset", CPI2[:, :], math.pi / 2.0)
        v_xc = [dma("sync", "x%d" % c, X[:, c * T:(c + 1) * T],
                    dx[c * 128:(c + 1) * 128, :]) for c in range(NC)]
        v_consts = 0
        for (name, tile) in (("c_ones", CONES), ("c_eye_f32", CEYEF),
                             ("c_eye_bf", CEYEB),
                             ("c_ones_row_f32", CONER),
                             ("c_qsel", CSEL128),
                             ("c_tri", CTRI), ("c_iotak", CIOTAK),
                             ):
            v_consts = dma("sync", "consts", tile[0:tile.shape[0], :], dram[name])
        v_wgb = dma("gp", "wgb", WGB[:, :].rearrange("p (c n) -> p c n", c=NC), dram["wg"].rearrange("(c p) n -> p c n", p=128))
        v_wsgb = dma("gp", "wsgb", WSGB[:, :].rearrange("p (c n) -> p c n", c=NC),
                     dram["wsg"].rearrange("(c p) n -> p c n", p=128))

        with contextlib.ExitStack() as pre:
            PSCR = sb("PSCR", [128, 448], F32, pre)
            CSEL2 = sb("CSEL2", [NH, NC * 128], BF16, pre)
            v_consts2 = dma("sync", "cols", CSEL2[0:NH, :], dram["c_sel2"])

            def load_cols(name, n, dst):
                vl = dma("sync", "cols", SMALLF[0:n, 0:128], dram[name],
                         waits=((sV, sV.n), (sT, sT.n), (sA, sA.n)))
                vt = tr(PS[0][:, 0:n], SMALLF[0:n, 0:128], CEYEF[0:n, 0:n],
                        waits=bw(PS[0]) + (vl, v_consts))
                vv = vop("tensor_copy", dst, PS[0][:, 0:n], waits=((sT, vt),))
                bs(PS[0], sV, vv)
                return vv

            load_cols("ln1_g", NC, COLS[:, 0:8])
            load_cols("ln1_b", NC, COLS[:, 8:16])
            load_cols("ln2_g", NC, COLS[:, 16:24])
            load_cols("ln2_b", NC, COLS[:, 24:32])
            load_cols("ff_b2", NC, COLS[:, 64:72])
            load_cols("b_p1", NC, COLS[:, 136:144])
            load_cols("b_qkv1", 24, QCOL[:, 0:24])
            v_th = load_cols("ff_th", 32, PSCR[:, 416:448])
            v_wr = load_cols("ff_wr", 32, PSCR[:, 0:32])
            v_wi = load_cols("ff_wi", 32, PSCR[:, 32:64])
            v_b1 = load_cols("ff_b1", 32, COLS[:, 104:136])
            TH = PSCR[:, 416:448]

            def sin32(dst, src, bias, base, extra_waits=()):
                u = PSCR[:, base:base + 32]
                kf = PSCR[:, base + 32:base + 64]
                r1 = PSCR[:, base + 64:base + 96]
                c1 = PSCR[:, base + 96:base + 128]
                vop("tensor_scalar", u, src, 1.0 / TWO_PI, bias / TWO_PI,
                    ALU.mult, ALU.add, waits=extra_waits)
                vop("tensor_copy", RI32A[:, 0:32], u)
                vop("tensor_copy", kf, RI32A[:, 0:32])
                vop("tensor_tensor", r1, u, kf, ALU.subtract)
                vop("tensor_scalar", c1, r1, 0.5, None, ALU.is_ge)
                vop("scalar_tensor_tensor", r1, c1, -1.0, r1, ALU.mult, ALU.add)
                vop("tensor_scalar", c1, r1, -0.5, None, ALU.is_lt)
                vv = vop("scalar_tensor_tensor", r1, c1, 1.0, r1, ALU.mult, ALU.add)
                return aop("activation", dst, r1, AF.Sin, scale=SIN_SCALE,
                           waits=((sV, vv),))

            va1 = sin32(PSCR[:, 320:352], TH, 0.0, 64, extra_waits=((sV, v_th),))
            va2 = sin32(PSCR[:, 352:384], TH, math.pi / 2.0, 192,
                        extra_waits=((sA, va1),))
            vop("tensor_scalar", PSCR[:, 128:160], TH, 1.0 / TWO_PI, None, ALU.mult,
                waits=((sA, va2),))
            vop("tensor_copy", RI32A[:, 32:64], PSCR[:, 128:160])
            vop("tensor_copy", PSCR[:, 160:192], RI32A[:, 32:64])
            vop("tensor_tensor", PSCR[:, 128:160], PSCR[:, 128:160],
                PSCR[:, 160:192], ALU.subtract)
            vop("tensor_scalar", PSCR[:, 224:256], PSCR[:, 128:160], 0.5, None,
                ALU.is_ge)
            vop("scalar_tensor_tensor", PSCR[:, 160:192], PSCR[:, 224:256], 1.0,
                PSCR[:, 160:192], ALU.mult, ALU.add)
            vop("tensor_scalar", PSCR[:, 224:256], PSCR[:, 128:160], -0.5, None,
                ALU.is_lt)
            vop("scalar_tensor_tensor", PSCR[:, 160:192], PSCR[:, 224:256], -1.0,
                PSCR[:, 160:192], ALU.mult, ALU.add)
            v_thst = vop("scalar_tensor_tensor", COLS[:, 32:64], PSCR[:, 160:192],
                         -TWO_PI, TH, ALU.mult, ALU.add)
            vop("tensor_tensor", PSCR[:, 384:416], PSCR[:, 352:384], PSCR[:, 0:32],
                ALU.mult, waits=((sA, va2), (sV, v_wr)))
            vop("tensor_tensor", PSCR[:, 64:96], PSCR[:, 320:352], PSCR[:, 32:64],
                ALU.mult, waits=((sV, v_wi),))
            vop("tensor_tensor", PSCR[:, 384:416], PSCR[:, 384:416], PSCR[:, 64:96],
                ALU.add)
            vop("tensor_tensor", PSCR[:, 384:416], PSCR[:, 384:416], PSCR[:, 32:64],
                ALU.mult)
            vop("tensor_scalar", COLS[:, 72:104], PSCR[:, 384:416], 0.1, None, ALU.mult)
            vop("tensor_copy", COLS[:, 160:192], PSCR[:, 320:352],
                waits=((sA, va2),))
            v_thb = vop("tensor_copy", COLS[:, 192:224], PSCR[:, 352:384])

            vl = dma("sync", "cols", SMALLF[0:1, 0:NH], dram["slopes"],
                     waits=((sV, sV.n), (sT, sT.n), (sA, sA.n)))
            vt = tr(PS[0][0:NH, 0:1], SMALLF[0:1, 0:NH], CEYEF[0:1, 0:1],
                    waits=bw(PS[0]) + (vl,))
            vv = vop("tensor_scalar", SLP8[0:NH, 0:1], PS[0][0:NH, 0:1], 8.0, None,
                     ALU.mult, waits=((sT, vt),))
            bs(PS[0], sV, vv)
            vt = mm(PS[1][:, 0:NH], CONER[:, :], SMALLF[0:1, 0:NH], True, True,
                    waits=bw(PS[1]) + (vl,))
            v_slb = vop("tensor_copy", SLPB[:, 0:16], PS[1][:, 0:NH], waits=((sT, vt),))
            bs(PS[1], sV, v_slb)
            vl = dma("sync", "cols", SMALLF[0:1, 16:16 + NH], dram["smear"],
                     waits=((sV, sV.n), (sT, sT.n), (sA, sA.n)))
            va = aop("activation", SMALLF[0:1, 32:32 + NH], SMALLF[0:1, 16:16 + NH],
                     AF.Sigmoid, waits=(vl,))
            vt = tr(PS[0][0:NH, 0:1], SMALLF[0:1, 32:32 + NH], CEYEF[0:1, 0:1],
                    waits=bw(PS[0]) + ((sA, va),))
            v_sg = vop("tensor_copy", SLP8B[0:NH, 0:1], PS[0][0:NH, 0:1],
                       waits=((sT, vt),))
            bs(PS[0], sV, v_sg)
            vt2 = 0
            for c in range(NC):
                vt2 = mm(PS[1][:, c:c + 1], CSEL2[:, c * 128:(c + 1) * 128],
                         SLP8B[0:NH, 0:1], c == 0, c == NC - 1,
                         waits=(bw(PS[1]) if c == 0 else ()) +
                         ((sV, v_sg), v_consts2))
            v_sc = vop("tensor_copy", SCOLS[:, 0:8], PS[1][:, 0:8], waits=((sT, vt2),))
            bs(PS[1], sV, v_sc)
            v_sc = vop("tensor_scalar", SCOLS[:, 8:16], SCOLS[:, 0:8], -1.0, 1.0,
                       ALU.mult, ALU.add, waits=((sV, v_sc),))
            vl = dma("sync", "cols", SMALLF[0:1, 48:49], dram["kv_s"],
                     waits=((sV, sV.n), (sT, sT.n), (sA, sA.n)))
            vl = dma("sync", "cols", SMALLF[0:1, 49:50], dram["kv_z"],
                     waits=((sV, sV.n), (sT, sT.n), (sA, sA.n)))
            vt = mm(PS[1][:, 0:2], CONER[:, :], SMALLF[0:1, 48:50], True, True,
                    waits=bw(PS[1]) + (vl,))
            vk = vop("tensor_copy", KVC[:, 2:3], PS[1][:, 0:1], waits=((sT, vt),))
            vk = vop("tensor_copy", KVC[:, 1:2], PS[1][:, 1:2])
            bs(PS[1], sV, vk)
            vk = vop("reciprocal", KVC[:, 0:1], KVC[:, 2:3], waits=((sV, vk),))
            vk = vop("tensor_tensor", KVC[:, 3:4], KVC[:, 2:3], KVC[:, 1:2], ALU.mult)
            v_kv = vop("tensor_scalar", KVC[:, 3:4], KVC[:, 3:4], -1.0, None, ALU.mult,
                       waits=((sV, vk),))

        def ln_full(MUt, RSTDt, SCRt, xb_fn, wait_a, gcol=None, bcol=None,
                    write_f32=False, src_fn=None, add_to_x=False):
            src_fn = src_fn or XC
            v_stats = [0] * NT
            v_last, a_last = 0, 0
            a_buf = {0: 0, 1: 0}
            a_tc = [[0] * NC for _ in range(NT)]
            v_cc = [0] * NC

            def wac(c):
                return wait_a[c] if isinstance(wait_a, list) else wait_a

            def stats(tq):
                sl = slice(tq * TQ, (tq + 1) * TQ)
                pa, pb = PS[4], PS[5]
                vta = 0
                for c in range(NC):
                    vta = mm(pa[:, :], CONES[:, :], xb_fn(c)[:, sl], c == 0,
                             c == NC - 1,
                             waits=(bw(pa) if c == 0 else ()) + ((sA, wac(c)),))
                vtb = 0
                for c in range(NC):
                    ring = SQR[:, (c % 2) * TQ:(c % 2 + 1) * TQ]
                    va_sq = aop("activation", ring, xb_fn(c)[:, sl], AF.Square,
                                waits=((sA, wac(c)), (sT, vtb)))
                    vtb = mm(pb[:, :], CONES[:, :], ring, c == 0, c == NC - 1,
                             waits=(bw(pb) if c == 0 else ()) + ((sA, va_sq),),
                             inc=True)
                va_mu = aop("activation", MUt[:, sl], pa[:, :], AF.Copy,
                            scale=1.0 / D, waits=((sT, vta),))
                bs(pa, sA, va_mu)
                v1 = vop("scalar_tensor_tensor", SCRt[:, 2 * TQ:3 * TQ], MUt[:, sl],
                         -1.0, MUt[:, sl], ALU.mult, ALU.mult, waits=((sA, va_mu),))
                v2 = vop("scalar_tensor_tensor", SCRt[:, 2 * TQ:3 * TQ], pb[:, :],
                         1.0 / D, SCRt[:, 2 * TQ:3 * TQ], ALU.mult, ALU.add,
                         waits=((sV, v1), (sT, vtb)))
                bs(pb, sV, v2)
                va_sd = aop("activation", RSTDt[:, sl], SCRt[:, 2 * TQ:3 * TQ],
                            AF.Sqrt, bias=CEPS[:, 0:1], waits=((sV, v2),))
                vx = vop("reciprocal", RSTDt[:, sl], RSTDt[:, sl],
                         waits=((sA, va_sd),))
                vx = vop("scalar_tensor_tensor", MUt[:, sl], MUt[:, sl], -1.0,
                         RSTDt[:, sl], ALU.mult, ALU.mult, waits=((sV, vx),))
                v_stats[tq] = vx

            def apply(tq):
                nonlocal v_last, a_last
                sl = slice(tq * TQ, (tq + 1) * TQ)
                for c in range(NC):
                    par = c % 2
                    buf = SCRt[:, 0:TQ] if par == 0 else SCRt[:, TQ:2 * TQ]
                    vw = vop("tensor_tensor", buf, src_fn(c)[:, sl], RSTDt[:, sl],
                             ALU.mult, waits=((sV, v_stats[tq]), (sA, a_buf[par])))
                    vw = vop("tensor_tensor", buf, buf, MUt[:, sl], ALU.add,
                             waits=((sV, vw),))
                    if gcol is not None:
                        vw = vop("tensor_scalar", buf, buf, gcol[:, c:c + 1],
                                 bcol[:, c:c + 1], ALU.mult, ALU.add,
                                 waits=((sV, vw),))
                    if add_to_x:
                        v_last = vop("tensor_tensor", XC(c)[:, sl], XC(c)[:, sl],
                                     buf, ALU.add, waits=((sV, vw),))
                        v_cc[c] = v_last
                        continue
                    if write_f32:
                        vw = vop("tensor_copy", XC(c)[:, sl], buf, waits=((sV, vw),))
                        v_cc[c] = vw
                    a_last = aop("activation", XBC(c)[:, sl], buf, AF.Copy,
                                 waits=((sV, vw),))
                    a_tc[tq][c] = a_last
                    a_buf[par] = a_last
                    v_last = vw

            for tq in range(NT):
                stats(tq)
                if tq > 0:
                    apply(tq - 1)
            apply(NT - 1)
            return v_last, a_last, a_tc, v_cc

        def post_ln_scoped(wait_vc, tag, t_xbr=0):
            with contextlib.ExitStack() as lnst:
                MUt = sb("MU" + tag, [128, T], F32, lnst)
                RSTDt = sb("RS" + tag, [128, T], F32, lnst)
                SCRt = sb("SC" + tag, [128, 3 * TQ], F32, lnst)
                a_cast_c = [0] * NC
                for c in range(NC):
                    wv = wait_vc[c] if isinstance(wait_vc, list) else wait_vc
                    a_cast_c[c] = aop("activation", XBC(c), XC(c), AF.Copy,
                                      waits=((sV, wv), (sT, t_xbr)))
                return ln_full(MUt, RSTDt, SCRt, lambda c: XBC(c), a_cast_c,
                               write_f32=True)

        def attention(layer, v_in_v, a_in_a, atc_in=None):
            is0 = (layer == 0)
            tag = str(layer)
            use = {}
            wqname = "w_qkv0" if is0 else "w_qkv1"
            stA = contextlib.ExitStack()
            with stA:
                ARK = sb("ARK" + tag, [128, NC * T], BF16, stA)
                ARV = sb("ARV" + tag, [128, 16 * D], BF16, stA)
                HROW = sb("HROW" + tag, [16, TQ], BF16, stA)
                AUXR = sb("AUXR" + tag, [128, TQ], BF16, stA) if is0 else None
                T1L = sb("T1L" + tag, [16, T], BF16, stA) if not is0 else None
                WS1L = sb("WS1L" + tag, [16, 3 * D], BF16, stA) if not is0 else None
                BIASC = sb("BIASC" + tag, [128, NH * 16], F32, stA) if is0 else None
                CAR = sb("CAR", [1, TQ], BF16, stA) if is0 else None

                def AK(c):
                    return ARK[:, c * T:(c + 1) * T]

                def AVS(kt):
                    return ARV[:, kt * D:(kt + 1) * D]

                if is0:
                    with contextlib.ExitStack() as lnst:
                        MUt = sb("MUa", [128, T], F32, lnst)
                        RSTDt = sb("RSa", [128, T], F32, lnst)
                        SCRt = sb("SCa", [128, 3 * TQ], F32, lnst)
                        a_cast = 0
                        for c in range(NC):
                            a_cast = aop("activation", XBC(c), XC(c), AF.Copy,
                                         waits=(v_xc[c], (sV, v_in_v)))
                        v_hin, a_hin, atc_h, _ = ln_full(
                            MUt, RSTDt, SCRt, lambda c: XBC(c), a_cast,
                            gcol=COLS[:, 0:8], bcol=COLS[:, 8:16])
                    v_biasc = 0
                    for h in range(NH):
                        v_biasc = vop("tensor_scalar", BIASC[:, h * 16:(h + 1) * 16],
                                      CIOTAK[:, :], SLPB[:, h:h + 1], None, ALU.mult,
                                      waits=((sV, v_slb), v_consts))
                else:
                    v_hin, a_hin = v_in_v, a_in_a
                    atc_h = atc_in
                    v_biasc = 0

                def wh(tq, c):
                    if atc_h is not None:
                        return ((sA, atc_h[tq][c]),)
                    return ((sA, a_hin), (sV, v_hin))

                if ATTN_UPTO <= 1:
                    return post_ln_scoped(sV.n, "pa" + tag)
                a_t1 = 0
                if not is0:
                    use["ws1"] = dma("gp", "ws1l", WS1L[:, :], dram["lb_a"],
                                     waits=((sT, sT.n),))
                    vla = dma("gp", "gscrb", GSCRB[:, 0:NC * 16].rearrange("p (c r) -> p c r", c=NC),
                              dram["la_a"].rearrange("(c p) r -> p c r", p=128),
                              waits=((sT, sT.n),))
                    for tq in range(NT):
                        pa = PS[4]
                        vt = 0
                        for c in range(NC):
                            vt = mm(pa[0:16, :], GSCRB[:, c * 16:(c + 1) * 16],
                                    XBC(c)[:, tq * TQ:(tq + 1) * TQ], c == 0,
                                    c == NC - 1,
                                    waits=(bw(pa) if c == 0 else ()) +
                                    (vla,) + wh(tq, c))
                        a_t1 = aop("activation", T1L[:, tq * TQ:(tq + 1) * TQ],
                                   pa[0:16, :], AF.Copy, scale=2.0, waits=((sT, vt),))
                        bs(pa, sA, a_t1)

                with contextlib.ExitStack() as kvst:
                    PSB0 = kvst.enter_context(
                        nc.psum_tensor("psb0" + tag, [128, 1024], BF16))
                    PSB1 = kvst.enter_context(
                        nc.psum_tensor("psb1" + tag, [128, 1024], BF16))
                    WBkv = sb("WBkv" + tag, [128, 2 * WSLOT], BF16, kvst)
                    VSTG = sb("VSTG" + tag, [128, 2 * TQ], BF16, kvst)
                    if is0:
                        KRAW = sb("KRAW", [128, 2 * (TQ + 1)], BF16, kvst)
                        KTMP = sb("KTMP", [128, TQ], BF16, kvst)
                    else:
                        FQY = sb("FQY", [128, TQ], F32, kvst)
                        FQC = SQR[:, 0:TQ]
                        FQR = sb("FQR", [128, TQ], F32, kvst)
                        FQI = RI32A

                    MAGIC = 12582912.0

                    def fakequant(psrc, dst_ap, wt):
                        a1 = aop("activation", FQY[:, :], psrc, AF.Identity,
                                 scale=KVC[:, 0:1], bias=KVC[:, 1:2],
                                 waits=wt + ((sV, v_kv), (sV, use.get("fq", 0))))
                        vop("tensor_scalar", FQR[:, :], FQY[:, :], MAGIC, None,
                            ALU.add, waits=((sA, a1),))
                        vop("tensor_scalar", FQR[:, :], FQR[:, :], -MAGIC, None,
                            ALU.add)
                        vq = vop("tensor_scalar", FQR[:, :], FQR[:, :], 0.0, 255.0,
                                 ALU.max, ALU.min)
                        use["fq"] = vq
                        return aop("activation", dst_ap, FQR[:, :], AF.Identity,
                                   scale=KVC[:, 2:3], bias=KVC[:, 3:4],
                                   waits=((sV, vq),))

                    wq = dram[wqname].rearrange("(c p) n -> p c n", p=128)
                    pb4 = [PS[0], PS[1], PS[2], PS[3]]
                    pidx = 0
                    pend_v = [None]

                    def emit_vtr(a_ev, tqv, c2v, vparv):
                        for j in range(4):
                            kt = tqv * 4 + j
                            pbx = PSB0 if j % 2 == 0 else PSB1
                            vt2 = tr(pbx[:, 0:128],
                                     VSTG[:, vparv * TQ + j * 128:
                                          vparv * TQ + (j + 1) * 128],
                                     CEYEB[:, :],
                                     waits=bw(pbx) + ((sA, a_ev),))
                            vv2 = vop("tensor_copy",
                                      AVS(kt)[:, c2v * 128:(c2v + 1) * 128],
                                      pbx[:, 0:128], waits=((sT, vt2),))
                            bs(pbx, sV, vv2)
                            use[("vstg", vparv)] = vt2

                    for ns in range(2, 6):
                        slot = ns % 2
                        v_wl = wload(WBkv, slot, wq[:, :, ns * TQ:(ns + 1) * TQ],
                                     WSLOT)
                        for nsub in range(4):
                            n128 = ns * 4 + nsub
                            for tq in range(NT):
                                pa = pb4[pidx]
                                pidx = (pidx + 1) % 4
                                vt = 0
                                for c in range(NC):
                                    w0 = bw(pa) if c == 0 else ()
                                    vt = mm(pa[:, :],
                                            WBkv[:, slot * WSLOT + c * TQ +
                                                 nsub * 128:slot * WSLOT +
                                                 c * TQ + nsub * 128 + 128],
                                            XBC(c)[:, tq * TQ:(tq + 1) * TQ],
                                            c == 0, (c == NC - 1) and is0,
                                            waits=w0 + (v_wl,) + wh(tq, c))
                                if not is0:
                                    vt = mm(pa[:, :],
                                            WS1L[:, n128 * 128:(n128 + 1) * 128],
                                            T1L[:, tq * TQ:(tq + 1) * TQ], False,
                                            True, waits=((sA, a_t1), use["ws1"]))
                                wuse(WBkv, slot, vt)
                                sl = slice(tq * TQ, (tq + 1) * TQ)
                                if n128 < 16:
                                    c2 = n128 - 8
                                    if is0:
                                        ksl = tq % 2
                                        base = ksl * (TQ + 1)
                                        a_ev = aop("activation",
                                                   KRAW[:, base + 1:base + 1 + TQ],
                                                   pa[:, :], AF.Copy,
                                                   waits=((sT, vt),
                                                          (sV, use.get(("kr", ksl), 0))))
                                        bs(pa, sA, a_ev)
                                        if tq == 0:
                                            vb = vop("memset",
                                                     KRAW[:, base:base + 1], 0.0,
                                                     waits=((sA, a_ev),))
                                        else:
                                            ob = (1 - ksl) * (TQ + 1)
                                            vb = vop("tensor_copy",
                                                     KRAW[:, base:base + 1],
                                                     KRAW[:, ob + TQ:ob + TQ + 1],
                                                     waits=((sA, a_ev),))
                                            use[("kr", 1 - ksl)] = max(
                                                use.get(("kr", 1 - ksl), 0), vb)
                                        vb = vop("tensor_scalar", KTMP[:, :],
                                                 KRAW[:, base + 1:base + 1 + TQ],
                                                 SCOLS[:, 8 + c2:9 + c2], None,
                                                 ALU.mult,
                                                 waits=((sV, vb), (sV, v_sc)))
                                        use[("kr", ksl)] = vop(
                                            "scalar_tensor_tensor", AK(c2)[:, sl],
                                            KRAW[:, base:base + TQ],
                                            SCOLS[:, c2:c2 + 1], KTMP[:, :],
                                            ALU.mult, ALU.add, waits=((sV, vb),))
                                    else:
                                        a_ev = fakequant(pa[:, :], AK(c2)[:, sl],
                                                         ((sT, vt),))
                                        bs(pa, sA, a_ev)
                                else:
                                    c2 = n128 - 16
                                    vpar = (n128 * NT + tq) % 2
                                    vdst = VSTG[:, vpar * TQ:(vpar + 1) * TQ]
                                    wv = ((sT, vt),
                                          (sT, use.get(("vstg", vpar), 0)))
                                    if is0:
                                        a_ev = aop("activation", vdst,
                                                   pa[:, :], AF.Copy, waits=wv)
                                    else:
                                        a_ev = fakequant(pa[:, :], vdst, wv)
                                    bs(pa, sA, a_ev)
                                    if pend_v[0] is not None:
                                        emit_vtr(*pend_v[0])
                                    pend_v[0] = (a_ev, tq, c2, vpar)
                    if pend_v[0] is not None:
                        emit_vtr(*pend_v[0])
                    v_kv_v = sV.n
                    a_kv_a = sA.n

                    vk2 = 0
                    for tq in range(NT):
                        sl = slice(tq * TQ, (tq + 1) * TQ)
                        pbk = PS[5]
                        vtb = 0
                        for cp in range(NC):
                            ring = SQR[:, (cp % 2) * TQ:(cp % 2 + 1) * TQ]
                            ak = aop("activation", ring[:, :], AK(cp)[:, sl],
                                     AF.Square,
                                     waits=((sV, v_kv_v), (sA, a_kv_a), (sT, vtb)))
                            vtb = mm(pbk[0:16, :],
                                     CSEL128[:, cp * 16:(cp + 1) * 16],
                                     ring[:, :], cp == 0, cp == NC - 1,
                                     waits=(bw(pbk) if cp == 0 else ()) + ((sA, ak),),
                                     inc=True)
                        vk2 = vop("reduce_max", MAXK[0:16, tq:tq + 1], pbk[0:16, :],
                                  AX, waits=((sT, vtb),))
                        bs(pbk, sV, vk2)
                    vk2 = vop("reduce_max", MAXK[0:16, 4:5], MAXK[0:16, 0:4], AX,
                              waits=((sV, vk2),))
                    a_mk = aop("activation", MAXK[0:16, 5:6], MAXK[0:16, 4:5],
                               AF.Sqrt, waits=((sV, vk2),))

                if ATTN_UPTO <= 2:
                    return post_ln_scoped(sV.n, "pa" + tag)
                PS6 = stA.enter_context(
                    nc.psum_tensor("ps6" + tag, [128, 512], F32))
                PS7 = stA.enter_context(
                    nc.psum_tensor("ps7" + tag, [128, 512], F32))
                ARQ = sb("ARQ" + tag, [128, 2 * NC * TQ], BF16, stA)
                PT = sb("PT" + tag, [128, 4 * TQ], BF16, stA)
                ONRM = SMALLF
                WBq = sb("WBq" + tag, [128, 2048], BF16, stA)
                BIA1 = sb("BIA1", [128, 16], F32, stA) if not is0 else None
                if is0:
                    NEGE16 = sb("NEGE16", [128, NH * 128], BF16, stA)
                    v_ng = dma("sync", "nege", NEGE16[:, :], dram["c_nege16"])
                else:
                    NEGE16, v_ng = None, 0

                def AQZ(c, hi):
                    return ARQ[:, (2 * c + hi) * TQ:(2 * c + hi + 1) * TQ]

                v_zq = 0
                for b16 in range(2 * NC):
                    v_zq = vop("memset", ARQ[:, b16 * TQ:(b16 + 1) * TQ], 0.0)
                if is0:
                    v_zq = vop("memset", AUXR[0:128, 0:TQ], 0.0)

                wq = dram[wqname].rearrange("(c p) n -> p c n", p=128)
                for qw in range(NT):
                    hqs = (qw,)
                    pb4 = [PS[0], PS[1], PS[2], PS[3]]
                    pidx = 0
                    for n128 in range(8):
                        v_wl = wload(WBq, n128 % 2,
                                     wq[:, :, n128 * 128:(n128 + 1) * 128], 1024,
                                     swidth=1024)
                        if True:
                            for ti, tq in enumerate(hqs):
                                pa = pb4[pidx]
                                pidx = (pidx + 1) % 4
                                vt = 0
                                for c in range(NC):
                                    w0 = bw(pa) if c == 0 else ()
                                    vt = mm(pa[:, :],
                                            WBq[:, (n128 % 2) * 1024 + c * 128:
                                                (n128 % 2) * 1024 + c * 128 + 128],
                                            XBC(c)[:, tq * TQ:(tq + 1) * TQ],
                                            c == 0, (c == NC - 1) and is0,
                                            waits=w0 + (v_wl,) + wh(tq, c))
                                if not is0:
                                    vt = mm(pa[:, :],
                                            WS1L[:, n128 * 128:(n128 + 1) * 128],
                                            T1L[:, tq * TQ:(tq + 1) * TQ], False,
                                            True, waits=((sA, a_t1),))
                                wuse(WBq, n128 % 2, vt)
                                dsl = slice(ti * TQ, (ti + 1) * TQ)
                                for hi in range(2):
                                    r0 = hi * 64
                                    dst = AQZ(n128, hi)[r0:r0 + 64, dsl]
                                    if is0:
                                        a_ev = aop("activation", dst,
                                                   pa[r0:r0 + 64, :], AF.Copy,
                                                   waits=((sT, vt), (sV, v_zq)))
                                    else:
                                        a_ev = aop("activation", dst,
                                                   pa[r0:r0 + 64, :], AF.Identity,
                                                   bias=QCOL[r0:r0 + 64,
                                                             n128:n128 + 1],
                                                   waits=((sT, vt), (sV, v_zq)))
                                bs(pa, sA, a_ev)
                    a_q = sA.n

                    v_aux = 0
                    for ti, tq in enumerate(hqs):
                        pbk = PS[5]
                        vtb = 0
                        for cp in range(NC):
                            ring = SQR[:, (cp % 2) * TQ:(cp % 2 + 1) * TQ]
                            ak = 0
                            for hi in range(2):
                                r0 = hi * 64
                                ak = aop("activation", ring[r0:r0 + 64, :],
                                         AQZ(cp, hi)[r0:r0 + 64,
                                                     ti * TQ:(ti + 1) * TQ],
                                         AF.Square,
                                         waits=((sA, a_q), (sT, vtb)))
                            vtb = mm(pbk[0:16, :],
                                     CSEL128[:, cp * 16:(cp + 1) * 16],
                                     ring[:, :], cp == 0, cp == NC - 1,
                                     waits=(bw(pbk) if cp == 0 else ()) + ((sA, ak),),
                                     inc=True)
                        a_qn = aop("activation", HROW[0:16, 0:TQ],
                                   pbk[0:16, :], AF.Sqrt,
                                   waits=((sT, vtb), (sV, use.get("aux", 0))))
                        bs(pbk, sA, a_qn)
                        if is0:
                            v_car = dma("sync", "car", CAR[0:1, :],
                                        dram["c_arange"][0:1, tq * TQ:(tq + 1) * TQ],
                                        waits=((sT, use.get("slq", 0)),))
                            pq = PS[4]
                            vt = mm(pq[0:16, :], CONES[0:1, 0:16],
                                    CAR[0:1, 0:TQ], True, True,
                                    waits=bw(pq) + (v_car,))
                            use["slq"] = vt
                            vsl = vop("tensor_scalar", SMALLF[0:16, 0:TQ],
                                      pq[0:16, :], SLP8[0:NH, 0:1], None, ALU.mult,
                                      waits=((sT, vt), (sT, use.get("smf", 0))))
                            bs(pq, sV, vsl)
                            v_aux = vop("scalar_tensor_tensor",
                                        AUXR[0:16, 0:TQ],
                                        HROW[0:16, 0:TQ],
                                        MAXK[0:16, 5:6], SMALLF[0:16, 0:TQ],
                                        ALU.mult, ALU.add,
                                        waits=((sA, a_mk), (sA, a_qn),
                                               (sV, vsl), (sT, use.get("aux_mm", 0))))
                        else:
                            vmq = vop("reduce_max", MAXK[0:16, 6:7],
                                      HROW[0:16, 0:TQ], AX, waits=((sA, a_qn),))
                            use["aux"] = vmq
                            vmq = vop("tensor_tensor", MAXK[0:16, 7:8],
                                      MAXK[0:16, 6:7], MAXK[0:16, 5:6], ALU.mult,
                                      waits=((sV, vmq), (sA, a_mk)))
                            vdg = vop("tensor_scalar", GSCRB[0:16, 0:16],
                                      CEYEF[0:16, 0:16], MAXK[0:16, 7:8], None,
                                      ALU.mult,
                                      waits=((sV, vmq), (sT, use.get("biamm", 0))))
                            pq = PS[4]
                            vtb2 = mm(pq[:, 0:16], CONES[0:16, :],
                                      GSCRB[0:16, 0:16], True, True,
                                      waits=bw(pq) + ((sV, vdg),))
                            use["biamm"] = vtb2
                            a_bia = aop("activation", BIA1[:, 0:16], pq[:, 0:16],
                                        AF.Copy, scale=-0.125,
                                        waits=((sT, vtb2),))
                            bs(pq, sA, a_bia)
                            use["bia_a"] = a_bia

                    SW[0] = bool(SW_P2[0])
                    psS2 = [[PS[0], PS[3]], [PS[1], PS[4]]]
                    psOL = [PS[2], PS[5]]
                    psL2 = [PS6, PS7]
                    for pr in range(NH // 2):
                        c = pr
                        for ti, tq in enumerate(hqs):
                            qc = tq
                            qsl_q = slice(ti * TQ, (ti + 1) * TQ)
                            ktmax = qc * 4 + 4
                            vt_s = {}
                            ostate = {"vt_o": 0, "vt_l": 0, "vt_oA": 0, "vt_lA": 0}

                            def emit_S(kt, pr=pr, qsl_q=qsl_q):
                                for hi in range(2):
                                    h = 2 * pr + hi
                                    pS = psS2[hi][kt % 2]
                                    if is0:
                                        mm(pS[:, 0:TQ],
                                           AK(c)[:, kt * 128:(kt + 1) * 128],
                                           AQZ(c, hi)[:, qsl_q], True, False,
                                           waits=bw(pS) + ((sA, a_q), (sV, v_zq)),
                                           swkind="s")
                                        vt = mm(pS[:, 0:TQ],
                                                NEGE16[:, h * 128:(h + 1) * 128],
                                                AUXR[0:128, 0:TQ], False, True,
                                                waits=((sV, v_aux), v_ng),
                                                swkind="s")
                                        use["aux_mm"] = vt
                                    else:
                                        vt = mm(pS[:, 0:TQ],
                                                AK(c)[:, kt * 128:(kt + 1) * 128],
                                                AQZ(c, hi)[:, qsl_q], True,
                                                True,
                                                waits=bw(pS) + ((sA, a_q),
                                                                (sV, v_zq)),
                                                swkind="s")
                                    vt_s[(hi, kt)] = vt

                            def emit_C(kt, pr=pr, qc=qc, ktmax=ktmax):
                                voff = max(0, (kt - qc * 4) * 128)
                                first = (kt == 0)
                                last = (kt == ktmax - 1)
                                for hi in range(2):
                                    h = 2 * pr + hi
                                    pS = psS2[hi][kt % 2]
                                    vt = vt_s[(hi, kt)]
                                    if kt >= qc * 4:
                                        vtri = vop("tensor_tensor",
                                                   pS[:, voff:voff + 128],
                                                   pS[:, voff:voff + 128],
                                                   CTRI[:, :], ALU.add,
                                                   waits=((sT, vt), v_consts))
                                        wa = ((sV, vtri),)
                                    else:
                                        wa = ((sT, vt),)
                                    slab_i = 2 * hi + kt % 2
                                    slab = PT[:, slab_i * TQ + voff:
                                              (slab_i + 1) * TQ]
                                    wa = wa + ((sT, use.get(("pt", slab_i), 0)),)
                                    if is0:
                                        a_exp = aop("activation", slab,
                                                    pS[:, voff:TQ], AF.Exp,
                                                    scale=0.125,
                                                    bias=BIASC[:, h * 16 + kt:
                                                               h * 16 + kt + 1],
                                                    waits=wa + ((sV, v_biasc),))
                                    else:
                                        a_exp = aop("activation", slab,
                                                    pS[:, voff:TQ], AF.Exp,
                                                    scale=0.125,
                                                    bias=BIA1[:, h:h + 1],
                                                    waits=wa +
                                                    ((sA, use.get("bia_a", 0)),))
                                    bs(pS, sA, a_exp)
                                    wo = bw(psOL[hi]) if first else ()
                                    wl = bw(psL2[hi]) if first else ()
                                    if hi == 0:
                                        vt_o = mm(psOL[0][0:64, voff:TQ],
                                                  AVS(kt)[:, h * DH:(h + 1) * DH],
                                                  slab, first, last,
                                                  waits=wo + ((sA, a_exp),),
                                                  swkind="av")
                                        vt_l = mm(psL2[0][:, voff:TQ],
                                                  CONES[:, :], slab, first, last,
                                                  waits=wl + ((sA, a_exp),),
                                                  inc=True, swkind="l")
                                        ostate["vt_oA"], ostate["vt_lA"] = vt_o, vt_l
                                    else:
                                        vt_o = mm(psOL[1][64:128, voff:TQ],
                                                  AVS(kt)[:, h * DH:(h + 1) * DH],
                                                  slab, first, last,
                                                  waits=wo + ((sA, a_exp),),
                                                  tp=(0, 64), swkind="av")
                                        vt_l = mm(psL2[1][:, voff:TQ],
                                                  CONES[:, :], slab, first, last,
                                                  waits=wl + ((sA, a_exp),),
                                                  inc=True, swkind="l")
                                        ostate["vt_o"], ostate["vt_l"] = vt_o, vt_l
                                    use[("pt", slab_i)] = vt_l

                            for kt in range(ktmax):
                                emit_S(kt)
                                if kt > 0:
                                    emit_C(kt - 1)
                            emit_C(ktmax - 1)

                            vr = vop("reciprocal", ONRM[0:64, 0:TQ],
                                     psL2[0][0:64, 0:TQ],
                                     waits=((sT, ostate["vt_lA"]),))
                            bs(psL2[0], sV, vr)
                            vr = vop("reciprocal", ONRM[64:128, 0:TQ],
                                     psL2[1][64:128, 0:TQ],
                                     waits=((sV, vr), (sT, ostate["vt_l"])))
                            bs(psL2[1], sV, vr)
                            vo = vop("tensor_tensor",
                                     XBC(c)[0:64, tq * TQ:(tq + 1) * TQ],
                                     psOL[0][0:64, 0:TQ], ONRM[0:64, 0:TQ],
                                     ALU.mult,
                                     waits=((sV, vr), (sT, ostate["vt_oA"])))
                            bs(psOL[0], sV, vo)
                            vo = vop("tensor_tensor",
                                     XBC(c)[64:128, tq * TQ:(tq + 1) * TQ],
                                     psOL[1][64:128, 0:TQ], ONRM[64:128, 0:TQ],
                                     ALU.mult,
                                     waits=((sV, vo), (sT, ostate["vt_o"])))
                            bs(psOL[1], sV, vo)
                            v_out_v = vo
                            use["aux"] = v_aux
                    SW[0] = False
            v_attn_v = v_out_v
            if ATTN_UPTO <= 3:
                return post_ln_scoped(sV.n, "pa" + tag)

            with contextlib.ExitStack() as est:
                WBp = sb("WBp" + tag, [128, WSLOT], BF16, est)
                MUt = sb("MUe" + tag, [128, T], F32, est)
                RSTDt = sb("RSe" + tag, [128, T], F32, est)
                SCRt = sb("SCe" + tag, [128, (3 if is0 else 1) * TQ], F32, est)
                if is0:
                    APF = sb("APF", [128, NC * T], F32, est)
                else:
                    T1P = sb("T1P", [16, T], BF16, est)
                    WS2L = sb("WS2L", [16, D], BF16, est)

                a_t1p = 0
                if not is0:
                    v_ws2 = dma("gp", "ws2l", WS2L[:, :], dram["lb_p"], waits=((sT, sT.n),))
                    vla = dma("gp", "gscrb", GSCRB[:, 0:NC * 16].rearrange("p (c r) -> p c r", c=NC),
                              dram["la_p"].rearrange("(c p) r -> p c r", p=128),
                              waits=((sT, sT.n),))
                    for tq in range(NT):
                        pa = PS[4]
                        vt = 0
                        for c in range(NC):
                            vt = mm(pa[0:16, :], GSCRB[:, c * 16:(c + 1) * 16],
                                    XBC(c)[:, tq * TQ:(tq + 1) * TQ], c == 0,
                                    c == NC - 1,
                                    waits=(bw(pa) if c == 0 else ()) +
                                    (vla, (sV, v_attn_v)))
                        a_t1p = aop("activation", T1P[:, tq * TQ:(tq + 1) * TQ],
                                    pa[0:16, :], AF.Copy, scale=2.0,
                                    waits=((sT, vt),))
                        bs(pa, sA, a_t1p)

                wp = dram["w_o0" if is0 else "w_p1"].rearrange(
                    "(c p) n -> p c n", p=128)
                pb4 = [PS[0], PS[1], PS[2], PS[3]]
                pidx = 0
                v_add = 0
                v_xcc = [0] * NC
                a_apf = 0
                for ns in range(2):
                    slot = 0
                    v_wl = wload(WBp, slot, wp[:, :, ns * TQ:(ns + 1) * TQ], WSLOT)
                    for nsub in range(4):
                        n128 = ns * 4 + nsub
                        for tq in range(NT):
                            pa = pb4[pidx]
                            pidx = (pidx + 1) % 4
                            vt = 0
                            for cc in range(NC):
                                w0 = bw(pa) if cc == 0 else ()
                                vt = mm(pa[:, :],
                                        WBp[:, cc * TQ + nsub * 128:
                                            cc * TQ + nsub * 128 + 128],
                                        XBC(cc)[:, tq * TQ:(tq + 1) * TQ],
                                        cc == 0, (cc == NC - 1) and is0,
                                        waits=w0 + (v_wl, (sV, v_attn_v)))
                            if not is0:
                                vt = mm(pa[:, :], WS2L[:, n128 * 128:(n128 + 1) * 128],
                                        T1P[:, tq * TQ:(tq + 1) * TQ], False, True,
                                        waits=((sA, a_t1p), v_ws2))
                            wuse(WBp, slot, vt)
                            sl = slice(tq * TQ, (tq + 1) * TQ)
                            if is0:
                                a_apf = aop("activation",
                                            APF[:, n128 * T:(n128 + 1) * T][:, sl],
                                            pa[:, :], AF.Copy, waits=((sT, vt),))
                                bs(pa, sA, a_apf)
                            else:
                                vv = vop("tensor_scalar", SCRt[:, 0:TQ], pa[:, :],
                                         COLS[:, 136 + n128:137 + n128], None,
                                         ALU.add, waits=((sT, vt),))
                                v_add = vop("tensor_tensor", XC(n128)[:, sl],
                                            XC(n128)[:, sl], SCRt[:, 0:TQ], ALU.add,
                                            waits=((sV, vv),))
                                v_xcc[n128] = v_add
                                bs(pa, sV, v_add)
                t_xbr = sT.n

                if is0:
                    a_c2 = 0
                    for c in range(NC):
                        a_c2 = aop("activation", XBC(c), APF[:, c * T:(c + 1) * T],
                                   AF.Copy, waits=((sA, a_apf), (sV, v_attn_v)))
                    v_add, _, _, v_xcc = ln_full(
                        MUt, RSTDt, SCRt, lambda c: XBC(c), a_c2,
                        gcol=COLS[:, 16:24], bcol=COLS[:, 24:32],
                        src_fn=lambda c: APF[:, c * T:(c + 1) * T],
                        add_to_x=True)
                    t_xbr = 0
            return post_ln_scoped(v_xcc, "pa" + tag, t_xbr=t_xbr)

        def ffn0(v_in_v, a_in_a, atc_in=None):

            def wh(tq, c):
                if atc_in is not None:
                    return ((sA, atc_in[tq][c]),)
                return ((sA, a_in_a), (sV, v_in_v))
            st = contextlib.ExitStack()
            with st:
                HID = sb("HID", [128, 32 * 1024], BF16, st)
                WBf = sb("WBf", [128, 2 * WSLOT], BF16, st)
                B0 = sb("B0f", [128, 2 * TQ], F32, st)
                B1 = sb("B1f", [128, 2 * TQ], F32, st)
                B2 = sb("B2f", [128, 2 * TQ], F32, st)
                SINB = sb("SINB", [128, 2 * TQ], F32, st)
                w1 = dram["w_ff1"].rearrange("(c p) n -> p c n", p=128)
                w2 = dram["w_ff2"].rearrange("(f p) n -> p f n", p=128)
                pb4 = [PS[0], PS[1], PS[2], PS[3]]
                pidx = 0
                v_add = 0
                v_xcc = [0] * NC
                use_f = {}
                for half in range(2):
                    tqs = (2 * half, 2 * half + 1)
                    a_hid = 0
                    a_hid_t = {}
                    for fs in range(8):
                        slot = fs % 2
                        v_wl = wload(WBf, slot, w1[:, :, fs * TQ:(fs + 1) * TQ],
                                     WSLOT)
                        for fsub in range(4):
                            fc = fs * 4 + fsub
                            sthc = COLS[:, 160 + fc:161 + fc]
                            cthc = COLS[:, 192 + fc:193 + fc]
                            ecol = COLS[:, 72 + fc:73 + fc]
                            b1c = COLS[:, 104 + fc:105 + fc]
                            for ti, tq in enumerate(tqs):
                                pa = pb4[pidx]
                                pidx = (pidx + 1) % 4
                                vt = 0
                                for c in range(NC):
                                    w0 = bw(pa) if c == 0 else ()
                                    vt = mm(pa[:, :],
                                            WBf[:, slot * WSLOT + c * TQ +
                                                fsub * 128:slot * WSLOT + c * TQ +
                                                fsub * 128 + 128],
                                            XBC(c)[:, tq * TQ:(tq + 1) * TQ],
                                            c == 0, c == NC - 1,
                                            waits=w0 + (v_wl,) + wh(tq, c))
                                wuse(WBf, slot, vt)
                                par = (fc * 2 + ti) % 2
                                qsl = slice(par * TQ, (par + 1) * TQ)
                                a_cz = aop("activation", B1[:, qsl], pa[:, :],
                                           AF.Sin, scale=0.1, bias=CPI2[:, 0:1],
                                           waits=((sT, vt),
                                                  (sV, use_f.get(("b1", par), 0))))
                                aop("activation", SINB[:, qsl], pa[:, :],
                                    AF.Sin, scale=0.1, waits=())
                                a_h = aop("activation", B0[:, qsl], pa[:, :],
                                          AF.Copy, waits=())
                                bs(pa, sA, a_h)
                                vop("tensor_scalar", B1[:, qsl], B1[:, qsl],
                                    sthc, None, ALU.mult,
                                    waits=((sA, a_h), (sV, v_thb)))
                                vop("scalar_tensor_tensor", B1[:, qsl],
                                    SINB[:, qsl], cthc, B1[:, qsl], ALU.mult,
                                    ALU.add)
                                vv = vop("scalar_tensor_tensor", B2[:, qsl],
                                         B1[:, qsl], ecol, B0[:, qsl], ALU.mult,
                                         ALU.add,
                                         waits=((sA, use_f.get(("rl", par), 0)),))
                                use_f[("b1", par)] = vv
                                dst = HID[:, fc * 1024 + ti * TQ:
                                          fc * 1024 + (ti + 1) * TQ]
                                a_hid = aop("activation", dst, B2[:, qsl], AF.Relu,
                                            bias=b1c, waits=((sV, vv),))
                                use_f[("rl", par)] = a_hid
                                a_hid_t[(fc, ti)] = a_hid
                    for n128 in range(8):
                        slot2 = n128 % 2
                        v_wl = wload(WBf, slot2,
                                     w2[:, :, n128 * 128:(n128 + 1) * 128],
                                     32 * 128)
                        b2c = COLS[:, 64 + n128:65 + n128]
                        for ti, tq in enumerate(tqs):
                            pa = pb4[pidx]
                            pidx = (pidx + 1) % 4
                            vt = 0
                            for fc in range(32):
                                w0 = bw(pa) if fc == 0 else ()
                                vt = mm(pa[:, :],
                                        WBf[:, slot2 * WSLOT + fc * 128:
                                            slot2 * WSLOT + fc * 128 + 128],
                                        HID[:, fc * 1024 + ti * TQ:
                                            fc * 1024 + (ti + 1) * TQ],
                                        fc == 0, fc == 31,
                                        waits=w0 + (v_wl, (sA, a_hid_t[(fc, ti)])))
                            wuse(WBf, slot2, vt)
                            sl = slice(tq * TQ, (tq + 1) * TQ)
                            vv = vop("scalar_tensor_tensor", XC(n128)[:, sl],
                                     pa[:, :], 1.0, XC(n128)[:, sl], ALU.bypass,
                                     ALU.add, waits=((sT, vt),))
                            v_add = vop("tensor_scalar", XC(n128)[:, sl],
                                        XC(n128)[:, sl], b2c, None, ALU.add,
                                        waits=((sV, vv),))
                            v_xcc[n128] = v_add
                            bs(pa, sV, v_add)
            return post_ln_scoped(v_xcc, "pf")

        def moe(v_in_v, a_in_a, atc_in=None):

            def wh(tq, c):
                if atc_in is not None:
                    return ((sA, atc_in[tq][c]),)
                return ((sA, a_in_a), (sV, v_in_v))
            st = contextlib.ExitStack()
            with st:
                HPAIR = sb("HPAIR", [128, 8 * T], BF16, st)
                WBm = sb("WBm", [128, 4 * WSLOT], BF16, st)
                WTOK = sb("WTOK", [128, 256], F32, st)
                WT = sb("WT", [16, T], BF16, st)
                WBC = sb("WBC", [128, T], F32, st)
                SGBr = sb("SGBr", [128, T], F32, st)
                SGR = sb("SGR", [128, TQ], BF16, st)
                E16P = sb("E16P", [NH, NH * 128], BF16, st)
                v_e16p = dma("sync", "e16p", E16P[0:NH, :], dram["c_e16p"])
                USR = sb("USR", [128, TQ], F32, st)
                SCRm = sb("SCRm", [128, TQ], F32, st)
                pb4 = [PS[0], PS[1], PS[2], PS[3]]
                pidx = 0
                v_wt = 0
                pend_w = [None]

                def emit_wtr(vvw, ttw):
                    vtt = tr(PS[4][0:16, 0:128], WTOK[:, ttw * 16:(ttw + 1) * 16],
                             CEYEF[:, :], waits=bw(PS[4]) + ((sV, vvw),))
                    vwt = vop("tensor_copy", WT[0:16, ttw * 128:(ttw + 1) * 128],
                              PS[4][0:16, 0:128], waits=((sT, vtt),))
                    bs(PS[4], sV, vwt)
                    return vwt

                for tt in range(16):
                    pa = PS[5]
                    vt = 0
                    for c in range(NC):
                        vt = mm(pa[:, 0:16], XBC(c)[:, tt * 128:(tt + 1) * 128],
                                WGB[:, c * 16:(c + 1) * 16], c == 0, c == NC - 1,
                                waits=(bw(pa) if c == 0 else ()) +
                                (v_wgb,) + wh(tt // 4, c))
                    g0 = GSCR[:, 16:32]
                    vv = vop("reduce_max", GSCR[:, 32:33], pa[:, 0:16], AX,
                             negate=True, waits=((sT, vt),))
                    a1 = aop("activation", g0, pa[:, 0:16], AF.Exp,
                             bias=GSCR[:, 32:33], waits=((sV, vv),))
                    bs(pa, sA, a1)
                    vv = vop("reduce_max", GSCR[:, 33:34], g0, AX, waits=((sA, a1),))
                    vv = vop("tensor_scalar", GSCR[:, 48:64], g0, GSCR[:, 33:34],
                             None, ALU.is_ge, waits=((sV, vv),))
                    vv = vop("scalar_tensor_tensor", GSCR[:, 0:16], GSCR[:, 48:64],
                             -1e30, g0, ALU.mult, ALU.add, waits=((sV, vv),))
                    vv = vop("reduce_max", GSCR[:, 34:35], GSCR[:, 0:16], AX,
                             waits=((sV, vv),))
                    vv = vop("tensor_scalar", GSCR[:, 48:64], g0, GSCR[:, 34:35],
                             None, ALU.is_ge, waits=((sV, vv),))
                    vv = vop("tensor_tensor", GSCR[:, 35:36], GSCR[:, 33:34],
                             GSCR[:, 34:35], ALU.add, waits=((sV, vv),))
                    vv = vop("reciprocal", GSCR[:, 35:36], GSCR[:, 35:36],
                             waits=((sV, vv),))
                    vv = vop("scalar_tensor_tensor", WTOK[:, tt * 16:(tt + 1) * 16],
                             g0, GSCR[:, 35:36], GSCR[:, 48:64], ALU.mult, ALU.mult,
                             waits=((sV, vv),))
                    if pend_w[0] is not None:
                        v_wt = emit_wtr(*pend_w[0])
                    pend_w[0] = (vv, tt)
                if pend_w[0] is not None:
                    v_wt = emit_wtr(*pend_w[0])

                v_add = 0
                v_xcc = [0] * NC
                for pr in range(8):
                    v_h = 0
                    v_h_t = {}
                    sg0 = (pr % 2) * 2
                    sg1 = 2 - sg0
                    sdn = sg0
                    for ei, e in enumerate((2 * pr, 2 * pr + 1)):
                        sgu = sg0 if ei == 0 else sg1
                        vbc = 0
                        for tq in range(NT):
                            pa = PS[5]
                            vt = mm(pa[:, :], E16P[:, e * 128:(e + 1) * 128],
                                    WT[0:NH, tq * TQ:(tq + 1) * TQ], True, True,
                                    waits=bw(pa) + ((sV, v_wt), v_e16p))
                            vbc = vop("tensor_copy", WBC[:, tq * TQ:(tq + 1) * TQ],
                                      pa[:, :], waits=((sT, vt),))
                            bs(pa, sV, vbc)
                        ge = dram["we_g"][e].rearrange("(c p) n -> p c n", p=128)
                        ue = dram["we_u"][e].rearrange("(c p) n -> p c n", p=128)
                        v_wg = wload(WBm, sgu, ge, WSLOT)
                        v_wu = wload(WBm, sgu + 1, ue, WSLOT)
                        if ei == 1:
                            vd0 = wload(WBm, sdn, dram["we_d"][2 * pr].rearrange(
                                "(h p) n -> p h n", p=128), WSLOT)
                            vd1 = wload(WBm, sdn + 1,
                                        dram["we_d"][2 * pr + 1].rearrange(
                                            "(h p) n -> p h n", p=128), WSLOT)
                        for hc in range(4):
                            for tq in range(NT):
                                pg = pb4[pidx]
                                pidx = (pidx + 1) % 4
                                pu = pb4[pidx]
                                pidx = (pidx + 1) % 4
                                vtg = vtu = 0
                                for c in range(NC):
                                    w0 = bw(pg) if c == 0 else ()
                                    vtg = mm(pg[:, :],
                                             WBm[:, sgu * WSLOT + c * TQ +
                                                 hc * 128:sgu * WSLOT + c * TQ +
                                                 hc * 128 + 128],
                                             XBC(c)[:, tq * TQ:(tq + 1) * TQ],
                                             c == 0, c == NC - 1,
                                             waits=w0 + (v_wg,))
                                for c in range(NC):
                                    w0 = bw(pu) if c == 0 else ()
                                    vtu = mm(pu[:, :],
                                             WBm[:, (sgu + 1) * WSLOT + c * TQ +
                                                 hc * 128:(sgu + 1) * WSLOT +
                                                 c * TQ + hc * 128 + 128],
                                             XBC(c)[:, tq * TQ:(tq + 1) * TQ],
                                             c == 0, c == NC - 1,
                                             waits=w0 + (v_wu,))
                                wuse(WBm, sgu, vtg)
                                wuse(WBm, sgu + 1, vtu)
                                a_sg = aop("activation", SGR[:, :], pg[:, :],
                                           AF.Silu, waits=((sT, vtg), (sV, v_h)))
                                bs(pg, sA, a_sg)
                                vv = vop("scalar_tensor_tensor", USR[:, :],
                                         pu[:, :], 1.0,
                                         WBC[:, tq * TQ:(tq + 1) * TQ], ALU.bypass,
                                         ALU.mult, waits=((sT, vtu), (sV, vbc)))
                                bs(pu, sV, vv)
                                dst = HPAIR[:, (ei * 4 + hc) * T + tq * TQ:
                                            (ei * 4 + hc) * T + (tq + 1) * TQ]
                                v_h = vop("tensor_tensor", dst, SGR[:, :],
                                          USR[:, :], ALU.mult,
                                          waits=((sV, vv), (sA, a_sg)))
                                v_h_t[(ei * 4 + hc, tq)] = v_h
                    for n128 in range(NC):
                        for tq in range(NT):
                            pa = pb4[pidx]
                            pidx = (pidx + 1) % 4
                            vt = 0
                            first = True
                            for ei in range(2):
                                vwl = vd0 if ei == 0 else vd1
                                for hc in range(4):
                                    w0 = bw(pa) if first else ()
                                    vt = mm(pa[:, :],
                                            WBm[:, (sdn + ei) * WSLOT + hc * 1024 +
                                                n128 * 128:(sdn + ei) * WSLOT +
                                                hc * 1024 + n128 * 128 + 128],
                                            HPAIR[:, (ei * 4 + hc) * T + tq * TQ:
                                                  (ei * 4 + hc) * T +
                                                  (tq + 1) * TQ],
                                            first, (ei == 1 and hc == 3),
                                            waits=w0 + (vwl,
                                                        (sV, v_h_t[(ei * 4 + hc,
                                                                    tq)])))
                                    first = False
                            wuse(WBm, sdn, vt)
                            wuse(WBm, sdn + 1, vt)
                            sl = slice(tq * TQ, (tq + 1) * TQ)
                            v_add = vop("scalar_tensor_tensor", XC(n128)[:, sl],
                                        pa[:, :], 1.0, XC(n128)[:, sl], ALU.bypass,
                                        ALU.add, waits=((sT, vt),))
                            bs(pa, sV, v_add)

                vsg = 0
                for tq in range(NT):
                    pa = PS[5]
                    vt = 0
                    for c in range(NC):
                        vt = mm(pa[:, :], WSGB[:, c * 128:(c + 1) * 128],
                                XBC(c)[:, tq * TQ:(tq + 1) * TQ], c == 0,
                                c == NC - 1,
                                waits=(bw(pa) if c == 0 else ()) + (v_wsgb,))
                    vsg = aop("activation", SGBr[:, tq * TQ:(tq + 1) * TQ],
                              pa[:, :], AF.Sigmoid, waits=((sT, vt),))
                    bs(pa, sA, vsg)
                wsg_ = dram["ws_g"].rearrange("(c p) n -> p c n", p=128)
                wsu_ = dram["ws_u"].rearrange("(c p) n -> p c n", p=128)
                wsd_ = dram["ws_d"].rearrange("(f p) n -> p f n", p=128)
                for half in range(2):
                    tqs = (2 * half, 2 * half + 1)
                    a_hid = 0
                    a_hid_t = {}
                    for fs in range(4):
                        v_wg = wload(WBm, 0, wsg_[:, :, fs * TQ:(fs + 1) * TQ],
                                     WSLOT)
                        v_wu = wload(WBm, 1, wsu_[:, :, fs * TQ:(fs + 1) * TQ],
                                     WSLOT)
                        for fsub in range(4):
                            fc = fs * 4 + fsub
                            for ti, tq in enumerate(tqs):
                                pg = pb4[pidx]
                                pidx = (pidx + 1) % 4
                                pu = pb4[pidx]
                                pidx = (pidx + 1) % 4
                                vtg = vtu = 0
                                for c in range(NC):
                                    w0 = bw(pg) if c == 0 else ()
                                    vtg = mm(pg[:, :],
                                             WBm[:, c * TQ + fsub * 128:
                                                 c * TQ + fsub * 128 + 128],
                                             XBC(c)[:, tq * TQ:(tq + 1) * TQ],
                                             c == 0, c == NC - 1,
                                             waits=w0 + (v_wg,))
                                for c in range(NC):
                                    w0 = bw(pu) if c == 0 else ()
                                    vtu = mm(pu[:, :],
                                             WBm[:, WSLOT + c * TQ + fsub * 128:
                                                 WSLOT + c * TQ + fsub * 128 + 128],
                                             XBC(c)[:, tq * TQ:(tq + 1) * TQ],
                                             c == 0, c == NC - 1,
                                             waits=w0 + (v_wu,))
                                wuse(WBm, 0, vtg)
                                wuse(WBm, 1, vtu)
                                a_sg2 = aop("activation", SGR[:, :], pg[:, :],
                                            AF.Silu, waits=((sT, vtg), (sV, a_hid)))
                                bs(pg, sA, a_sg2)
                                dst = HPAIR[:, fc * 1024 + ti * TQ:
                                            fc * 1024 + (ti + 1) * TQ]
                                a_hid = vop("tensor_tensor", dst, SGR[:, :],
                                            pu[:, :], ALU.mult,
                                            waits=((sA, a_sg2), (sT, vtu)))
                                a_hid_t[(fc, ti)] = a_hid
                                bs(pu, sV, a_hid)
                    for ws in range(2):
                        v_wl = wload(WBm, 0, wsd_[:, :, ws * TQ:(ws + 1) * TQ],
                                     16 * TQ)
                        for nsub in range(4):
                            n128 = ws * 4 + nsub
                            for ti, tq in enumerate(tqs):
                                pa = pb4[pidx]
                                pidx = (pidx + 1) % 4
                                vt = 0
                                for fc in range(16):
                                    w0 = bw(pa) if fc == 0 else ()
                                    vt = mm(pa[:, :],
                                            WBm[:, fc * TQ + nsub * 128:
                                                fc * TQ + nsub * 128 + 128],
                                            HPAIR[:, fc * 1024 + ti * TQ:
                                                  fc * 1024 + (ti + 1) * TQ],
                                            fc == 0, fc == 15,
                                            waits=w0 + (v_wl,
                                                        (sV, a_hid_t[(fc, ti)])))
                                wuse(WBm, 0, vt, ncols=2 * WSLOT)
                                sl = slice(tq * TQ, (tq + 1) * TQ)
                                vv = vop("scalar_tensor_tensor", SCRm[:, 0:TQ],
                                         pa[:, :], 1.0, SGBr[:, sl], ALU.bypass,
                                         ALU.mult, waits=((sT, vt), (sA, vsg)))
                                bs(pa, sV, vv)
                                v_add = vop("tensor_tensor", XC(n128)[:, sl],
                                            XC(n128)[:, sl], SCRm[:, 0:TQ],
                                            ALU.add, waits=((sV, vv),))
                                v_xcc[n128] = v_add
            return post_ln_scoped(v_xcc, "pm")

        vv, va, atc = sV.n, sA.n, None
        if 0 in phases:
            vv, va, atc, _ = attention(0, vv, va, atc)
        if 1 in phases:
            vv, va, atc, _ = ffn0(vv, va, atc)
        if 2 in phases:
            vv, va, atc, _ = attention(1, vv, va, atc)
        if 3 in phases:
            vv, va, atc, _ = moe(vv, va, atc)

        P.wait("sync", sV, vv)
        P.wait("sync", sA, va)
        for vxi in v_xc:
            P.wait("sync", vxi[0], vxi[1])
        v_st = dma("sync", "out", dout.rearrange("(c p) t -> p c t", p=128),
                   X[:, :].rearrange("p (c t) -> p c t", c=NC))
        P.wait("sync", v_st[0], v_st[1])

        with nc.Block() as block:
            P.finalize(block)

    return nc


def kernel(**inputs):
    g = host_prepare(inputs)
    x = np.asarray(inputs["L0_attn_x"], np.float32)
    nc = build_program()
    in_maps = []
    for core in range(8):
        m = dict(g)
        m["x"] = np.ascontiguousarray(x[core].T)
        in_maps.append(m)
    res = run_bass_kernel_spmd(nc, in_maps, core_ids=list(range(8)))
    out = np.stack([np.ascontiguousarray(r["out"].T) for r in res.results], axis=0)
    return out.astype(np.float32)


# version 26
# speedup vs baseline: 1.0112x; 1.0101x over previous
import contextlib
import math
import numpy as np
import concourse.bass as bass
import concourse.mybir as mybir
from concourse.bass_utils import run_bass_kernel_spmd

F32 = mybir.dt.float32
BF16 = mybir.dt.bfloat16
I32 = mybir.dt.int32
AX = mybir.AxisListType.X
AF = mybir.ActivationFunctionType
ALU = mybir.AluOpType

D = 1024
T = 2048
NC = 8
NT = 4
TQ = 512
NH = 16
DH = 64
EPS = 1e-5
TWO_PI = 2.0 * math.pi
SIN_SCALE = float(np.nextafter(np.float32(TWO_PI), np.float32(0.0)))
WSLOT = NC * TQ


def _bf():
    import ml_dtypes
    return ml_dtypes.bfloat16


class Sem:
    def __init__(self, handle, name):
        self.h = handle
        self.name = name
        self.n = 0

    def inc(self, k=1):
        self.n += k
        return self.n


class Prog:
    ENGINES = ("sync", "tensor", "vector", "scalar", "gpsimd")

    def __init__(self):
        self.items = {e: [] for e in self.ENGINES}
        self.lastwait = {}

    def em(self, eng, fn):
        self.items[eng].append(fn)

    def wait(self, eng, sem, val):
        if sem is None or not val or val <= 0:
            return
        key = (eng, sem.name)
        if self.lastwait.get(key, -1) >= val:
            return
        self.lastwait[key] = val
        self.em(eng, lambda E, s=sem.h, v=val: E.wait_ge(s, v))

    def finalize(self, block):
        for name in self.ENGINES:
            fns = self.items[name]

            def body(E, fns=fns):
                for fn in fns:
                    fn(E)
            getattr(block, name)(body)


def host_prepare(inputs):
    bf = _bf()

    def t_(a):
        return np.ascontiguousarray(np.asarray(a, np.float32).T)

    def c_(a, shape=None):
        a = np.asarray(a, np.float32)
        if shape is not None:
            a = a.reshape(shape)
        return np.ascontiguousarray(a)

    g = {}
    g["w_qkv0"] = t_(inputs["L0_attn_in_w"])
    g["w_o0"] = t_(inputs["L0_attn_out_w"])
    g["ln1_g"] = c_(inputs["L0_attn_ln1_g"], (NC, 128))
    g["ln1_b"] = c_(inputs["L0_attn_ln1_b"], (NC, 128))
    g["ln2_g"] = c_(inputs["L0_attn_ln2_g"], (NC, 128))
    g["ln2_b"] = c_(inputs["L0_attn_ln2_b"], (NC, 128))
    g["slopes"] = c_(inputs["L0_attn_slopes"], (1, NH))
    g["smear"] = c_(inputs["L0_attn_smear"], (1, NH))
    g["w_ff1"] = c_(inputs["L0_ffn_W1"])
    g["w_ff2"] = c_(inputs["L0_ffn_W2"])
    g["ff_b1"] = c_(inputs["L0_ffn_b1"], (32, 128))
    g["ff_b2"] = c_(inputs["L0_ffn_b2"], (NC, 128))
    g["ff_th"] = c_(inputs["L0_ffn_quantum_phases"], (32, 128))
    g["ff_wr"] = c_(inputs["L0_ffn_quantum_weights_real"], (32, 128))
    g["ff_wi"] = c_(inputs["L0_ffn_quantum_weights_imag"], (32, 128))
    g["w_qkv1"] = c_(inputs["L1_attn_w_attn"])
    g["b_qkv1"] = c_(inputs["L1_attn_b_attn"], (24, 128))
    g["la_a"] = c_(inputs["L1_attn_la_attn"])
    g["lb_a"] = c_(inputs["L1_attn_lb_attn"])
    g["w_p1"] = c_(inputs["L1_attn_w_proj"])
    g["b_p1"] = c_(inputs["L1_attn_b_proj"], (NC, 128))
    g["la_p"] = c_(inputs["L1_attn_la_proj"])
    g["lb_p"] = c_(inputs["L1_attn_lb_proj"])
    g["kv_s"] = c_(inputs["L1_attn_kv_scale"], (1, 1))
    g["kv_z"] = c_(inputs["L1_attn_kv_zero_point"], (1, 1))
    g["wg"] = t_(inputs["L1_ffn_Wg"])
    g["we_g"] = np.ascontiguousarray(np.transpose(np.asarray(inputs["L1_ffn_We_gate"], np.float32), (0, 2, 1)))
    g["we_u"] = np.ascontiguousarray(np.transpose(np.asarray(inputs["L1_ffn_We_up"], np.float32), (0, 2, 1)))
    g["we_d"] = np.ascontiguousarray(np.transpose(np.asarray(inputs["L1_ffn_We_down"], np.float32), (0, 2, 1)))
    g["ws_g"] = t_(inputs["L1_ffn_Ws_gate"])
    g["ws_u"] = t_(inputs["L1_ffn_Ws_up"])
    g["ws_d"] = t_(inputs["L1_ffn_Ws_down"])
    g["wsg"] = np.ascontiguousarray(
        np.tile(t_(inputs["L1_ffn_Wsg"]), (1, 128)))

    g["c_ones"] = np.ones((128, 128), np.float32).astype(bf)
    g["c_eye_f32"] = np.eye(128, dtype=np.float32)
    g["c_eye_bf"] = np.eye(128, dtype=np.float32).astype(bf)
    g["c_ones_row_f32"] = np.ones((1, 128), np.float32)
    p = np.arange(128)
    esel = np.zeros((64, NH, NH), np.float32)
    for h in range(NH):
        esel[:, h, h] = 1.0
    esel = np.ascontiguousarray(esel.reshape(64, NH * NH))
    g["c_esel"] = np.concatenate([esel, esel], axis=0).astype(bf)
    e64ab = np.zeros((96, 128), np.float32)
    e64ab[0, 0:64] = 1.0
    e64ab[32, 64:128] = 1.0
    g["c_e64ab"] = e64ab
    nege16 = np.zeros((NH, NH, 128), np.float32)
    e16p = np.zeros((NH, NH, 128), np.float32)
    for h in range(NH):
        nege16[h, h, :] = -1.0
        e16p[h, h, :] = 1.0
    g["c_nege16"] = np.concatenate(
        [np.ascontiguousarray(nege16.reshape(NH, NH * 128)),
         np.zeros((128 - NH, NH * 128), np.float32)], axis=0).astype(bf)
    g["c_e16p"] = np.ascontiguousarray(e16p.reshape(NH, NH * 128)).astype(bf)
    qsel = np.zeros((128, NC, NH), np.float32)
    for c in range(NC):
        for j in (2 * c, 2 * c + 1):
            qsel[(j % 2) * 64:(j % 2) * 64 + 64, c, j] = 1.0
    g["c_qsel"] = np.ascontiguousarray(qsel.reshape(128, NC * NH)).astype(bf)
    g["c_tri"] = np.where(p[:, None] > p[None, :], np.float32(-1e30),
                          np.float32(0.0)).astype(np.float32)
    g["c_iotak"] = (p[:, None] + 128.0 * np.arange(16)[None, :]).astype(np.float32)
    g["c_arange"] = np.arange(T, dtype=np.float32).reshape(1, T).astype(bf)
    sel2 = np.zeros((NH, NC * 128), np.float32)
    for c in range(NC):
        for pp in range(128):
            sel2[2 * c + pp // 64, c * 128 + pp] = 1.0
    g["c_sel2"] = sel2.astype(bf)
    return g


DRAM_SPECS_F32 = {
    "w_qkv0": [D, 3 * D], "w_o0": [D, D],
    "ln1_g": [NC, 128], "ln1_b": [NC, 128], "ln2_g": [NC, 128], "ln2_b": [NC, 128],
    "slopes": [1, NH], "smear": [1, NH],
    "w_ff1": [D, 4096], "w_ff2": [4096, D],
    "ff_b1": [32, 128], "ff_b2": [NC, 128], "ff_th": [32, 128],
    "ff_wr": [32, 128], "ff_wi": [32, 128],
    "w_qkv1": [D, 3 * D], "b_qkv1": [24, 128],
    "la_a": [D, 16], "lb_a": [16, 3 * D],
    "w_p1": [D, D], "b_p1": [NC, 128], "la_p": [D, 16], "lb_p": [16, D],
    "kv_s": [1, 1], "kv_z": [1, 1],
    "wg": [D, 16],
    "we_g": [16, D, 512], "we_u": [16, D, 512], "we_d": [16, 512, D],
    "ws_g": [D, 2048], "ws_u": [D, 2048], "ws_d": [2048, D], "wsg": [D, 128],
    "c_eye_f32": [128, 128], "c_ones_row_f32": [1, 128],
    "c_e64ab": [96, 128],
    "c_tri": [128, 128], "c_iotak": [128, 16],
}
DRAM_SPECS_BF = {"c_ones": [128, 128], "c_eye_bf": [128, 128],
                 "c_esel": [128, NH * NH],
                 "c_nege16": [128, NH * 128], "c_e16p": [NH, NH * 128],
                 "c_qsel": [128, NC * NH],
                 "c_arange": [1, T], "c_sel2": [NH, NC * 128]}


ATTN_UPTO = 9
SW_P2 = [True]
SW_KINDS = [("r",)]


def build_program(phases=(0, 1, 2, 3)):
    nc = bass.Bass("TRN2", target_bir_lowering=False)
    dx = nc.dram_tensor("x", [D, T], F32, kind="ExternalInput").ap()
    dout = nc.dram_tensor("out", [D, T], F32, kind="ExternalOutput").ap()
    dram = {}
    for k, shp in DRAM_SPECS_F32.items():
        dram[k] = nc.dram_tensor(k, shp, F32, kind="ExternalInput").ap()
    for k, shp in DRAM_SPECS_BF.items():
        dram[k] = nc.dram_tensor(k, shp, BF16, kind="ExternalInput").ap()

    P = Prog()
    top = contextlib.ExitStack()
    wb_birth = {}

    def sb(name, shp, dt=BF16, st=None):
        t = (st or top).enter_context(nc.sbuf_tensor(name, shp, dt))
        wb_birth[id(t)] = sT.n if "sT" in locals() or True else 0
        return t

    with top:
        sG = Sem(top.enter_context(nc.semaphore("sG")), "sG")
        sS = Sem(top.enter_context(nc.semaphore("sS")), "sS")
        sT = Sem(top.enter_context(nc.semaphore("sT")), "sT")
        sV = Sem(top.enter_context(nc.semaphore("sV")), "sV")
        sA = Sem(top.enter_context(nc.semaphore("sA")), "sA")

        def sb(name, shp, dt=BF16, st=None):
            t = (st or top).enter_context(nc.sbuf_tensor(name, shp, dt))
            wb_birth[id(t)] = (sT.n, sV.n, sA.n)
            return t

        X = sb("X", [128, NC * T], F32)
        XB = sb("XB", [128, NC * T], BF16)
        SQR = sb("SQR", [128, 2 * TQ], BF16)
        SMALLF = sb("SMALLF", [128, 512], F32)
        COLS = sb("COLS", [128, 224], F32)
        QCOL = sb("QCOL", [128, 24], F32)
        KVC = sb("KVC", [128, 8], F32)
        CONES = sb("CONES", [128, 128], BF16)
        CEYEF = sb("CEYEF", [128, 128], F32)
        CEYEB = sb("CEYEB", [128, 128], BF16)
        CONER = sb("CONER", [1, 128], F32)
        CSEL128 = sb("CSEL128", [128, NC * NH], BF16)
        CTRI = sb("CTRI", [128, 128], F32)
        CIOTAK = sb("CIOTAK", [128, 16], F32)

        SLP8 = sb("SLP8", [16, 4], F32)
        SLP8B = sb("SLP8B", [16, 2], BF16)
        SLPB = sb("SLPB", [128, 16], F32)
        SCOLS = sb("SCOLS", [128, 16], F32)
        MAXK = sb("MAXK", [16, 8], F32)
        WGB = sb("WGB", [128, NC * 16], BF16)
        WSGB = sb("WSGB", [128, NC * 128], BF16)
        GSCR = sb("GSCR", [128, 64], F32)
        GSCRB = sb("GSCRB", [128, NC * 16], BF16)
        RI32A = sb("RI32A", [128, TQ], I32)
        CEPS = sb("CEPS", [128, 1], F32)
        CPI2 = sb("CPI2", [128, 1], F32)

        PS = [top.enter_context(nc.psum_tensor(f"ps{i}", [128, 512], F32))
              for i in range(6)]

        psum_owner = {}

        def bw(pst):
            w = psum_owner.get(id(pst))
            return (w,) if w else ()

        def bs(pst, sem, val):
            psum_owner[id(pst)] = (sem, val)

        dma_sems = {}

        def dma(kind, stream, out_ap, in_ap, waits=(), slow=False):
            eng = "sync" if kind == "sync" else "gpsimd"
            if stream not in dma_sems:
                h = top.enter_context(nc.semaphore("sD_" + stream))
                dma_sems[stream] = Sem(h, "sD_" + stream)
            s = dma_sems[stream]
            tb = wb_birth.get(id(getattr(out_ap, "tensor", None)))
            if tb is not None:
                waits = waits + ((sT, tb[0]), (sV, tb[1]), (sA, tb[2]))
            for (sem, val) in waits:
                P.wait(eng, sem, val)
            v = s.inc(16)
            P.em(eng, lambda E, o=out_ap, i=in_ap, sh=s.h, sl=slow:
                 E.dma_start(out=o, in_=i, allow_slow_non_contiguous=sl)
                 .then_inc(sh, 16))
            return (s, v)

        PACE = [12]
        SW = [False]
        SW_P2[0] = SW_P2[0]

        def mm(out_ap, lhsT, rhs, start, stop, waits=(), tp=None, inc=None,
               swkind=None):
            if inc is None:
                inc = stop
            if SW[0] and (swkind in SW_KINDS[0]):
                inc = True
                P.em("tensor", lambda E, v=sT.n: E.wait_ge(sT.h, v))
            elif start:
                P.em("tensor", lambda E, v=max(0, sT.n - PACE[0]):
                     E.wait_ge(sT.h, v))
            for (sem, val) in waits:
                P.wait("tensor", sem, val)
            if inc:
                v = sT.inc()
                P.em("tensor",
                     lambda E, o=out_ap, l=lhsT, r=rhs, a=start, b=stop, t=tp:
                     E.matmul(o, l, r, start=a, stop=b, tile_position=t)
                     .then_inc(sT.h, 1))
                return v
            P.em("tensor", lambda E, o=out_ap, l=lhsT, r=rhs, a=start, b=stop, t=tp:
                 E.matmul(o, l, r, start=a, stop=b, tile_position=t))
            return sT.n

        def tr(out_ap, in_ap, ident, waits=()):
            P.em("tensor", lambda E, v=max(0, sT.n - PACE[0]): E.wait_ge(sT.h, v))
            for (sem, val) in waits:
                P.wait("tensor", sem, val)
            v = sT.inc()
            P.em("tensor", lambda E, o=out_ap, i=in_ap, d=ident:
                 E.transpose(o, i, d).then_inc(sT.h, 1))
            return v

        def vop(name, *args, waits=(), **kw):
            P.em("vector", lambda E, v=sV.n: E.wait_ge(sV.h, v))
            for (sem, val) in waits:
                P.wait("vector", sem, val)
            v = sV.inc()
            P.em("vector", lambda E, nm=name, a=args, k=kw:
                 getattr(E, nm)(*a, **k).then_inc(sV.h, 1))
            return v

        def aop(name, *args, waits=(), **kw):
            P.em("scalar", lambda E, v=sA.n: E.wait_ge(sA.h, v))
            for (sem, val) in waits:
                P.wait("scalar", sem, val)
            v = sA.inc()
            P.em("scalar", lambda E, nm=name, a=args, k=kw:
                 getattr(E, nm)(*a, **k).then_inc(sA.h, 1))
            return v

        def gop(name, *args, waits=(), **kw):
            P.em("gpsimd", lambda E, v=sG.n: E.wait_ge(sG.h, v))
            for (sem, val) in waits:
                P.wait("gpsimd", sem, val)
            v = sG.inc()
            P.em("gpsimd", lambda E, nm=name, a=args, k=kw:
                 getattr(E, nm)(*a, **k).then_inc(sG.h, 1))
            return v

        def XC(c):
            return X[:, c * T:(c + 1) * T]

        def XBC(c):
            return XB[:, c * T:(c + 1) * T]

        slot_use = {}

        def wload(wb, slot, dram_ap, ncols, swidth=None):
            bt, bv, ba = wb_birth.get(id(wb), (0, 0, 0))
            if ncols <= WSLOT:
                off = slot * (swidth or WSLOT)
                waits = ((sT, slot_use.get((id(wb), slot), 0)),
                         (sT, slot_use.get((id(wb), "all"), 0)),
                         (sT, bt), (sV, bv), (sA, ba))
                stream = wb.name + "_" + str(slot)
            else:
                off = 0
                waits = ((sT, max(slot_use.get((id(wb), 0), 0),
                                  slot_use.get((id(wb), 1), 0),
                                  slot_use.get((id(wb), "all"), 0), bt)),
                         (sV, bv), (sA, ba))
                stream = wb.name + "_all"
            return dma("gp", stream, wb[:, off:off + ncols], dram_ap, waits=waits)

        def wuse(wb, slot, vt, ncols=0):
            key = (id(wb), slot) if ncols <= WSLOT else (id(wb), "all")
            slot_use[key] = max(slot_use.get(key, 0), vt)

        vop("memset", CEPS[:, :], EPS)
        vop("memset", CPI2[:, :], math.pi / 2.0)
        v_x = dma("sync", "x", X[:, :].rearrange("p (c t) -> p c t", c=NC), dx.rearrange("(c p) t -> p c t", p=128))
        v_consts = 0
        for (name, tile) in (("c_ones", CONES), ("c_eye_f32", CEYEF),
                             ("c_eye_bf", CEYEB),
                             ("c_ones_row_f32", CONER),
                             ("c_qsel", CSEL128),
                             ("c_tri", CTRI), ("c_iotak", CIOTAK),
                             ):
            v_consts = dma("sync", "consts", tile[0:tile.shape[0], :], dram[name])
        v_wgb = dma("gp", "wgb", WGB[:, :].rearrange("p (c n) -> p c n", c=NC), dram["wg"].rearrange("(c p) n -> p c n", p=128))
        v_wsgb = dma("gp", "wsgb", WSGB[:, :].rearrange("p (c n) -> p c n", c=NC),
                     dram["wsg"].rearrange("(c p) n -> p c n", p=128))

        with contextlib.ExitStack() as pre:
            PSCR = sb("PSCR", [128, 448], F32, pre)
            CSEL2 = sb("CSEL2", [NH, NC * 128], BF16, pre)
            v_consts2 = dma("sync", "cols", CSEL2[0:NH, :], dram["c_sel2"])

            def load_cols(name, n, dst):
                vl = dma("sync", "cols", SMALLF[0:n, 0:128], dram[name],
                         waits=((sV, sV.n), (sT, sT.n), (sA, sA.n)))
                vt = tr(PS[0][:, 0:n], SMALLF[0:n, 0:128], CEYEF[0:n, 0:n],
                        waits=bw(PS[0]) + (vl, v_consts))
                vv = vop("tensor_copy", dst, PS[0][:, 0:n], waits=((sT, vt),))
                bs(PS[0], sV, vv)
                return vv

            load_cols("ln1_g", NC, COLS[:, 0:8])
            load_cols("ln1_b", NC, COLS[:, 8:16])
            load_cols("ln2_g", NC, COLS[:, 16:24])
            load_cols("ln2_b", NC, COLS[:, 24:32])
            load_cols("ff_b2", NC, COLS[:, 64:72])
            load_cols("b_p1", NC, COLS[:, 136:144])
            load_cols("b_qkv1", 24, QCOL[:, 0:24])
            v_th = load_cols("ff_th", 32, PSCR[:, 416:448])
            v_wr = load_cols("ff_wr", 32, PSCR[:, 0:32])
            v_wi = load_cols("ff_wi", 32, PSCR[:, 32:64])
            v_b1 = load_cols("ff_b1", 32, COLS[:, 104:136])
            TH = PSCR[:, 416:448]

            def sin32(dst, src, bias, base, extra_waits=()):
                u = PSCR[:, base:base + 32]
                kf = PSCR[:, base + 32:base + 64]
                r1 = PSCR[:, base + 64:base + 96]
                c1 = PSCR[:, base + 96:base + 128]
                vop("tensor_scalar", u, src, 1.0 / TWO_PI, bias / TWO_PI,
                    ALU.mult, ALU.add, waits=extra_waits)
                vop("tensor_copy", RI32A[:, 0:32], u)
                vop("tensor_copy", kf, RI32A[:, 0:32])
                vop("tensor_tensor", r1, u, kf, ALU.subtract)
                vop("tensor_scalar", c1, r1, 0.5, None, ALU.is_ge)
                vop("scalar_tensor_tensor", r1, c1, -1.0, r1, ALU.mult, ALU.add)
                vop("tensor_scalar", c1, r1, -0.5, None, ALU.is_lt)
                vv = vop("scalar_tensor_tensor", r1, c1, 1.0, r1, ALU.mult, ALU.add)
                return aop("activation", dst, r1, AF.Sin, scale=SIN_SCALE,
                           waits=((sV, vv),))

            va1 = sin32(PSCR[:, 320:352], TH, 0.0, 64, extra_waits=((sV, v_th),))
            va2 = sin32(PSCR[:, 352:384], TH, math.pi / 2.0, 192,
                        extra_waits=((sA, va1),))
            vop("tensor_scalar", PSCR[:, 128:160], TH, 1.0 / TWO_PI, None, ALU.mult,
                waits=((sA, va2),))
            vop("tensor_copy", RI32A[:, 32:64], PSCR[:, 128:160])
            vop("tensor_copy", PSCR[:, 160:192], RI32A[:, 32:64])
            vop("tensor_tensor", PSCR[:, 128:160], PSCR[:, 128:160],
                PSCR[:, 160:192], ALU.subtract)
            vop("tensor_scalar", PSCR[:, 224:256], PSCR[:, 128:160], 0.5, None,
                ALU.is_ge)
            vop("scalar_tensor_tensor", PSCR[:, 160:192], PSCR[:, 224:256], 1.0,
                PSCR[:, 160:192], ALU.mult, ALU.add)
            vop("tensor_scalar", PSCR[:, 224:256], PSCR[:, 128:160], -0.5, None,
                ALU.is_lt)
            vop("scalar_tensor_tensor", PSCR[:, 160:192], PSCR[:, 224:256], -1.0,
                PSCR[:, 160:192], ALU.mult, ALU.add)
            v_thst = vop("scalar_tensor_tensor", COLS[:, 32:64], PSCR[:, 160:192],
                         -TWO_PI, TH, ALU.mult, ALU.add)
            vop("tensor_tensor", PSCR[:, 384:416], PSCR[:, 352:384], PSCR[:, 0:32],
                ALU.mult, waits=((sA, va2), (sV, v_wr)))
            vop("tensor_tensor", PSCR[:, 64:96], PSCR[:, 320:352], PSCR[:, 32:64],
                ALU.mult, waits=((sV, v_wi),))
            vop("tensor_tensor", PSCR[:, 384:416], PSCR[:, 384:416], PSCR[:, 64:96],
                ALU.add)
            vop("tensor_tensor", PSCR[:, 384:416], PSCR[:, 384:416], PSCR[:, 32:64],
                ALU.mult)
            vop("tensor_scalar", COLS[:, 72:104], PSCR[:, 384:416], 0.1, None, ALU.mult)
            vop("tensor_copy", COLS[:, 160:192], PSCR[:, 320:352],
                waits=((sA, va2),))
            v_thb = vop("tensor_copy", COLS[:, 192:224], PSCR[:, 352:384])

            vl = dma("sync", "cols", SMALLF[0:1, 0:NH], dram["slopes"],
                     waits=((sV, sV.n), (sT, sT.n), (sA, sA.n)))
            vt = tr(PS[0][0:NH, 0:1], SMALLF[0:1, 0:NH], CEYEF[0:1, 0:1],
                    waits=bw(PS[0]) + (vl,))
            vv = vop("tensor_scalar", SLP8[0:NH, 0:1], PS[0][0:NH, 0:1], 8.0, None,
                     ALU.mult, waits=((sT, vt),))
            bs(PS[0], sV, vv)
            vt = mm(PS[1][:, 0:NH], CONER[:, :], SMALLF[0:1, 0:NH], True, True,
                    waits=bw(PS[1]) + (vl,))
            v_slb = vop("tensor_copy", SLPB[:, 0:16], PS[1][:, 0:NH], waits=((sT, vt),))
            bs(PS[1], sV, v_slb)
            vl = dma("sync", "cols", SMALLF[0:1, 16:16 + NH], dram["smear"],
                     waits=((sV, sV.n), (sT, sT.n), (sA, sA.n)))
            va = aop("activation", SMALLF[0:1, 32:32 + NH], SMALLF[0:1, 16:16 + NH],
                     AF.Sigmoid, waits=(vl,))
            vt = tr(PS[0][0:NH, 0:1], SMALLF[0:1, 32:32 + NH], CEYEF[0:1, 0:1],
                    waits=bw(PS[0]) + ((sA, va),))
            v_sg = vop("tensor_copy", SLP8B[0:NH, 0:1], PS[0][0:NH, 0:1],
                       waits=((sT, vt),))
            bs(PS[0], sV, v_sg)
            vt2 = 0
            for c in range(NC):
                vt2 = mm(PS[1][:, c:c + 1], CSEL2[:, c * 128:(c + 1) * 128],
                         SLP8B[0:NH, 0:1], c == 0, c == NC - 1,
                         waits=(bw(PS[1]) if c == 0 else ()) +
                         ((sV, v_sg), v_consts2))
            v_sc = vop("tensor_copy", SCOLS[:, 0:8], PS[1][:, 0:8], waits=((sT, vt2),))
            bs(PS[1], sV, v_sc)
            v_sc = vop("tensor_scalar", SCOLS[:, 8:16], SCOLS[:, 0:8], -1.0, 1.0,
                       ALU.mult, ALU.add, waits=((sV, v_sc),))
            vl = dma("sync", "cols", SMALLF[0:1, 48:49], dram["kv_s"],
                     waits=((sV, sV.n), (sT, sT.n), (sA, sA.n)))
            vl = dma("sync", "cols", SMALLF[0:1, 49:50], dram["kv_z"],
                     waits=((sV, sV.n), (sT, sT.n), (sA, sA.n)))
            vt = mm(PS[1][:, 0:2], CONER[:, :], SMALLF[0:1, 48:50], True, True,
                    waits=bw(PS[1]) + (vl,))
            vk = vop("tensor_copy", KVC[:, 2:3], PS[1][:, 0:1], waits=((sT, vt),))
            vk = vop("tensor_copy", KVC[:, 1:2], PS[1][:, 1:2])
            bs(PS[1], sV, vk)
            vk = vop("reciprocal", KVC[:, 0:1], KVC[:, 2:3], waits=((sV, vk),))
            vk = vop("tensor_tensor", KVC[:, 3:4], KVC[:, 2:3], KVC[:, 1:2], ALU.mult)
            v_kv = vop("tensor_scalar", KVC[:, 3:4], KVC[:, 3:4], -1.0, None, ALU.mult,
                       waits=((sV, vk),))

        def ln_full(MUt, RSTDt, SCRt, xb_fn, wait_a, gcol=None, bcol=None,
                    write_f32=False, src_fn=None, add_to_x=False):
            src_fn = src_fn or XC
            v_stats = [0] * NT
            v_last, a_last = 0, 0
            a_buf = {0: 0, 1: 0}
            a_tc = [[0] * NC for _ in range(NT)]
            v_cc = [0] * NC

            def wac(c):
                return wait_a[c] if isinstance(wait_a, list) else wait_a

            def stats(tq):
                sl = slice(tq * TQ, (tq + 1) * TQ)
                pa, pb = PS[4], PS[5]
                vta = 0
                for c in range(NC):
                    vta = mm(pa[:, :], CONES[:, :], xb_fn(c)[:, sl], c == 0,
                             c == NC - 1,
                             waits=(bw(pa) if c == 0 else ()) + ((sA, wac(c)),))
                vtb = 0
                for c in range(NC):
                    ring = SQR[:, (c % 2) * TQ:(c % 2 + 1) * TQ]
                    va_sq = aop("activation", ring, xb_fn(c)[:, sl], AF.Square,
                                waits=((sA, wac(c)), (sT, vtb)))
                    vtb = mm(pb[:, :], CONES[:, :], ring, c == 0, c == NC - 1,
                             waits=(bw(pb) if c == 0 else ()) + ((sA, va_sq),),
                             inc=True)
                va_mu = aop("activation", MUt[:, sl], pa[:, :], AF.Copy,
                            scale=1.0 / D, waits=((sT, vta),))
                bs(pa, sA, va_mu)
                v1 = vop("scalar_tensor_tensor", SCRt[:, 2 * TQ:3 * TQ], MUt[:, sl],
                         -1.0, MUt[:, sl], ALU.mult, ALU.mult, waits=((sA, va_mu),))
                v2 = vop("scalar_tensor_tensor", SCRt[:, 2 * TQ:3 * TQ], pb[:, :],
                         1.0 / D, SCRt[:, 2 * TQ:3 * TQ], ALU.mult, ALU.add,
                         waits=((sV, v1), (sT, vtb)))
                bs(pb, sV, v2)
                va_sd = aop("activation", RSTDt[:, sl], SCRt[:, 2 * TQ:3 * TQ],
                            AF.Sqrt, bias=CEPS[:, 0:1], waits=((sV, v2),))
                vx = vop("reciprocal", RSTDt[:, sl], RSTDt[:, sl],
                         waits=((sA, va_sd),))
                vx = vop("scalar_tensor_tensor", MUt[:, sl], MUt[:, sl], -1.0,
                         RSTDt[:, sl], ALU.mult, ALU.mult, waits=((sV, vx),))
                v_stats[tq] = vx

            def apply(tq):
                nonlocal v_last, a_last
                sl = slice(tq * TQ, (tq + 1) * TQ)
                for c in range(NC):
                    par = c % 2
                    buf = SCRt[:, 0:TQ] if par == 0 else SCRt[:, TQ:2 * TQ]
                    vw = vop("tensor_tensor", buf, src_fn(c)[:, sl], RSTDt[:, sl],
                             ALU.mult, waits=((sV, v_stats[tq]), (sA, a_buf[par])))
                    vw = vop("tensor_tensor", buf, buf, MUt[:, sl], ALU.add,
                             waits=((sV, vw),))
                    if gcol is not None:
                        vw = vop("tensor_scalar", buf, buf, gcol[:, c:c + 1],
                                 bcol[:, c:c + 1], ALU.mult, ALU.add,
                                 waits=((sV, vw),))
                    if add_to_x:
                        v_last = vop("tensor_tensor", XC(c)[:, sl], XC(c)[:, sl],
                                     buf, ALU.add, waits=((sV, vw),))
                        v_cc[c] = v_last
                        continue
                    if write_f32:
                        vw = vop("tensor_copy", XC(c)[:, sl], buf, waits=((sV, vw),))
                        v_cc[c] = vw
                    a_last = aop("activation", XBC(c)[:, sl], buf, AF.Copy,
                                 waits=((sV, vw),))
                    a_tc[tq][c] = a_last
                    a_buf[par] = a_last
                    v_last = vw

            for tq in range(NT):
                stats(tq)
                if tq > 0:
                    apply(tq - 1)
            apply(NT - 1)
            return v_last, a_last, a_tc, v_cc

        def post_ln_scoped(wait_vc, tag, t_xbr=0):
            with contextlib.ExitStack() as lnst:
                MUt = sb("MU" + tag, [128, T], F32, lnst)
                RSTDt = sb("RS" + tag, [128, T], F32, lnst)
                SCRt = sb("SC" + tag, [128, 3 * TQ], F32, lnst)
                a_cast_c = [0] * NC
                for c in range(NC):
                    wv = wait_vc[c] if isinstance(wait_vc, list) else wait_vc
                    a_cast_c[c] = aop("activation", XBC(c), XC(c), AF.Copy,
                                      waits=((sV, wv), (sT, t_xbr)))
                return ln_full(MUt, RSTDt, SCRt, lambda c: XBC(c), a_cast_c,
                               write_f32=True)

        def attention(layer, v_in_v, a_in_a, atc_in=None):
            is0 = (layer == 0)
            tag = str(layer)
            use = {}
            wqname = "w_qkv0" if is0 else "w_qkv1"
            stA = contextlib.ExitStack()
            with stA:
                ARK = sb("ARK" + tag, [128, NC * T], BF16, stA)
                ARV = sb("ARV" + tag, [128, 16 * D], BF16, stA)
                HROW = sb("HROW" + tag, [16, TQ], BF16, stA)
                AUXR = sb("AUXR" + tag, [128, TQ], BF16, stA) if is0 else None
                T1L = sb("T1L" + tag, [16, T], BF16, stA) if not is0 else None
                WS1L = sb("WS1L" + tag, [16, 3 * D], BF16, stA) if not is0 else None
                BIASC = sb("BIASC" + tag, [128, NH * 16], F32, stA) if is0 else None
                CAR = sb("CAR", [1, TQ], BF16, stA) if is0 else None

                def AK(c):
                    return ARK[:, c * T:(c + 1) * T]

                def AVS(kt):
                    return ARV[:, kt * D:(kt + 1) * D]

                if is0:
                    with contextlib.ExitStack() as lnst:
                        MUt = sb("MUa", [128, T], F32, lnst)
                        RSTDt = sb("RSa", [128, T], F32, lnst)
                        SCRt = sb("SCa", [128, 3 * TQ], F32, lnst)
                        a_cast = 0
                        for c in range(NC):
                            a_cast = aop("activation", XBC(c), XC(c), AF.Copy,
                                         waits=(v_x, (sV, v_in_v)))
                        v_hin, a_hin, atc_h, _ = ln_full(
                            MUt, RSTDt, SCRt, lambda c: XBC(c), a_cast,
                            gcol=COLS[:, 0:8], bcol=COLS[:, 8:16])
                    v_biasc = 0
                    for h in range(NH):
                        v_biasc = vop("tensor_scalar", BIASC[:, h * 16:(h + 1) * 16],
                                      CIOTAK[:, :], SLPB[:, h:h + 1], None, ALU.mult,
                                      waits=((sV, v_slb), v_consts))
                else:
                    v_hin, a_hin = v_in_v, a_in_a
                    atc_h = atc_in
                    v_biasc = 0

                def wh(tq, c):
                    if atc_h is not None:
                        return ((sA, atc_h[tq][c]),)
                    return ((sA, a_hin), (sV, v_hin))

                if ATTN_UPTO <= 1:
                    return post_ln_scoped(sV.n, "pa" + tag)
                a_t1 = 0
                if not is0:
                    use["ws1"] = dma("gp", "ws1l", WS1L[:, :], dram["lb_a"],
                                     waits=((sT, sT.n),))
                    vla = dma("gp", "gscrb", GSCRB[:, 0:NC * 16].rearrange("p (c r) -> p c r", c=NC),
                              dram["la_a"].rearrange("(c p) r -> p c r", p=128),
                              waits=((sT, sT.n),))
                    for tq in range(NT):
                        pa = PS[4]
                        vt = 0
                        for c in range(NC):
                            vt = mm(pa[0:16, :], GSCRB[:, c * 16:(c + 1) * 16],
                                    XBC(c)[:, tq * TQ:(tq + 1) * TQ], c == 0,
                                    c == NC - 1,
                                    waits=(bw(pa) if c == 0 else ()) +
                                    (vla,) + wh(tq, c))
                        a_t1 = aop("activation", T1L[:, tq * TQ:(tq + 1) * TQ],
                                   pa[0:16, :], AF.Copy, scale=2.0, waits=((sT, vt),))
                        bs(pa, sA, a_t1)

                with contextlib.ExitStack() as kvst:
                    PSB0 = kvst.enter_context(
                        nc.psum_tensor("psb0" + tag, [128, 1024], BF16))
                    PSB1 = kvst.enter_context(
                        nc.psum_tensor("psb1" + tag, [128, 1024], BF16))
                    WBkv = sb("WBkv" + tag, [128, 2 * WSLOT], BF16, kvst)
                    VSTG = sb("VSTG" + tag, [128, 2 * TQ], BF16, kvst)
                    if is0:
                        KRAW = sb("KRAW", [128, 2 * (TQ + 1)], BF16, kvst)
                        KTMP = sb("KTMP", [128, TQ], BF16, kvst)
                    else:
                        FQY = sb("FQY", [128, TQ], F32, kvst)
                        FQC = SQR[:, 0:TQ]
                        FQR = sb("FQR", [128, TQ], F32, kvst)
                        FQI = RI32A

                    MAGIC = 12582912.0

                    def fakequant(psrc, dst_ap, wt):
                        a1 = aop("activation", FQY[:, :], psrc, AF.Identity,
                                 scale=KVC[:, 0:1], bias=KVC[:, 1:2],
                                 waits=wt + ((sV, v_kv), (sV, use.get("fq", 0))))
                        vop("tensor_scalar", FQR[:, :], FQY[:, :], MAGIC, None,
                            ALU.add, waits=((sA, a1),))
                        vop("tensor_scalar", FQR[:, :], FQR[:, :], -MAGIC, None,
                            ALU.add)
                        vq = vop("tensor_scalar", FQR[:, :], FQR[:, :], 0.0, 255.0,
                                 ALU.max, ALU.min)
                        use["fq"] = vq
                        return aop("activation", dst_ap, FQR[:, :], AF.Identity,
                                   scale=KVC[:, 2:3], bias=KVC[:, 3:4],
                                   waits=((sV, vq),))

                    wq = dram[wqname].rearrange("(c p) n -> p c n", p=128)
                    pb4 = [PS[0], PS[1], PS[2], PS[3]]
                    pidx = 0
                    pend_v = [None]

                    def emit_vtr(a_ev, tqv, c2v, vparv):
                        for j in range(4):
                            kt = tqv * 4 + j
                            pbx = PSB0 if j % 2 == 0 else PSB1
                            vt2 = tr(pbx[:, 0:128],
                                     VSTG[:, vparv * TQ + j * 128:
                                          vparv * TQ + (j + 1) * 128],
                                     CEYEB[:, :],
                                     waits=bw(pbx) + ((sA, a_ev),))
                            vv2 = vop("tensor_copy",
                                      AVS(kt)[:, c2v * 128:(c2v + 1) * 128],
                                      pbx[:, 0:128], waits=((sT, vt2),))
                            bs(pbx, sV, vv2)
                            use[("vstg", vparv)] = vt2

                    for ns in range(2, 6):
                        slot = ns % 2
                        v_wl = wload(WBkv, slot, wq[:, :, ns * TQ:(ns + 1) * TQ],
                                     WSLOT)
                        for nsub in range(4):
                            n128 = ns * 4 + nsub
                            for tq in range(NT):
                                pa = pb4[pidx]
                                pidx = (pidx + 1) % 4
                                vt = 0
                                for c in range(NC):
                                    w0 = bw(pa) if c == 0 else ()
                                    vt = mm(pa[:, :],
                                            WBkv[:, slot * WSLOT + c * TQ +
                                                 nsub * 128:slot * WSLOT +
                                                 c * TQ + nsub * 128 + 128],
                                            XBC(c)[:, tq * TQ:(tq + 1) * TQ],
                                            c == 0, (c == NC - 1) and is0,
                                            waits=w0 + (v_wl,) + wh(tq, c))
                                if not is0:
                                    vt = mm(pa[:, :],
                                            WS1L[:, n128 * 128:(n128 + 1) * 128],
                                            T1L[:, tq * TQ:(tq + 1) * TQ], False,
                                            True, waits=((sA, a_t1), use["ws1"]))
                                wuse(WBkv, slot, vt)
                                sl = slice(tq * TQ, (tq + 1) * TQ)
                                if n128 < 16:
                                    c2 = n128 - 8
                                    if is0:
                                        ksl = tq % 2
                                        base = ksl * (TQ + 1)
                                        a_ev = aop("activation",
                                                   KRAW[:, base + 1:base + 1 + TQ],
                                                   pa[:, :], AF.Copy,
                                                   waits=((sT, vt),
                                                          (sV, use.get(("kr", ksl), 0))))
                                        bs(pa, sA, a_ev)
                                        if tq == 0:
                                            vb = vop("memset",
                                                     KRAW[:, base:base + 1], 0.0,
                                                     waits=((sA, a_ev),))
                                        else:
                                            ob = (1 - ksl) * (TQ + 1)
                                            vb = vop("tensor_copy",
                                                     KRAW[:, base:base + 1],
                                                     KRAW[:, ob + TQ:ob + TQ + 1],
                                                     waits=((sA, a_ev),))
                                            use[("kr", 1 - ksl)] = max(
                                                use.get(("kr", 1 - ksl), 0), vb)
                                        vb = vop("tensor_scalar", KTMP[:, :],
                                                 KRAW[:, base + 1:base + 1 + TQ],
                                                 SCOLS[:, 8 + c2:9 + c2], None,
                                                 ALU.mult,
                                                 waits=((sV, vb), (sV, v_sc)))
                                        use[("kr", ksl)] = vop(
                                            "scalar_tensor_tensor", AK(c2)[:, sl],
                                            KRAW[:, base:base + TQ],
                                            SCOLS[:, c2:c2 + 1], KTMP[:, :],
                                            ALU.mult, ALU.add, waits=((sV, vb),))
                                    else:
                                        a_ev = fakequant(pa[:, :], AK(c2)[:, sl],
                                                         ((sT, vt),))
                                        bs(pa, sA, a_ev)
                                else:
                                    c2 = n128 - 16
                                    vpar = (n128 * NT + tq) % 2
                                    vdst = VSTG[:, vpar * TQ:(vpar + 1) * TQ]
                                    wv = ((sT, vt),
                                          (sT, use.get(("vstg", vpar), 0)))
                                    if is0:
                                        a_ev = aop("activation", vdst,
                                                   pa[:, :], AF.Copy, waits=wv)
                                    else:
                                        a_ev = fakequant(pa[:, :], vdst, wv)
                                    bs(pa, sA, a_ev)
                                    if pend_v[0] is not None:
                                        emit_vtr(*pend_v[0])
                                    pend_v[0] = (a_ev, tq, c2, vpar)
                    if pend_v[0] is not None:
                        emit_vtr(*pend_v[0])
                    v_kv_v = sV.n
                    a_kv_a = sA.n

                    vk2 = 0
                    vtb_p = {0: 0, 1: 0}
                    for tq in range(NT):
                        sl = slice(tq * TQ, (tq + 1) * TQ)
                        pbk = PS[5]
                        vtb = 0
                        for cp in range(NC):
                            par = cp % 2
                            ring = SQR[:, par * TQ:(par + 1) * TQ]
                            ak = aop("activation", ring[:, :], AK(cp)[:, sl],
                                     AF.Square,
                                     waits=((sV, v_kv_v), (sA, a_kv_a),
                                            (sT, vtb_p[par])))
                            vtb = mm(pbk[0:16, :],
                                     CSEL128[:, cp * 16:(cp + 1) * 16],
                                     ring[:, :], cp == 0, cp == NC - 1,
                                     waits=(bw(pbk) if cp == 0 else ()) + ((sA, ak),),
                                     inc=True)
                            vtb_p[par] = vtb
                        vk2 = vop("reduce_max", MAXK[0:16, tq:tq + 1], pbk[0:16, :],
                                  AX, waits=((sT, vtb),))
                        bs(pbk, sV, vk2)
                    vk2 = vop("reduce_max", MAXK[0:16, 4:5], MAXK[0:16, 0:4], AX,
                              waits=((sV, vk2),))
                    a_mk = aop("activation", MAXK[0:16, 5:6], MAXK[0:16, 4:5],
                               AF.Sqrt, waits=((sV, vk2),))

                if ATTN_UPTO <= 2:
                    return post_ln_scoped(sV.n, "pa" + tag)
                PS6 = stA.enter_context(
                    nc.psum_tensor("ps6" + tag, [128, 512], F32))
                PS7 = stA.enter_context(
                    nc.psum_tensor("ps7" + tag, [128, 512], F32))
                ARQ = sb("ARQ" + tag, [128, 2 * NC * TQ], BF16, stA)
                PT = sb("PT" + tag, [128, 4 * TQ], BF16, stA)
                ONRM = SMALLF
                WBq = sb("WBq" + tag, [128, 2048], BF16, stA)
                BIA1 = sb("BIA1", [128, 16], F32, stA) if not is0 else None
                if is0:
                    NEGE16 = sb("NEGE16", [128, NH * 128], BF16, stA)
                    v_ng = dma("sync", "nege", NEGE16[:, :], dram["c_nege16"])
                else:
                    NEGE16, v_ng = None, 0

                def AQZ(c, hi):
                    return ARQ[:, (2 * c + hi) * TQ:(2 * c + hi + 1) * TQ]

                v_zq = 0
                for b16 in range(2 * NC):
                    v_zq = vop("memset", ARQ[:, b16 * TQ:(b16 + 1) * TQ], 0.0)
                if is0:
                    v_zq = vop("memset", AUXR[0:128, 0:TQ], 0.0)

                wq = dram[wqname].rearrange("(c p) n -> p c n", p=128)
                for qw in range(NT):
                    hqs = (qw,)
                    pb4 = [PS[0], PS[1], PS[2], PS[3]]
                    pidx = 0
                    for n128 in range(8):
                        v_wl = wload(WBq, n128 % 2,
                                     wq[:, :, n128 * 128:(n128 + 1) * 128], 1024,
                                     swidth=1024)
                        if True:
                            for ti, tq in enumerate(hqs):
                                pa = pb4[pidx]
                                pidx = (pidx + 1) % 4
                                vt = 0
                                for c in range(NC):
                                    w0 = bw(pa) if c == 0 else ()
                                    vt = mm(pa[:, :],
                                            WBq[:, (n128 % 2) * 1024 + c * 128:
                                                (n128 % 2) * 1024 + c * 128 + 128],
                                            XBC(c)[:, tq * TQ:(tq + 1) * TQ],
                                            c == 0, (c == NC - 1) and is0,
                                            waits=w0 + (v_wl,) + wh(tq, c))
                                if not is0:
                                    vt = mm(pa[:, :],
                                            WS1L[:, n128 * 128:(n128 + 1) * 128],
                                            T1L[:, tq * TQ:(tq + 1) * TQ], False,
                                            True, waits=((sA, a_t1),))
                                wuse(WBq, n128 % 2, vt)
                                dsl = slice(ti * TQ, (ti + 1) * TQ)
                                for hi in range(2):
                                    r0 = hi * 64
                                    dst = AQZ(n128, hi)[r0:r0 + 64, dsl]
                                    if is0:
                                        a_ev = aop("activation", dst,
                                                   pa[r0:r0 + 64, :], AF.Copy,
                                                   waits=((sT, vt), (sV, v_zq)))
                                    else:
                                        a_ev = aop("activation", dst,
                                                   pa[r0:r0 + 64, :], AF.Identity,
                                                   bias=QCOL[r0:r0 + 64,
                                                             n128:n128 + 1],
                                                   waits=((sT, vt), (sV, v_zq)))
                                bs(pa, sA, a_ev)
                    a_q = sA.n

                    v_aux = 0
                    for ti, tq in enumerate(hqs):
                        pbk = PS[5]
                        vtb = 0
                        vtq_p = use.setdefault("qn_p", {0: 0, 1: 0})
                        for cp in range(NC):
                            par = cp % 2
                            ring = SQR[:, par * TQ:(par + 1) * TQ]
                            ak = 0
                            for hi in range(2):
                                r0 = hi * 64
                                ak = aop("activation", ring[r0:r0 + 64, :],
                                         AQZ(cp, hi)[r0:r0 + 64,
                                                     ti * TQ:(ti + 1) * TQ],
                                         AF.Square,
                                         waits=((sA, a_q), (sT, vtq_p[par])))
                            vtb = mm(pbk[0:16, :],
                                     CSEL128[:, cp * 16:(cp + 1) * 16],
                                     ring[:, :], cp == 0, cp == NC - 1,
                                     waits=(bw(pbk) if cp == 0 else ()) + ((sA, ak),),
                                     inc=True)
                            vtq_p[par] = vtb
                        a_qn = aop("activation", HROW[0:16, 0:TQ],
                                   pbk[0:16, :], AF.Sqrt,
                                   waits=((sT, vtb), (sV, use.get("aux", 0))))
                        bs(pbk, sA, a_qn)
                        if is0:
                            v_car = dma("sync", "car", CAR[0:1, :],
                                        dram["c_arange"][0:1, tq * TQ:(tq + 1) * TQ],
                                        waits=((sT, use.get("slq", 0)),))
                            pq = PS[4]
                            vt = mm(pq[0:16, :], CONES[0:1, 0:16],
                                    CAR[0:1, 0:TQ], True, True,
                                    waits=bw(pq) + (v_car,))
                            use["slq"] = vt
                            vsl = vop("tensor_scalar", SMALLF[0:16, 0:TQ],
                                      pq[0:16, :], SLP8[0:NH, 0:1], None, ALU.mult,
                                      waits=((sT, vt), (sT, use.get("smf", 0))))
                            bs(pq, sV, vsl)
                            v_aux = vop("scalar_tensor_tensor",
                                        AUXR[0:16, 0:TQ],
                                        HROW[0:16, 0:TQ],
                                        MAXK[0:16, 5:6], SMALLF[0:16, 0:TQ],
                                        ALU.mult, ALU.add,
                                        waits=((sA, a_mk), (sA, a_qn),
                                               (sV, vsl), (sT, use.get("aux_mm", 0))))
                        else:
                            vmq = vop("reduce_max", MAXK[0:16, 6:7],
                                      HROW[0:16, 0:TQ], AX, waits=((sA, a_qn),))
                            use["aux"] = vmq
                            vmq = vop("tensor_tensor", MAXK[0:16, 7:8],
                                      MAXK[0:16, 6:7], MAXK[0:16, 5:6], ALU.mult,
                                      waits=((sV, vmq), (sA, a_mk)))
                            vdg = vop("tensor_scalar", GSCRB[0:16, 0:16],
                                      CEYEF[0:16, 0:16], MAXK[0:16, 7:8], None,
                                      ALU.mult,
                                      waits=((sV, vmq), (sT, use.get("biamm", 0))))
                            pq = PS[4]
                            vtb2 = mm(pq[:, 0:16], CONES[0:16, :],
                                      GSCRB[0:16, 0:16], True, True,
                                      waits=bw(pq) + ((sV, vdg),))
                            use["biamm"] = vtb2
                            a_bia = aop("activation", BIA1[:, 0:16], pq[:, 0:16],
                                        AF.Copy, scale=-0.125,
                                        waits=((sT, vtb2),))
                            bs(pq, sA, a_bia)
                            use["bia_a"] = a_bia

                    SW[0] = bool(SW_P2[0])
                    psS2 = [[PS[0], PS[3]], [PS[1], PS[4]]]
                    psOL = [PS[2], PS[5]]
                    psL2 = [PS6, PS7]
                    for pr in range(NH // 2):
                        c = pr
                        for ti, tq in enumerate(hqs):
                            qc = tq
                            qsl_q = slice(ti * TQ, (ti + 1) * TQ)
                            ktmax = qc * 4 + 4
                            vt_s = {}
                            ostate = {"vt_o": 0, "vt_l": 0, "vt_oA": 0, "vt_lA": 0}

                            def emit_S(kt, pr=pr, qsl_q=qsl_q):
                                for hi in range(2):
                                    h = 2 * pr + hi
                                    pS = psS2[hi][kt % 2]
                                    if is0:
                                        mm(pS[:, 0:TQ],
                                           AK(c)[:, kt * 128:(kt + 1) * 128],
                                           AQZ(c, hi)[:, qsl_q], True, False,
                                           waits=bw(pS) + ((sA, a_q), (sV, v_zq)),
                                           swkind="s")
                                        vt = mm(pS[:, 0:TQ],
                                                NEGE16[:, h * 128:(h + 1) * 128],
                                                AUXR[0:128, 0:TQ], False, True,
                                                waits=((sV, v_aux), v_ng),
                                                swkind="s")
                                        use["aux_mm"] = vt
                                    else:
                                        vt = mm(pS[:, 0:TQ],
                                                AK(c)[:, kt * 128:(kt + 1) * 128],
                                                AQZ(c, hi)[:, qsl_q], True,
                                                True,
                                                waits=bw(pS) + ((sA, a_q),
                                                                (sV, v_zq)),
                                                swkind="s")
                                    vt_s[(hi, kt)] = vt

                            def emit_C(kt, pr=pr, qc=qc, ktmax=ktmax):
                                voff = max(0, (kt - qc * 4) * 128)
                                first = (kt == 0)
                                last = (kt == ktmax - 1)
                                for hi in range(2):
                                    h = 2 * pr + hi
                                    pS = psS2[hi][kt % 2]
                                    vt = vt_s[(hi, kt)]
                                    if kt >= qc * 4:
                                        vtri = vop("tensor_tensor",
                                                   pS[:, voff:voff + 128],
                                                   pS[:, voff:voff + 128],
                                                   CTRI[:, :], ALU.add,
                                                   waits=((sT, vt), v_consts))
                                        wa = ((sV, vtri),)
                                    else:
                                        wa = ((sT, vt),)
                                    slab_i = 2 * hi + kt % 2
                                    slab = PT[:, slab_i * TQ + voff:
                                              (slab_i + 1) * TQ]
                                    wa = wa + ((sT, use.get(("pt", slab_i), 0)),)
                                    if is0:
                                        a_exp = aop("activation", slab,
                                                    pS[:, voff:TQ], AF.Exp,
                                                    scale=0.125,
                                                    bias=BIASC[:, h * 16 + kt:
                                                               h * 16 + kt + 1],
                                                    waits=wa + ((sV, v_biasc),))
                                    else:
                                        a_exp = aop("activation", slab,
                                                    pS[:, voff:TQ], AF.Exp,
                                                    scale=0.125,
                                                    bias=BIA1[:, h:h + 1],
                                                    waits=wa +
                                                    ((sA, use.get("bia_a", 0)),))
                                    bs(pS, sA, a_exp)
                                    wo = bw(psOL[hi]) if first else ()
                                    wl = bw(psL2[hi]) if first else ()
                                    if hi == 0:
                                        vt_o = mm(psOL[0][0:64, voff:TQ],
                                                  AVS(kt)[:, h * DH:(h + 1) * DH],
                                                  slab, first, last,
                                                  waits=wo + ((sA, a_exp),),
                                                  swkind="av")
                                        vt_l = mm(psL2[0][:, voff:TQ],
                                                  CONES[:, :], slab, first, last,
                                                  waits=wl + ((sA, a_exp),),
                                                  inc=True, swkind="l")
                                        ostate["vt_oA"], ostate["vt_lA"] = vt_o, vt_l
                                    else:
                                        vt_o = mm(psOL[1][64:128, voff:TQ],
                                                  AVS(kt)[:, h * DH:(h + 1) * DH],
                                                  slab, first, last,
                                                  waits=wo + ((sA, a_exp),),
                                                  tp=(0, 64), swkind="av")
                                        vt_l = mm(psL2[1][:, voff:TQ],
                                                  CONES[:, :], slab, first, last,
                                                  waits=wl + ((sA, a_exp),),
                                                  inc=True, swkind="l")
                                        ostate["vt_o"], ostate["vt_l"] = vt_o, vt_l
                                    use[("pt", slab_i)] = vt_l

                            for kt in range(ktmax):
                                emit_S(kt)
                                if kt > 0:
                                    emit_C(kt - 1)
                            emit_C(ktmax - 1)

                            vr = vop("reciprocal", ONRM[0:64, 0:TQ],
                                     psL2[0][0:64, 0:TQ],
                                     waits=((sT, ostate["vt_lA"]),))
                            bs(psL2[0], sV, vr)
                            vr = vop("reciprocal", ONRM[64:128, 0:TQ],
                                     psL2[1][64:128, 0:TQ],
                                     waits=((sV, vr), (sT, ostate["vt_l"])))
                            bs(psL2[1], sV, vr)
                            vo = vop("tensor_tensor",
                                     XBC(c)[0:64, tq * TQ:(tq + 1) * TQ],
                                     psOL[0][0:64, 0:TQ], ONRM[0:64, 0:TQ],
                                     ALU.mult,
                                     waits=((sV, vr), (sT, ostate["vt_oA"])))
                            bs(psOL[0], sV, vo)
                            vo = vop("tensor_tensor",
                                     XBC(c)[64:128, tq * TQ:(tq + 1) * TQ],
                                     psOL[1][64:128, 0:TQ], ONRM[64:128, 0:TQ],
                                     ALU.mult,
                                     waits=((sV, vo), (sT, ostate["vt_o"])))
                            bs(psOL[1], sV, vo)
                            v_out_v = vo
                            use["aux"] = v_aux
                    SW[0] = False
            v_attn_v = v_out_v
            if ATTN_UPTO <= 3:
                return post_ln_scoped(sV.n, "pa" + tag)

            with contextlib.ExitStack() as est:
                WBp = sb("WBp" + tag, [128, WSLOT], BF16, est)
                MUt = sb("MUe" + tag, [128, T], F32, est)
                RSTDt = sb("RSe" + tag, [128, T], F32, est)
                SCRt = sb("SCe" + tag, [128, (3 if is0 else 1) * TQ], F32, est)
                if is0:
                    APF = sb("APF", [128, NC * T], F32, est)
                else:
                    T1P = sb("T1P", [16, T], BF16, est)
                    WS2L = sb("WS2L", [16, D], BF16, est)

                a_t1p = 0
                if not is0:
                    v_ws2 = dma("gp", "ws2l", WS2L[:, :], dram["lb_p"], waits=((sT, sT.n),))
                    vla = dma("gp", "gscrb", GSCRB[:, 0:NC * 16].rearrange("p (c r) -> p c r", c=NC),
                              dram["la_p"].rearrange("(c p) r -> p c r", p=128),
                              waits=((sT, sT.n),))
                    for tq in range(NT):
                        pa = PS[4]
                        vt = 0
                        for c in range(NC):
                            vt = mm(pa[0:16, :], GSCRB[:, c * 16:(c + 1) * 16],
                                    XBC(c)[:, tq * TQ:(tq + 1) * TQ], c == 0,
                                    c == NC - 1,
                                    waits=(bw(pa) if c == 0 else ()) +
                                    (vla, (sV, v_attn_v)))
                        a_t1p = aop("activation", T1P[:, tq * TQ:(tq + 1) * TQ],
                                    pa[0:16, :], AF.Copy, scale=2.0,
                                    waits=((sT, vt),))
                        bs(pa, sA, a_t1p)

                wp = dram["w_o0" if is0 else "w_p1"].rearrange(
                    "(c p) n -> p c n", p=128)
                pb4 = [PS[0], PS[1], PS[2], PS[3]]
                pidx = 0
                v_add = 0
                v_xcc = [0] * NC
                a_apf = 0
                for ns in range(2):
                    slot = 0
                    v_wl = wload(WBp, slot, wp[:, :, ns * TQ:(ns + 1) * TQ], WSLOT)
                    for nsub in range(4):
                        n128 = ns * 4 + nsub
                        for tq in range(NT):
                            pa = pb4[pidx]
                            pidx = (pidx + 1) % 4
                            vt = 0
                            for cc in range(NC):
                                w0 = bw(pa) if cc == 0 else ()
                                vt = mm(pa[:, :],
                                        WBp[:, cc * TQ + nsub * 128:
                                            cc * TQ + nsub * 128 + 128],
                                        XBC(cc)[:, tq * TQ:(tq + 1) * TQ],
                                        cc == 0, (cc == NC - 1) and is0,
                                        waits=w0 + (v_wl, (sV, v_attn_v)))
                            if not is0:
                                vt = mm(pa[:, :], WS2L[:, n128 * 128:(n128 + 1) * 128],
                                        T1P[:, tq * TQ:(tq + 1) * TQ], False, True,
                                        waits=((sA, a_t1p), v_ws2))
                            wuse(WBp, slot, vt)
                            sl = slice(tq * TQ, (tq + 1) * TQ)
                            if is0:
                                a_apf = aop("activation",
                                            APF[:, n128 * T:(n128 + 1) * T][:, sl],
                                            pa[:, :], AF.Copy, waits=((sT, vt),))
                                bs(pa, sA, a_apf)
                            else:
                                vv = vop("tensor_scalar", SCRt[:, 0:TQ], pa[:, :],
                                         COLS[:, 136 + n128:137 + n128], None,
                                         ALU.add, waits=((sT, vt),))
                                v_add = vop("tensor_tensor", XC(n128)[:, sl],
                                            XC(n128)[:, sl], SCRt[:, 0:TQ], ALU.add,
                                            waits=((sV, vv),))
                                v_xcc[n128] = v_add
                                bs(pa, sV, v_add)
                t_xbr = sT.n

                if is0:
                    a_c2 = 0
                    for c in range(NC):
                        a_c2 = aop("activation", XBC(c), APF[:, c * T:(c + 1) * T],
                                   AF.Copy, waits=((sA, a_apf), (sV, v_attn_v)))
                    v_add, _, _, v_xcc = ln_full(
                        MUt, RSTDt, SCRt, lambda c: XBC(c), a_c2,
                        gcol=COLS[:, 16:24], bcol=COLS[:, 24:32],
                        src_fn=lambda c: APF[:, c * T:(c + 1) * T],
                        add_to_x=True)
                    t_xbr = 0
            return post_ln_scoped(v_xcc, "pa" + tag, t_xbr=t_xbr)

        def ffn0(v_in_v, a_in_a, atc_in=None):

            def wh(tq, c):
                if atc_in is not None:
                    return ((sA, atc_in[tq][c]),)
                return ((sA, a_in_a), (sV, v_in_v))
            st = contextlib.ExitStack()
            with st:
                HID = sb("HID", [128, 32 * 1024], BF16, st)
                WBf = sb("WBf", [128, 2 * WSLOT], BF16, st)
                B0 = sb("B0f", [128, 2 * TQ], F32, st)
                B1 = sb("B1f", [128, 2 * TQ], F32, st)
                B2 = sb("B2f", [128, 2 * TQ], F32, st)
                SINB = sb("SINB", [128, 2 * TQ], F32, st)
                w1 = dram["w_ff1"].rearrange("(c p) n -> p c n", p=128)
                w2 = dram["w_ff2"].rearrange("(f p) n -> p f n", p=128)
                pb4 = [PS[0], PS[1], PS[2], PS[3]]
                pidx = 0
                v_add = 0
                v_xcc = [0] * NC
                use_f = {}
                for half in range(2):
                    tqs = (2 * half, 2 * half + 1)
                    a_hid = 0
                    a_hid_t = {}
                    for fs in range(8):
                        slot = fs % 2
                        v_wl = wload(WBf, slot, w1[:, :, fs * TQ:(fs + 1) * TQ],
                                     WSLOT)
                        for fsub in range(4):
                            fc = fs * 4 + fsub
                            sthc = COLS[:, 160 + fc:161 + fc]
                            cthc = COLS[:, 192 + fc:193 + fc]
                            ecol = COLS[:, 72 + fc:73 + fc]
                            b1c = COLS[:, 104 + fc:105 + fc]
                            for ti, tq in enumerate(tqs):
                                pa = pb4[pidx]
                                pidx = (pidx + 1) % 4
                                vt = 0
                                for c in range(NC):
                                    w0 = bw(pa) if c == 0 else ()
                                    vt = mm(pa[:, :],
                                            WBf[:, slot * WSLOT + c * TQ +
                                                fsub * 128:slot * WSLOT + c * TQ +
                                                fsub * 128 + 128],
                                            XBC(c)[:, tq * TQ:(tq + 1) * TQ],
                                            c == 0, c == NC - 1,
                                            waits=w0 + (v_wl,) + wh(tq, c))
                                wuse(WBf, slot, vt)
                                par = (fc * 2 + ti) % 2
                                qsl = slice(par * TQ, (par + 1) * TQ)
                                a_cz = aop("activation", B1[:, qsl], pa[:, :],
                                           AF.Sin, scale=0.1, bias=CPI2[:, 0:1],
                                           waits=((sT, vt),
                                                  (sV, use_f.get(("b1", par), 0))))
                                aop("activation", SINB[:, qsl], pa[:, :],
                                    AF.Sin, scale=0.1, waits=())
                                a_h = aop("activation", B0[:, qsl], pa[:, :],
                                          AF.Copy, waits=())
                                bs(pa, sA, a_h)
                                vop("tensor_scalar", B1[:, qsl], B1[:, qsl],
                                    sthc, None, ALU.mult,
                                    waits=((sA, a_h), (sV, v_thb)))
                                vop("scalar_tensor_tensor", B1[:, qsl],
                                    SINB[:, qsl], cthc, B1[:, qsl], ALU.mult,
                                    ALU.add)
                                vv = vop("scalar_tensor_tensor", B2[:, qsl],
                                         B1[:, qsl], ecol, B0[:, qsl], ALU.mult,
                                         ALU.add,
                                         waits=((sA, use_f.get(("rl", par), 0)),))
                                use_f[("b1", par)] = vv
                                dst = HID[:, fc * 1024 + ti * TQ:
                                          fc * 1024 + (ti + 1) * TQ]
                                a_hid = aop("activation", dst, B2[:, qsl], AF.Relu,
                                            bias=b1c, waits=((sV, vv),))
                                use_f[("rl", par)] = a_hid
                                a_hid_t[(fc, ti)] = a_hid
                    for n128 in range(8):
                        slot2 = n128 % 2
                        v_wl = wload(WBf, slot2,
                                     w2[:, :, n128 * 128:(n128 + 1) * 128],
                                     32 * 128)
                        b2c = COLS[:, 64 + n128:65 + n128]
                        for ti, tq in enumerate(tqs):
                            pa = pb4[pidx]
                            pidx = (pidx + 1) % 4
                            vt = 0
                            for fc in range(32):
                                w0 = bw(pa) if fc == 0 else ()
                                vt = mm(pa[:, :],
                                        WBf[:, slot2 * WSLOT + fc * 128:
                                            slot2 * WSLOT + fc * 128 + 128],
                                        HID[:, fc * 1024 + ti * TQ:
                                            fc * 1024 + (ti + 1) * TQ],
                                        fc == 0, fc == 31,
                                        waits=w0 + (v_wl, (sA, a_hid_t[(fc, ti)])))
                            wuse(WBf, slot2, vt)
                            sl = slice(tq * TQ, (tq + 1) * TQ)
                            vv = vop("scalar_tensor_tensor", XC(n128)[:, sl],
                                     pa[:, :], 1.0, XC(n128)[:, sl], ALU.bypass,
                                     ALU.add, waits=((sT, vt),))
                            v_add = vop("tensor_scalar", XC(n128)[:, sl],
                                        XC(n128)[:, sl], b2c, None, ALU.add,
                                        waits=((sV, vv),))
                            v_xcc[n128] = v_add
                            bs(pa, sV, v_add)
            return post_ln_scoped(v_xcc, "pf")

        def moe(v_in_v, a_in_a, atc_in=None):

            def wh(tq, c):
                if atc_in is not None:
                    return ((sA, atc_in[tq][c]),)
                return ((sA, a_in_a), (sV, v_in_v))
            st = contextlib.ExitStack()
            with st:
                HPAIR = sb("HPAIR", [128, 8 * T], BF16, st)
                WBm = sb("WBm", [128, 4 * WSLOT], BF16, st)
                WTOK = sb("WTOK", [128, 256], F32, st)
                WT = sb("WT", [16, T], BF16, st)
                WBC = sb("WBC", [128, T], F32, st)
                SGBr = sb("SGBr", [128, T], F32, st)
                SGR = sb("SGR", [128, TQ], BF16, st)
                E16P = sb("E16P", [NH, NH * 128], BF16, st)
                v_e16p = dma("sync", "e16p", E16P[0:NH, :], dram["c_e16p"])
                USR = sb("USR", [128, TQ], F32, st)
                SCRm = sb("SCRm", [128, TQ], F32, st)
                pb4 = [PS[0], PS[1], PS[2], PS[3]]
                pidx = 0
                v_wt = 0
                pend_w = [None]

                def emit_wtr(vvw, ttw):
                    vtt = tr(PS[4][0:16, 0:128], WTOK[:, ttw * 16:(ttw + 1) * 16],
                             CEYEF[:, :], waits=bw(PS[4]) + ((sV, vvw),))
                    vwt = vop("tensor_copy", WT[0:16, ttw * 128:(ttw + 1) * 128],
                              PS[4][0:16, 0:128], waits=((sT, vtt),))
                    bs(PS[4], sV, vwt)
                    return vwt

                for tt in range(16):
                    pa = PS[5]
                    vt = 0
                    for c in range(NC):
                        vt = mm(pa[:, 0:16], XBC(c)[:, tt * 128:(tt + 1) * 128],
                                WGB[:, c * 16:(c + 1) * 16], c == 0, c == NC - 1,
                                waits=(bw(pa) if c == 0 else ()) +
                                (v_wgb,) + wh(tt // 4, c))
                    g0 = GSCR[:, 16:32]
                    vv = vop("reduce_max", GSCR[:, 32:33], pa[:, 0:16], AX,
                             negate=True, waits=((sT, vt),))
                    a1 = aop("activation", g0, pa[:, 0:16], AF.Exp,
                             bias=GSCR[:, 32:33], waits=((sV, vv),))
                    bs(pa, sA, a1)
                    vv = vop("reduce_max", GSCR[:, 33:34], g0, AX, waits=((sA, a1),))
                    vv = vop("tensor_scalar", GSCR[:, 48:64], g0, GSCR[:, 33:34],
                             None, ALU.is_ge, waits=((sV, vv),))
                    vv = vop("scalar_tensor_tensor", GSCR[:, 0:16], GSCR[:, 48:64],
                             -1e30, g0, ALU.mult, ALU.add, waits=((sV, vv),))
                    vv = vop("reduce_max", GSCR[:, 34:35], GSCR[:, 0:16], AX,
                             waits=((sV, vv),))
                    vv = vop("tensor_scalar", GSCR[:, 48:64], g0, GSCR[:, 34:35],
                             None, ALU.is_ge, waits=((sV, vv),))
                    vv = vop("tensor_tensor", GSCR[:, 35:36], GSCR[:, 33:34],
                             GSCR[:, 34:35], ALU.add, waits=((sV, vv),))
                    vv = vop("reciprocal", GSCR[:, 35:36], GSCR[:, 35:36],
                             waits=((sV, vv),))
                    vv = vop("scalar_tensor_tensor", WTOK[:, tt * 16:(tt + 1) * 16],
                             g0, GSCR[:, 35:36], GSCR[:, 48:64], ALU.mult, ALU.mult,
                             waits=((sV, vv),))
                    if pend_w[0] is not None:
                        v_wt = emit_wtr(*pend_w[0])
                    pend_w[0] = (vv, tt)
                if pend_w[0] is not None:
                    v_wt = emit_wtr(*pend_w[0])

                v_add = 0
                v_xcc = [0] * NC
                for pr in range(8):
                    v_h = 0
                    v_h_t = {}
                    sg0 = (pr % 2) * 2
                    sg1 = 2 - sg0
                    sdn = sg0
                    for ei, e in enumerate((2 * pr, 2 * pr + 1)):
                        sgu = sg0 if ei == 0 else sg1
                        vbc = 0
                        for tq in range(NT):
                            pa = PS[5]
                            vt = mm(pa[:, :], E16P[:, e * 128:(e + 1) * 128],
                                    WT[0:NH, tq * TQ:(tq + 1) * TQ], True, True,
                                    waits=bw(pa) + ((sV, v_wt), v_e16p))
                            vbc = vop("tensor_copy", WBC[:, tq * TQ:(tq + 1) * TQ],
                                      pa[:, :], waits=((sT, vt),))
                            bs(pa, sV, vbc)
                        ge = dram["we_g"][e].rearrange("(c p) n -> p c n", p=128)
                        ue = dram["we_u"][e].rearrange("(c p) n -> p c n", p=128)
                        v_wg = wload(WBm, sgu, ge, WSLOT)
                        v_wu = wload(WBm, sgu + 1, ue, WSLOT)
                        if ei == 1:
                            vd0 = wload(WBm, sdn, dram["we_d"][2 * pr].rearrange(
                                "(h p) n -> p h n", p=128), WSLOT)
                            vd1 = wload(WBm, sdn + 1,
                                        dram["we_d"][2 * pr + 1].rearrange(
                                            "(h p) n -> p h n", p=128), WSLOT)
                        for hc in range(4):
                            for tq in range(NT):
                                pg = pb4[pidx]
                                pidx = (pidx + 1) % 4
                                pu = pb4[pidx]
                                pidx = (pidx + 1) % 4
                                vtg = vtu = 0
                                for c in range(NC):
                                    w0 = bw(pg) if c == 0 else ()
                                    vtg = mm(pg[:, :],
                                             WBm[:, sgu * WSLOT + c * TQ +
                                                 hc * 128:sgu * WSLOT + c * TQ +
                                                 hc * 128 + 128],
                                             XBC(c)[:, tq * TQ:(tq + 1) * TQ],
                                             c == 0, c == NC - 1,
                                             waits=w0 + (v_wg,))
                                for c in range(NC):
                                    w0 = bw(pu) if c == 0 else ()
                                    vtu = mm(pu[:, :],
                                             WBm[:, (sgu + 1) * WSLOT + c * TQ +
                                                 hc * 128:(sgu + 1) * WSLOT +
                                                 c * TQ + hc * 128 + 128],
                                             XBC(c)[:, tq * TQ:(tq + 1) * TQ],
                                             c == 0, c == NC - 1,
                                             waits=w0 + (v_wu,))
                                wuse(WBm, sgu, vtg)
                                wuse(WBm, sgu + 1, vtu)
                                a_sg = aop("activation", SGR[:, :], pg[:, :],
                                           AF.Silu, waits=((sT, vtg), (sV, v_h)))
                                bs(pg, sA, a_sg)
                                vv = vop("scalar_tensor_tensor", USR[:, :],
                                         pu[:, :], 1.0,
                                         WBC[:, tq * TQ:(tq + 1) * TQ], ALU.bypass,
                                         ALU.mult, waits=((sT, vtu), (sV, vbc)))
                                bs(pu, sV, vv)
                                dst = HPAIR[:, (ei * 4 + hc) * T + tq * TQ:
                                            (ei * 4 + hc) * T + (tq + 1) * TQ]
                                v_h = vop("tensor_tensor", dst, SGR[:, :],
                                          USR[:, :], ALU.mult,
                                          waits=((sV, vv), (sA, a_sg)))
                                v_h_t[(ei * 4 + hc, tq)] = v_h
                    for n128 in range(NC):
                        for tq in range(NT):
                            pa = pb4[pidx]
                            pidx = (pidx + 1) % 4
                            vt = 0
                            first = True
                            for ei in range(2):
                                vwl = vd0 if ei == 0 else vd1
                                for hc in range(4):
                                    w0 = bw(pa) if first else ()
                                    vt = mm(pa[:, :],
                                            WBm[:, (sdn + ei) * WSLOT + hc * 1024 +
                                                n128 * 128:(sdn + ei) * WSLOT +
                                                hc * 1024 + n128 * 128 + 128],
                                            HPAIR[:, (ei * 4 + hc) * T + tq * TQ:
                                                  (ei * 4 + hc) * T +
                                                  (tq + 1) * TQ],
                                            first, (ei == 1 and hc == 3),
                                            waits=w0 + (vwl,
                                                        (sV, v_h_t[(ei * 4 + hc,
                                                                    tq)])))
                                    first = False
                            wuse(WBm, sdn, vt)
                            wuse(WBm, sdn + 1, vt)
                            sl = slice(tq * TQ, (tq + 1) * TQ)
                            v_add = vop("scalar_tensor_tensor", XC(n128)[:, sl],
                                        pa[:, :], 1.0, XC(n128)[:, sl], ALU.bypass,
                                        ALU.add, waits=((sT, vt),))
                            bs(pa, sV, v_add)

                vsg = 0
                for tq in range(NT):
                    pa = PS[5]
                    vt = 0
                    for c in range(NC):
                        vt = mm(pa[:, :], WSGB[:, c * 128:(c + 1) * 128],
                                XBC(c)[:, tq * TQ:(tq + 1) * TQ], c == 0,
                                c == NC - 1,
                                waits=(bw(pa) if c == 0 else ()) + (v_wsgb,))
                    vsg = aop("activation", SGBr[:, tq * TQ:(tq + 1) * TQ],
                              pa[:, :], AF.Sigmoid, waits=((sT, vt),))
                    bs(pa, sA, vsg)
                wsg_ = dram["ws_g"].rearrange("(c p) n -> p c n", p=128)
                wsu_ = dram["ws_u"].rearrange("(c p) n -> p c n", p=128)
                wsd_ = dram["ws_d"].rearrange("(f p) n -> p f n", p=128)
                for half in range(2):
                    tqs = (2 * half, 2 * half + 1)
                    a_hid = 0
                    a_hid_t = {}
                    for fs in range(4):
                        v_wg = wload(WBm, 0, wsg_[:, :, fs * TQ:(fs + 1) * TQ],
                                     WSLOT)
                        v_wu = wload(WBm, 1, wsu_[:, :, fs * TQ:(fs + 1) * TQ],
                                     WSLOT)
                        for fsub in range(4):
                            fc = fs * 4 + fsub
                            for ti, tq in enumerate(tqs):
                                pg = pb4[pidx]
                                pidx = (pidx + 1) % 4
                                pu = pb4[pidx]
                                pidx = (pidx + 1) % 4
                                vtg = vtu = 0
                                for c in range(NC):
                                    w0 = bw(pg) if c == 0 else ()
                                    vtg = mm(pg[:, :],
                                             WBm[:, c * TQ + fsub * 128:
                                                 c * TQ + fsub * 128 + 128],
                                             XBC(c)[:, tq * TQ:(tq + 1) * TQ],
                                             c == 0, c == NC - 1,
                                             waits=w0 + (v_wg,))
                                for c in range(NC):
                                    w0 = bw(pu) if c == 0 else ()
                                    vtu = mm(pu[:, :],
                                             WBm[:, WSLOT + c * TQ + fsub * 128:
                                                 WSLOT + c * TQ + fsub * 128 + 128],
                                             XBC(c)[:, tq * TQ:(tq + 1) * TQ],
                                             c == 0, c == NC - 1,
                                             waits=w0 + (v_wu,))
                                wuse(WBm, 0, vtg)
                                wuse(WBm, 1, vtu)
                                a_sg2 = aop("activation", SGR[:, :], pg[:, :],
                                            AF.Silu, waits=((sT, vtg), (sV, a_hid)))
                                bs(pg, sA, a_sg2)
                                dst = HPAIR[:, fc * 1024 + ti * TQ:
                                            fc * 1024 + (ti + 1) * TQ]
                                a_hid = vop("tensor_tensor", dst, SGR[:, :],
                                            pu[:, :], ALU.mult,
                                            waits=((sA, a_sg2), (sT, vtu)))
                                a_hid_t[(fc, ti)] = a_hid
                                bs(pu, sV, a_hid)
                    for ws in range(2):
                        v_wl = wload(WBm, 0, wsd_[:, :, ws * TQ:(ws + 1) * TQ],
                                     16 * TQ)
                        for nsub in range(4):
                            n128 = ws * 4 + nsub
                            for ti, tq in enumerate(tqs):
                                pa = pb4[pidx]
                                pidx = (pidx + 1) % 4
                                vt = 0
                                for fc in range(16):
                                    w0 = bw(pa) if fc == 0 else ()
                                    vt = mm(pa[:, :],
                                            WBm[:, fc * TQ + nsub * 128:
                                                fc * TQ + nsub * 128 + 128],
                                            HPAIR[:, fc * 1024 + ti * TQ:
                                                  fc * 1024 + (ti + 1) * TQ],
                                            fc == 0, fc == 15,
                                            waits=w0 + (v_wl,
                                                        (sV, a_hid_t[(fc, ti)])))
                                wuse(WBm, 0, vt, ncols=2 * WSLOT)
                                sl = slice(tq * TQ, (tq + 1) * TQ)
                                vv = vop("scalar_tensor_tensor", SCRm[:, 0:TQ],
                                         pa[:, :], 1.0, SGBr[:, sl], ALU.bypass,
                                         ALU.mult, waits=((sT, vt), (sA, vsg)))
                                bs(pa, sV, vv)
                                v_add = vop("tensor_tensor", XC(n128)[:, sl],
                                            XC(n128)[:, sl], SCRm[:, 0:TQ],
                                            ALU.add, waits=((sV, vv),))
                                v_xcc[n128] = v_add
            return post_ln_scoped(v_xcc, "pm")

        vv, va, atc = sV.n, sA.n, None
        if 0 in phases:
            vv, va, atc, _ = attention(0, vv, va, atc)
        if 1 in phases:
            vv, va, atc, _ = ffn0(vv, va, atc)
        if 2 in phases:
            vv, va, atc, _ = attention(1, vv, va, atc)
        if 3 in phases:
            vv, va, atc, _ = moe(vv, va, atc)

        P.wait("sync", sV, vv)
        P.wait("sync", sA, va)
        P.wait("sync", v_x[0], v_x[1])
        v_st = dma("sync", "out", dout.rearrange("(c p) t -> p c t", p=128),
                   X[:, :].rearrange("p (c t) -> p c t", c=NC))
        P.wait("sync", v_st[0], v_st[1])

        with nc.Block() as block:
            P.finalize(block)

    return nc


def kernel(**inputs):
    g = host_prepare(inputs)
    x = np.asarray(inputs["L0_attn_x"], np.float32)
    nc = build_program()
    in_maps = []
    for core in range(8):
        m = dict(g)
        m["x"] = np.ascontiguousarray(x[core].T)
        in_maps.append(m)
    res = run_bass_kernel_spmd(nc, in_maps, core_ids=list(range(8)))
    out = np.stack([np.ascontiguousarray(r["out"].T) for r in res.results], axis=0)
    return out.astype(np.float32)
